# version 02
# baseline (speedup 1.0000x reference)
import os
import math
import contextlib
import numpy as np
import ml_dtypes
import concourse.bass as bass
import concourse.mybir as mybir
from concourse.bass_utils import run_bass_kernel_spmd

F32 = mybir.dt.float32
BF16 = mybir.dt.bfloat16
F16 = mybir.dt.float16
F8 = mybir.dt.float8e4
FP8 = mybir.dt.float8e4
AF = mybir.ActivationFunctionType
ALU = mybir.AluOpType
AX = mybir.AxisListType

NST = int(os.environ.get("MK_STAGES", "6"))
DBG = int(os.environ.get("MK_DEBUG", "0"))
SUB = int(os.environ.get("MK_SUB", "9"))

NC, B, S, TL, NT, D, NCH = 8, 2, 8192, 2048, 16384, 1024, 8


class KB:

    def __init__(self, nc, block, sems):
        self.nc = nc
        self.block = block
        self.sems = sems
        self.cnt = {k: 0 for k in sems}
        self._pid = None

    def pid(self, eng):
        if self._pid is None:
            self._pid = eng.partition_id()
        return self._pid

    def te(self, f): self.block.tensor(f)
    def ve(self, f): self.block.vector(f)
    def sc(self, f): self.block.scalar(f)
    def gp(self, f): self.block.gpsimd(f)
    def sy(self, f): self.block.sync(f)

    def tag(self, instr, name, amt=1):
        instr.then_inc(self.sems[name], amt)
        self.cnt[name] += amt
        return (name, self.cnt[name])

    def dma(self, eng, name, **kw):
        return self.tag(eng.dma_start(**kw), name, 16)

    def wait(self, eng, tok):
        if tok is not None:
            eng.wait_ge(self.sems[tok[0]], tok[1])

    def tok(self, name):
        return (name, self.cnt[name])


def build_program():
    nc = bass.Bass("TRN2", target_bir_lowering=False)

    d_xin = nc.dram_tensor("xin", [D, TL], F32, kind="ExternalInput")
    d_xout = nc.dram_tensor("xout", [D, TL], F32, kind="ExternalOutput")
    d_wqfT = nc.dram_tensor("wqfT", [D, D], F32, kind="ExternalInput")
    d_wrtT = nc.dram_tensor("wrtT", [D, 16], F32, kind="ExternalInput")
    d_wqkvT = nc.dram_tensor("wqkvT", [D, 384], F32, kind="ExternalInput")
    d_woutT = nc.dram_tensor("woutT", [2048, D], F32, kind="ExternalInput")
    d_ktf = nc.dram_tensor("ktf", [16, 8, 128, 256], F32, kind="ExternalInput")
    d_vf = nc.dram_tensor("vf", [16, 8, 256, 128], F32, kind="ExternalInput")
    d_wqkv2 = nc.dram_tensor("wqkv2", [D, 384], F32, kind="ExternalInput")
    d_w1T = nc.dram_tensor("w1T", [D, 4096], F32, kind="ExternalInput")
    d_w2T = nc.dram_tensor("w2T", [4096, D], F32, kind="ExternalInput")
    d_b1c = nc.dram_tensor("b1c", [128, 32], F32, kind="ExternalInput")
    d_b2c = nc.dram_tensor("b2c", [128, 8], F32, kind="ExternalInput")
    d_mskT = nc.dram_tensor("mskT", [D, TL], BF16, kind="ExternalInput")
    d_w5T = nc.dram_tensor("w5T", [D, 576], F32, kind="ExternalInput")
    d_mu5c = nc.dram_tensor("mu5c", [128, 40], F32, kind="ExternalInput")
    d_w0c = nc.dram_tensor("w0c", [128, 1], F32, kind="ExternalInput")
    d_wb5 = nc.dram_tensor("wb5", [64, 128], F32, kind="ExternalInput")
    d_uc = nc.dram_tensor("uc", [128, 1], F32, kind="ExternalInput")
    d_gnwc = nc.dram_tensor("gnwc", [128, 1], F32, kind="ExternalInput")
    d_gnbc = nc.dram_tensor("gnbc", [128, 1], F32, kind="ExternalInput")
    d_w1lc = nc.dram_tensor("w1lc", [128, 8], F32, kind="ExternalInput")
    d_w2lc = nc.dram_tensor("w2lc", [128, 8], F32, kind="ExternalInput")
    d_wo5 = nc.dram_tensor("wo5", [D, D], F32, kind="ExternalInput")
    d_fw1 = nc.dram_tensor("fw1", [D, 4096], F32, kind="ExternalInput")
    d_fw2 = nc.dram_tensor("fw2", [4096, D], F32, kind="ExternalInput")
    d_seg5 = nc.dram_tensor("seg5", [128, 5 * 1024], F32, kind="ExternalInput")
    d_msk5 = nc.dram_tensor("msk5", [128, 7 * 128], F32, kind="ExternalInput")
    _k5 = "ExternalOutput" if DBG else "Internal"
    d_r5 = nc.dram_tensor("r5", [128, NT], BF16, kind=_k5)
    d_k5 = nc.dram_tensor("k5", [128, NT], BF16, kind=_k5)
    d_v5 = nc.dram_tensor("v5", [128, NT], BF16, kind=_k5)
    d_g5 = nc.dram_tensor("g5", [128, NT], BF16, kind=_k5)
    d_rate5 = nc.dram_tensor("rate5", [128, NT], F32, kind=_k5)
    if DBG:
        d_dbg_h5 = nc.dram_tensor("dbg_h5", [128, NCH * TL], BF16, kind="ExternalOutput")
        d_dbg_os5 = nc.dram_tensor("dbg_os5", [128, NT], BF16, kind="ExternalOutput")
        d_dbg_ot5 = nc.dram_tensor("dbg_ot5", [128, NT], BF16, kind="ExternalOutput")
        d_dbg_xi5 = nc.dram_tensor("dbg_xi5", [128, NCH * TL], BF16, kind="ExternalOutput")
    d_wo0 = nc.dram_tensor("wo0", [D, D], F32, kind="ExternalInput")
    d_qkvb = nc.dram_tensor("qkvb", [128, NT], BF16, kind="Internal")
    d_vsp2 = nc.dram_tensor("vsp2", [128, NT], BF16, kind="Internal")
    d_cos = nc.dram_tensor("cosr", [64, S], BF16, kind="ExternalInput")
    d_sin = nc.dram_tensor("sinr", [64, S], BF16, kind="ExternalInput")
    d_cmask = nc.dram_tensor("cmask", [4, 128, 512], BF16, kind="ExternalInput")

    d_ag_in = [nc.dram_tensor(f"ag{i}_in", [D, TL], BF16, kind="Internal") for i in range(4)]
    d_ag_out = [nc.dram_tensor(f"ag{i}_out", [NC * D, TL], BF16, kind="Internal") for i in range(4)]
    d_ret_in = [nc.dram_tensor(f"ret{i}_in", [128, NT], BF16, kind="Internal") for i in range(4)]
    d_ffo = nc.dram_tensor("ffo", [NCH * 128, TL], BF16, kind="Internal")
    d_gt = nc.dram_tensor("gtrow", [16, TL], BF16, kind="Internal")
    if DBG:
        d_dbg_hb = nc.dram_tensor("dbg_hb", [D, TL], BF16, kind="ExternalOutput")
        d_dbg_rstd = nc.dram_tensor("dbg_rstd", [128, TL], F32, kind="ExternalOutput")
        d_dbg_mrs = nc.dram_tensor("dbg_mrs", [128, TL], F32, kind="ExternalOutput")
        d_dbg_qf = nc.dram_tensor("dbg_qf", [128, NCH * TL], BF16, kind="ExternalOutput")
        d_dbg_gt = nc.dram_tensor("dbg_gt", [16, TL], BF16, kind="ExternalOutput")
        d_dbg_ffo = nc.dram_tensor("dbg_ffo", [NCH * 128, TL], BF16, kind="ExternalOutput")
        d_dbg_os = nc.dram_tensor("dbg_os", [128, NT], BF16, kind="ExternalOutput")
    d_eye = nc.dram_tensor("eye", [128, 128], BF16, kind="ExternalInput")
    d_vsp = nc.dram_tensor("vsp", [128, S], BF16, kind="Internal")
    d_ret_out = [nc.dram_tensor(f"ret{i}_out", [NC * 128, NT], BF16, kind="Internal") for i in range(4)]

    with contextlib.ExitStack() as top:
        ec = top.enter_context
        s_dq = ec(nc.semaphore("k_dq"))
        s_dg = ec(nc.semaphore("k_dg"))
        s_cc = ec(nc.semaphore("k_cc"))
        s_mm = ec(nc.semaphore("k_mm"))
        s_ev = ec(nc.semaphore("k_ev"))
        s_ac = ec(nc.semaphore("k_ac"))
        s_gv = ec(nc.semaphore("k_gv"))
        s_ph = ec(nc.semaphore("k_ph"))
        X = ec(nc.sbuf_tensor("X", [128, NCH * TL], F32))
        CPOOL = ec(nc.sbuf_tensor("CPOOL", [128, 8], F32))
        ONESB = ec(nc.sbuf_tensor("ONESB", [128, 1], BF16))
        ONESROW = ec(nc.sbuf_tensor("ONESROW", [1, 128], BF16))
        ONESROWF = ec(nc.sbuf_tensor("ONESROWF", [1, 128], F32))
        STAT = ec(nc.sbuf_tensor("STAT", [1, 2048], F32))
        RSTDR = ec(nc.sbuf_tensor("RSTDR", [128, TL], F32))
        MRSR = ec(nc.sbuf_tensor("MRSR", [128, TL], F32))
        IDE = ec(nc.sbuf_tensor("IDE", [128, 128], BF16))
        EPSR = ec(nc.sbuf_tensor("EPSR", [1, 512], F32))
        ZERO16 = ec(nc.sbuf_tensor("ZERO16", [128, 16], F32))
        block = ec(nc.Block())
        sems = {"dq": s_dq, "dg": s_dg, "cc": s_cc, "mm": s_mm,
                "ev": s_ev, "ac": s_ac, "gv": s_gv, "ph": s_ph}
        kb = KB(nc, block, sems)
        C = kb.cnt

        def f_sy(sy):
            kb.dma(sy, "dq", out=X[:, :].rearrange("p (c t) -> p c t", t=TL),
                   in_=d_xin[:, :].rearrange("(c p) t -> p c t", p=128))
            kb.dma(sy, "dq", out=IDE[:, :], in_=d_eye[:, :])
        kb.sy(f_sy)
        tok_init = kb.tok("dq")

        def f_ve(v):
            kb.wait(v, tok_init)
            v.memset(CPOOL[:, 0:1], 0.5)
            v.memset(CPOOL[:, 1:2], -1e30)
            v.memset(CPOOL[:, 2:3], 1.0)
            v.memset(ONESB[:, :], 1.0)
            v.memset(EPSR[:, :], 1e-5)
            v.memset(ZERO16[:, :], 0.0)
            v.memset(ONESROW[:, :], 1.0)
            kb.tag(v.memset(ONESROWF[:, :], 1.0), "ev")
        kb.ve(f_ve)
        tok_consts = kb.tok("ev")

        def barrier():
            target = C["ph"] + 5

            def f(eng):
                eng.nop().then_inc(sems["ph"], 1)
                eng.wait_ge(sems["ph"], target)
            for emit in (kb.te, kb.ve, kb.sc, kb.gp, kb.sy):
                emit(f)
            C["ph"] += 5

        def emit_ln(tag, HB, extra_apply=None, extra_rows=None, apply5=None):
            with contextlib.ExitStack() as lnst:
                XB = lnst.enter_context(nc.sbuf_tensor(f"lnxb_{tag}", [128, 2 * 512], BF16))
                SQ = lnst.enter_context(nc.sbuf_tensor(f"lnsq_{tag}", [128, 2 * 512], BF16))
                TT = lnst.enter_context(nc.sbuf_tensor(f"lntt_{tag}", [128, 512], F32))
                PST = lnst.enter_context(nc.psum_tensor(f"lnst_{tag}", [1, 1024], F32))
                PREP = lnst.enter_context(nc.psum_tensor(f"lnrep_{tag}", [128, 1024], F32))
                tok_rep_done = None
                tok_rep_mm_prev = None
                tok_extra_prev = None
                cast_free = [None, None]
                for tb in range(4):
                    t0 = tb * 512
                    tok_cast = None
                    for c in range(NCH):
                        xs = X[:, c * TL + t0: c * TL + t0 + 512]
                        cpar = c % 2

                        def f_ve(v, xs=xs, c=c, tb=tb, cpar=cpar, cf=cast_free[cpar]):
                            if c == 0:
                                kb.wait(v, tok_consts)
                                if tb > 0:
                                    kb.wait(v, tok_stats_read)
                            kb.wait(v, cf)
                            v.tensor_copy(XB[:, cpar * 512:(cpar + 1) * 512], xs)
                            kb.tag(v.tensor_mul(SQ[:, cpar * 512:(cpar + 1) * 512], xs, xs), "ev")
                        kb.ve(f_ve)
                        tok_cast = kb.tok("ev")

                        def f_te(te, c=c, cpar=cpar, tok_cast=tok_cast):
                            kb.wait(te, tok_cast)
                            te.matmul(PST[0:1, 0:512], ONESB[:, :],
                                      XB[:, cpar * 512:(cpar + 1) * 512],
                                      start=(c == 0), stop=(c == NCH - 1))
                            mi = te.matmul(PST[0:1, 512:1024], ONESB[:, :],
                                           SQ[:, cpar * 512:(cpar + 1) * 512],
                                           start=(c == 0), stop=(c == NCH - 1))
                            kb.tag(mi, "mm")
                        kb.te(f_te)
                        cast_free[cpar] = kb.tok("mm")
                    tok_stats_mm = kb.tok("mm")

                    def f_sc(sc, t0=t0, tok=tok_stats_mm, trm=tok_rep_mm_prev,
                             trx=tok_extra_prev):
                        kb.wait(sc, tok)
                        kb.wait(sc, trm)
                        kb.wait(sc, trx)
                        sc.activation(STAT[0:1, 0:512], PST[0:1, 0:512], AF.Copy,
                                      scale=1.0 / D)
                        kb.tag(sc.activation(STAT[0:1, 1024:1536], PST[0:1, 512:1024],
                                             AF.Copy, scale=1.0 / D), "ac")
                    kb.sc(f_sc)
                    tok_me2 = kb.tok("ac")

                    def f_ve(v, t0=t0, tok=tok_me2):
                        kb.wait(v, tok)
                        m = STAT[0:1, 0:512]
                        vv = STAT[0:1, 1024:1536]
                        sc2 = STAT[0:1, 1536:2048]
                        v.tensor_mul(sc2, m, m)
                        v.drain()
                        v.tensor_sub(vv, vv, sc2)
                        v.drain()
                        ti = v.tensor_add(vv, vv, EPSR[0:1, :])
                        v.drain()
                        kb.tag(ti, "ev")
                    kb.ve(f_ve)
                    tok_stats_read = kb.tok("ev")
                    tok_var = tok_stats_read

                    def f_sc(sc, t0=t0, tok=tok_var):
                        kb.wait(sc, tok)
                        kb.tag(sc.activation(STAT[0:1, 1536:2048],
                                             STAT[0:1, 1024:1536],
                                             AF.Sqrt), "ac")
                    kb.sc(f_sc)
                    tok_sqrt = kb.tok("ac")

                    def f_ve(v, t0=t0, tok=tok_sqrt):
                        kb.wait(v, tok)
                        rs = STAT[0:1, 512:1024]
                        mrs = STAT[0:1, 1536:2048]
                        v.reciprocal(rs, mrs)
                        v.drain()
                        ti = v.tensor_mul(mrs, STAT[0:1, 0:512], rs)
                        v.drain()
                        kb.tag(ti, "ev")
                    kb.ve(f_ve)
                    tok_rows = kb.tok("ev")

                    def f_te(te, t0=t0, tok=tok_rows, tok_prev=tok_rep_done):
                        kb.wait(te, tok)
                        kb.wait(te, tok_prev)
                        te.matmul(PREP[:, 0:512], ONESROWF[:, :],
                                  STAT[0:1, 512:1024], start=True, stop=True)
                        kb.tag(te.matmul(PREP[:, 512:1024], ONESROWF[:, :],
                                         STAT[0:1, 1536:2048],
                                         start=True, stop=True), "mm")
                    kb.te(f_te)
                    tok_rep_mm = kb.tok("mm")
                    tok_rep_mm_prev = tok_rep_mm

                    def f_ve(v, t0=t0, tok=tok_rep_mm):
                        kb.wait(v, tok)
                        v.tensor_copy(RSTDR[:, t0:t0 + 512], PREP[:, 0:512])
                        kb.tag(v.tensor_copy(MRSR[:, t0:t0 + 512], PREP[:, 512:1024]), "ev")
                    kb.ve(f_ve)
                    tok_rep_done = kb.tok("ev")
                    if extra_rows is not None:
                        tok_extra_prev = extra_rows(tb, tok_rows)

                for c in range(NCH):
                    for tb in range(4):
                        xs = X[:, c * TL + tb * 512: c * TL + tb * 512 + 512]
                        hs = HB[:, c * TL + tb * 512: c * TL + tb * 512 + 512]

                        if apply5 is not None:
                            RAR, RBR, W1LC, ZERO512 = apply5

                            def f_ve(v, xs=xs, hs=hs, c=c, tb=tb):
                                v.tensor_mul(TT[:, :], xs, RAR[:, tb * 512:tb * 512 + 512])
                                v.drain()
                                v.tensor_sub(TT[:, :], TT[:, :],
                                             RBR[:, tb * 512:tb * 512 + 512])
                                v.drain()
                                ti = v.scalar_tensor_tensor(
                                    hs, TT[:, :], W1LC[:, c:c + 1], ZERO512[:, :],
                                    ALU.mult, ALU.add)
                                if c == NCH - 1 and tb == 3:
                                    kb.tag(ti, "ev")
                            kb.ve(f_ve)
                            continue

                        def f_ve(v, xs=xs, hs=hs, c=c, tb=tb):
                            v.tensor_mul(TT[:, :], xs, RSTDR[:, tb * 512:tb * 512 + 512])
                            v.drain()
                            ti = v.tensor_sub(hs, TT[:, :], MRSR[:, tb * 512:tb * 512 + 512])
                            if extra_apply is not None:
                                extra_apply(v, c, tb, TT)
                            if c == NCH - 1 and tb == 3:
                                kb.tag(ti, "ev")
                        kb.ve(f_ve)
                h_done = kb.tok("ev")
            return h_done

        def emit_stage_l0ffn(stg, exi):
            ag_in, ag_out = d_ag_in[exi], d_ag_out[exi]
            ret_in, ret_out = d_ret_in[exi], d_ret_out[exi]

            with contextlib.ExitStack() as stx:
                QF = stx.enter_context(nc.sbuf_tensor(f"QF{stg}", [128, NCH * TL], BF16))
                GT = stx.enter_context(nc.sbuf_tensor(f"GT{stg}", [16, TL], BF16))
                qsx = contextlib.ExitStack()
                WQF = qsx.enter_context(nc.sbuf_tensor(f"WQF{stg}", [128, NCH * 1024], BF16))
                WRT = qsx.enter_context(nc.sbuf_tensor(f"WRT{stg}", [128, NCH * 16], BF16))
                RTT = qsx.enter_context(nc.sbuf_tensor(f"RTT{stg}", [128, 16], F32))
                MCOL = qsx.enter_context(nc.sbuf_tensor(f"MCOL{stg}", [128, 8], F32))
                EQT = qsx.enter_context(nc.sbuf_tensor(f"EQT{stg}", [128, 16], F32))
                SGT = qsx.enter_context(nc.sbuf_tensor(f"SGT{stg}", [128, 16], F32))
                GTM = qsx.enter_context(nc.sbuf_tensor(f"GTM{stg}", [128, 16], BF16))
                stx0 = contextlib.ExitStack()
                HB = stx0.enter_context(nc.sbuf_tensor(f"HBs{stg}", [128, NCH * TL], BF16))
                h_done = emit_ln(f"s{stg}", HB)
                if SUB <= 1:
                    stx0.close()
                    qsx.close()
                    return

                def f_sy(sy):
                    kb.wait(sy, h_done)
                    kb.dma(sy, "dq", out=ag_in[:, :].rearrange("(c p) t -> p c t", p=128),
                           in_=HB[:, :].rearrange("p (c t) -> p c t", t=TL))
                kb.sy(f_sy)
                tok_hstaged = kb.tok("dq")

                PP = qsx.enter_context(nc.psum_tensor(f"pp{stg}", [128, 1024], F32))
                PRT = qsx.enter_context(nc.psum_tensor(f"prt{stg}", [128, 1024], F32))
                PTR = qsx.enter_context(nc.psum_tensor(f"ptr{stg}", [16, 2048], BF16))

                if True:

                    def f_gp(g):
                        kb.dma(g, "dg", out=WQF[:, :].rearrange("p (dc o) -> p dc o", o=1024),
                               in_=d_wqfT[:, :].rearrange("(dc p) o -> p dc o", p=128))
                        kb.dma(g, "dg", out=WRT[:, :].rearrange("p (dc o) -> p dc o", o=16),
                               in_=d_wrtT[:, :].rearrange("(dc p) o -> p dc o", p=128))
                    kb.gp(f_gp)
                    tok_wqf = kb.tok("dg")

                    if DBG:
                        def f_dbgw(sy, tok=tok_wqf):
                            kb.wait(sy, tok)
                            kb.dma(sy, "dq", out=d_dbg_os[:, 0:8192], in_=WQF[:, :])
                        kb.sy(f_dbgw)

                    def f_gp(g):
                        kb.wait(g, tok_hstaged)
                        kb.tag(g.collective_compute(
                            "AllGather", ALU.bypass, ins=[ag_in[:, :]], outs=[ag_out[:, :]],
                            replica_groups=[list(range(NC))]), "cc")
                    kb.gp(f_gp)
                    tok_ag = kb.tok("cc")

                    ev_par = [None, None]
                    par = 0
                    first = True
                    for oc in range(NCH):
                        for tb in range(4):
                            w8 = tok_wqf if first else None
                            evp = ev_par[par]

                            def f_te(te, oc=oc, tb=tb, par=par, w8=w8, evp=evp):
                                if w8 is not None:
                                    kb.wait(te, w8)
                                    kb.wait(te, h_done)
                                kb.wait(te, evp)
                                for dc in range(NCH):
                                    mi = te.matmul(
                                        PP[:, par * 512:par * 512 + 512],
                                        WQF[:, dc * 1024 + oc * 128: dc * 1024 + oc * 128 + 128],
                                        HB[:, dc * TL + tb * 512: dc * TL + tb * 512 + 512],
                                        start=(dc == 0), stop=(dc == NCH - 1))
                                kb.tag(mi, "mm")
                            kb.te(f_te)
                            tok_mm = kb.tok("mm")

                            def f_ve(v, oc=oc, tb=tb, par=par, tok=tok_mm):
                                kb.wait(v, tok)
                                kb.tag(v.tensor_copy(
                                    QF[:, oc * TL + tb * 512: oc * TL + tb * 512 + 512],
                                    PP[:, par * 512:par * 512 + 512]), "ev")
                            kb.ve(f_ve)
                            ev_par[par] = kb.tok("ev")
                            par ^= 1
                            first = False
                    tok_qf_done_dbg = kb.tok("ev")

                    gate_ev = None
                    prt_free = [None, None]
                    ptr_free = [None, None]
                    prev_tpose = None
                    for tt in range(16):
                        tpar = tt % 2

                        def f_te(te, tt=tt, tpar=tpar, pf=prt_free[tpar]):
                            kb.wait(te, pf)
                            for dc in range(NCH):
                                mi = te.matmul(PRT[:, tpar * 512:tpar * 512 + 16],
                                               HB[:, dc * TL + tt * 128: dc * TL + tt * 128 + 128],
                                               WRT[:, dc * 16:(dc + 1) * 16],
                                               start=(dc == 0), stop=(dc == NCH - 1))
                            kb.tag(mi, "mm")
                        kb.te(f_te)
                        tok_prt = kb.tok("mm")

                        def f_ve(v, tt=tt, tpar=tpar, tok=tok_prt):
                            kb.wait(v, tok)
                            v.tensor_copy(RTT[:, 0:16], PRT[:, tpar * 512:tpar * 512 + 16])
                            v.drain()
                            ti = v.reduce_max(MCOL[:, 0:1], RTT[:, 0:16], axis=AX.X)
                            v.drain()
                            kb.tag(ti, "ev")
                        kb.ve(f_ve)
                        tok_rtt = kb.tok("ev")
                        prt_free[tpar] = tok_rtt

                        def f_sc(sc, tok=tok_rtt):
                            kb.wait(sc, tok)
                            kb.tag(sc.activation(EQT[:, :], RTT[:, 0:16], AF.Identity,
                                                 scale=-1.0, bias=MCOL[:, 0:1]), "ac")
                        kb.sc(f_sc)
                        tok_d1 = kb.tok("ac")

                        def f_ve(v, tok=tok_d1, ptp=prev_tpose):
                            kb.wait(v, tok)
                            kb.wait(v, ptp)
                            ti = v.tensor_tensor(EQT[:, :], EQT[:, :], ZERO16[:, :],
                                                 ALU.is_equal)
                            v.drain()
                            kb.tag(ti, "ev")
                        kb.ve(f_ve)
                        tok_eq = kb.tok("ev")

                        def f_sc(sc, tok=tok_eq):
                            kb.wait(sc, tok)
                            kb.tag(sc.activation(SGT[:, 0:16], EQT[:, :], AF.Copy,
                                                 scale=-1e30), "ac")
                        kb.sc(f_sc)
                        tok_t2 = kb.tok("ac")

                        def f_ve(v, tok=tok_t2):
                            kb.wait(v, tok)
                            v.tensor_add(EQT[:, :], SGT[:, 0:16], RTT[:, 0:16])
                            v.drain()
                            ti = v.reduce_max(MCOL[:, 0:1], EQT[:, :], axis=AX.X)
                            v.drain()
                            kb.tag(ti, "ev")
                        kb.ve(f_ve)
                        tok_m2 = kb.tok("ev")

                        def f_sc(sc, tok=tok_m2):
                            kb.wait(sc, tok)
                            sc.activation(EQT[:, :], RTT[:, 0:16], AF.Identity,
                                          scale=-1.0, bias=MCOL[:, 0:1])
                            kb.tag(sc.activation(SGT[:, 0:16], RTT[:, 0:16], AF.Sigmoid), "ac")
                        kb.sc(f_sc)
                        tok_d2 = kb.tok("ac")

                        def f_ve(v, tt=tt, tok=tok_d2):
                            kb.wait(v, tok)
                            v.tensor_tensor(EQT[:, :], EQT[:, :], ZERO16[:, :], ALU.is_le)
                            v.drain()
                            ti = v.tensor_mul(GTM[:, 0:16], EQT[:, :], SGT[:, 0:16])
                            v.drain()
                            if tt == 0 and DBG:
                                v.tensor_copy(MRSR[:, 0:16], RTT[:, 0:16])
                                v.tensor_copy(MRSR[:, 16:32], EQT[:, :])
                                v.tensor_copy(MRSR[:, 32:48], SGT[:, 0:16])
                                v.tensor_copy(MRSR[:, 48:64], GTM[:, 0:16])
                                v.tensor_copy(MRSR[:, 64:65], MCOL[:, 0:1])
                            kb.tag(ti, "ev")
                        kb.ve(f_ve)
                        tok_gtm = kb.tok("ev")

                        def f_te(te, tt=tt, tpar=tpar, tok=tok_gtm, pf=ptr_free[tpar]):
                            kb.wait(te, tok)
                            kb.wait(te, pf)
                            kb.tag(te.transpose(PTR[0:16, tpar * 1024:tpar * 1024 + 128],
                                                GTM[:, 0:16], IDE[:, :]), "mm")
                        kb.te(f_te)
                        tok_ptr = kb.tok("mm")
                        prev_tpose = tok_ptr

                        def f_ve(v, tt=tt, tpar=tpar, tok=tok_ptr):
                            kb.wait(v, tok)
                            kb.tag(v.tensor_copy(GT[0:16, tt * 128:(tt + 1) * 128],
                                                 PTR[0:16, tpar * 1024:tpar * 1024 + 128]), "ev")
                        kb.ve(f_ve)
                        ptr_free[tpar] = kb.tok("ev")
                    gates_done = kb.tok("ev")

                    def f_sy(sy, tok=gates_done):
                        kb.wait(sy, tok)
                        kb.dma(sy, "dq", out=d_gt[:, :], in_=GT[0:16, :])
                    kb.sy(f_sy)
                    gates_hbm = kb.tok("dq")

                if DBG:
                    def f_dbg0(sy):
                        sy.wait_ge(sems["ev"], C["ev"])
                        kb.dma(sy, "dq", out=d_dbg_hb[:, :], in_=ag_in[:, :])
                        kb.dma(sy, "dq", out=d_dbg_rstd[:, :], in_=RSTDR[:, :])
                        kb.dma(sy, "dq", out=d_dbg_mrs[:, :], in_=MRSR[:, :])
                        kb.dma(sy, "dq", out=d_dbg_gt[:, :], in_=d_gt[:, :])
                    kb.sy(f_dbg0)
                if SUB <= 2:
                    stx0.close()
                    return
                stx0.close()
                qsx.close()

                with contextlib.ExitStack() as esx:
                    KTE = esx.enter_context(nc.sbuf_tensor(f"KTE{stg}", [128, 2 * 4096], BF16))
                    VE = esx.enter_context(nc.sbuf_tensor(f"VE{stg}", [128, 2 * 4096], BF16))
                    GTB = esx.enter_context(nc.sbuf_tensor(f"GTB{stg}", [128, 16 * 512], BF16))
                    ASB = esx.enter_context(nc.sbuf_tensor(f"ASB{stg}", [128, 2 * 1024], BF16))
                    AGB = esx.enter_context(nc.sbuf_tensor(f"AGB{stg}", [128, 2 * 1024], BF16))
                    FB = esx.enter_context(nc.sbuf_tensor(f"FB{stg}", [128, 2 * 512], BF16))
                    GROW = esx.enter_context(nc.sbuf_tensor(f"GROW{stg}", [1, 16 * 512], BF16))
                    PSS = esx.enter_context(nc.psum_tensor(f"pss{stg}", [128, 2048], F32))
                    POFF = esx.enter_context(nc.psum_tensor(f"poff{stg}", [128, 512], F32))
                    PGR = esx.enter_context(nc.psum_tensor(f"pgr{stg}", [128, 1024], F32))

                    wtoks = {}
                    hh_done_toks = {}
                    fb_toks = [None, None]

                    def load_expw(g, tb, hh, par):
                        kb.dma(g, "dg",
                               out=KTE[:, par * 4096:(par + 1) * 4096].rearrange(
                                   "p (e s) -> p e s", s=256),
                               in_=d_ktf[:, hh].rearrange("e p s -> p e s"))
                        for sc_ in range(2):
                            tk = kb.dma(
                                g, "dg",
                                out=VE[:, par * 4096 + sc_ * 2048: par * 4096 + sc_ * 2048 + 2048].rearrange(
                                    "p (e d) -> p e d", d=128),
                                in_=d_vf[:, hh, sc_ * 128:(sc_ + 1) * 128, :].rearrange("e p d -> p e d"))
                        wtoks[(tb, hh)] = tk

                    def f_gp(g):
                        kb.wait(g, gates_done)
                        load_expw(g, 0, 0, 0)
                        load_expw(g, 0, 1, 1)
                    kb.gp(f_gp)
                    if DBG:
                        def f_dbgk(sy, tok=wtoks[(0, 1)]):
                            kb.wait(sy, tok)
                            kb.dma(sy, "dq", out=d_dbg_os[:, 8192:12288],
                                   in_=KTE[:, 0:4096])
                        kb.sy(f_dbgk)

                    gtb_done = None
                    grep_ev = [None, None]
                    grow_prev = None
                    for tb in range(4):
                        def f_sy(sy, tb=tb, tok=gates_hbm, pv=grow_prev):
                            kb.wait(sy, tok)
                            kb.wait(sy, pv)
                            kb.dma(sy, "dq",
                                   out=GROW[0:1, :].rearrange("p (e t) -> p e t", t=512),
                                   in_=d_gt[:, tb * 512:tb * 512 + 512])
                        kb.sy(f_sy)
                        tok_grow = kb.tok("dq")

                        gpar = 0
                        for e in range(16):
                            def f_te(te, e=e, tb=tb, gpar=gpar,
                                     gd=(tok_grow if e == 0 else None),
                                     evp=grep_ev[gpar]):
                                if gd is not None:
                                    kb.wait(te, gd)
                                kb.wait(te, evp)
                                kb.tag(te.matmul(PGR[:, gpar * 512:gpar * 512 + 512],
                                                 ONESROW[:, :],
                                                 GROW[0:1, e * 512:(e + 1) * 512],
                                                 start=True, stop=True), "mm")
                            kb.te(f_te)
                            tok_g = kb.tok("mm")

                            def f_ve(v, e=e, gpar=gpar, tok=tok_g):
                                kb.wait(v, tok)
                                kb.tag(v.tensor_copy(GTB[:, e * 512:(e + 1) * 512],
                                                     PGR[:, gpar * 512:gpar * 512 + 512]), "ev")
                            kb.ve(f_ve)
                            grep_ev[gpar] = kb.tok("ev")
                            gpar ^= 1
                        gtb_done = kb.tok("ev")
                        grow_prev = gtb_done
                        if DBG and tb == 0:
                            def f_dbgg(sy, tok=gtb_done):
                                kb.wait(sy, tok)
                                kb.dma(sy, "dq", out=d_dbg_os[:, 12288:16384],
                                       in_=GTB[:, 0:4096])
                            kb.sy(f_dbgg)

                        for hh in range(8):
                            par = hh % 2

                            spar_toks = [None, None]
                            mul_toks = [None, None]
                            omm_tok = None
                            for e in range(16):
                                epar = e % 2
                                wtok = wtoks[(tb, hh)] if e == 0 else None

                                def f_te(te, e=e, tb=tb, hh=hh, par=par, epar=epar,
                                         wtok=wtok, gf=spar_toks[epar],
                                         gtbd=(gtb_done if e == 0 else None)):
                                    if wtok is not None:
                                        kb.wait(te, wtok)
                                    if gtbd is not None:
                                        kb.wait(te, gtbd)
                                    kb.wait(te, gf)
                                    for scc in range(2):
                                        mi = te.matmul(
                                            PSS[:, epar * 1024 + scc * 512: epar * 1024 + scc * 512 + 512],
                                            KTE[:, par * 4096 + e * 256 + scc * 128: par * 4096 + e * 256 + scc * 128 + 128],
                                            QF[:, hh * TL + tb * 512: hh * TL + tb * 512 + 512],
                                            start=True, stop=True)
                                    kb.tag(mi, "mm")
                                kb.te(f_te)
                                tok_smm = kb.tok("mm")

                                def f_sc(scx, epar=epar, tok=tok_smm, mf=mul_toks[epar]):
                                    kb.wait(scx, tok)
                                    if mf is not None:
                                        kb.wait(scx, mf[0])
                                        kb.wait(scx, mf[1])
                                    kb.tag(scx.activation(ASB[:, epar * 1024:(epar + 1) * 1024],
                                                          PSS[:, epar * 1024:(epar + 1) * 1024],
                                                          AF.Gelu), "ac")
                                kb.sc(f_sc)
                                tok_gelu = kb.tok("ac")
                                spar_toks[epar] = tok_gelu

                                def f_ve(v, e=e, epar=epar, tok=tok_gelu):
                                    kb.wait(v, tok)
                                    kb.tag(v.tensor_mul(AGB[:, epar * 1024:epar * 1024 + 512],
                                                        ASB[:, epar * 1024:epar * 1024 + 512],
                                                        GTB[:, e * 512:(e + 1) * 512]), "ev")
                                kb.ve(f_ve)
                                tok_mv = kb.tok("ev")

                                def f_gp(g, e=e, epar=epar, tok=tok_gelu):
                                    kb.wait(g, tok)
                                    kb.tag(g.tensor_mul(AGB[:, epar * 1024 + 512:epar * 1024 + 1024],
                                                        ASB[:, epar * 1024 + 512:epar * 1024 + 1024],
                                                        GTB[:, e * 512:(e + 1) * 512]), "gv")
                                kb.gp(f_gp)
                                tok_mg = kb.tok("gv")
                                mul_toks[epar] = (tok_mv, tok_mg)
                                if DBG and tb == 0 and hh == 0 and e == 0:
                                    def f_dbga(v, tv=tok_mv, tg=tok_mg):
                                        kb.wait(v, tv)
                                        kb.wait(v, tg)
                                        v.tensor_copy(MRSR[:, 0:1024], ASB[:, 0:1024])
                                        kb.tag(v.tensor_copy(MRSR[:, 1024:2048], AGB[:, 0:1024]), "ev")
                                    kb.ve(f_dbga)

                                prev_ffo = None
                                if e == 0:
                                    pk = (tb, hh - 1) if hh >= 1 else (((tb - 1, 7)) if tb > 0 else None)
                                    prev_ffo = hh_done_toks.get(pk) if pk else None

                                def f_te(te, e=e, par=par, epar=epar, tv=tok_mv, tg=tok_mg, pf=prev_ffo):
                                    kb.wait(te, tv)
                                    kb.wait(te, tg)
                                    kb.wait(te, pf)
                                    for scc in range(2):
                                        mi = te.matmul(
                                            POFF[:, :],
                                            VE[:, par * 4096 + scc * 2048 + e * 128: par * 4096 + scc * 2048 + e * 128 + 128],
                                            AGB[:, epar * 1024 + scc * 512: epar * 1024 + scc * 512 + 512],
                                            start=(e == 0 and scc == 0), stop=(e == 15 and scc == 1))
                                    kb.tag(mi, "mm")
                                kb.te(f_te)
                                omm_tok = kb.tok("mm")

                            fpar = (tb * 8 + hh) % 2

                            def f_ve(v, hh=hh, tb=tb, fpar=fpar, tok=omm_tok, fbf=fb_toks[fpar]):
                                kb.wait(v, tok)
                                kb.wait(v, fbf)
                                kb.tag(v.tensor_copy(FB[:, fpar * 512:(fpar + 1) * 512],
                                                     POFF[:, :]), "ev")
                            kb.ve(f_ve)
                            tok_fb = kb.tok("ev")
                            hh_done_toks[(tb, hh)] = tok_fb

                            nxt = (tb, hh + 2) if hh + 2 < 8 else ((tb + 1, hh + 2 - 8) if tb + 1 < 4 else None)
                            if nxt is not None:
                                def f_gp(g, nxt=nxt, par=par, pu=omm_tok):
                                    kb.wait(g, pu)
                                    load_expw(g, nxt[0], nxt[1], par)
                                kb.gp(f_gp)

                            def f_sy(sy, hh=hh, tb=tb, fpar=fpar, tok=tok_fb):
                                kb.wait(sy, tok)
                                kb.dma(sy, "dq",
                                       out=d_ffo[hh * 128:(hh + 1) * 128, tb * 512:tb * 512 + 512],
                                       in_=FB[:, fpar * 512:(fpar + 1) * 512])
                            kb.sy(f_sy)
                            fb_toks[fpar] = kb.tok("dq")
                    experts_done = kb.tok("dq")
                    if DBG:
                        def f_dbg(sy, tok=experts_done):
                            kb.wait(sy, tok)
                            kb.wait(sy, tok_qf_done_dbg)
                            kb.dma(sy, "dq", out=d_dbg_qf[:, :], in_=QF[:, :])
                        kb.sy(f_dbg)

            if DBG:
                def f_dbgf(sy, tok=experts_done):
                    kb.wait(sy, tok)
                    sy.wait_ge(sems["ev"], C["ev"])
                    kb.dma(sy, "dq", out=d_dbg_ffo[:, :], in_=d_ffo[:, :])
                    kb.dma(sy, "dq", out=d_dbg_mrs[:, :], in_=MRSR[:, :])
                kb.sy(f_dbgf)
            if SUB <= 3:
                return
            with contextlib.ExitStack() as asx:
                CSN = asx.enter_context(nc.sbuf_tensor(f"CSN{stg}", [128, S], BF16))
                CMASK = asx.enter_context(nc.sbuf_tensor(f"CMASK{stg}", [128, 4 * 512], BF16))
                WQKV = asx.enter_context(nc.sbuf_tensor(f"WQKV{stg}", [128, NCH * 384], BF16))
                QS = asx.enter_context(nc.sbuf_tensor(f"QS{stg}", [128, S], BF16))
                KS = asx.enter_context(nc.sbuf_tensor(f"KS{stg}", [128, S], BF16))
                VT = asx.enter_context(nc.sbuf_tensor(f"VT{stg}", [128, S], BF16))
                HL = asx.enter_context(nc.sbuf_tensor(f"HL{stg}", [128, 2 * 8 * 512], BF16))
                UB = asx.enter_context(nc.sbuf_tensor(f"UB{stg}", [128, 2 * 512], BF16))
                AU = asx.enter_context(nc.sbuf_tensor(f"AU{stg}", [128, 512], BF16))
                CSR = asx.enter_context(nc.sbuf_tensor(f"CSR{stg}", [1, 512], F32))
                RECS = asx.enter_context(nc.sbuf_tensor(f"RECS{stg}", [128, 512], F32))
                RSA = asx.enter_context(nc.sbuf_tensor(f"RSA{stg}", [64, 512], F32))
                RSB = asx.enter_context(nc.sbuf_tensor(f"RSB{stg}", [64, 512], F32))
                VSP = asx.enter_context(nc.sbuf_tensor(f"VSP{stg}", [128, 2 * 512], BF16))
                OSB = asx.enter_context(nc.sbuf_tensor(f"OSB{stg}", [128, 2 * 512], BF16))
                PQKV = asx.enter_context(nc.psum_tensor(f"pqkv{stg}", [128, 1024], F32))
                PSS2 = asx.enter_context(nc.psum_tensor(f"psS{stg}", [128, 1024], F32))
                PSO = asx.enter_context(nc.psum_tensor(f"psO{stg}", [128, 512], F32))
                PREC = asx.enter_context(nc.psum_tensor(f"prec{stg}", [128, 512], F32))
                CS1 = asx.enter_context(nc.psum_tensor(f"cs1{stg}", [1, 512], F32))

                def f_gp(g):
                    kb.wait(g, experts_done)
                    kb.dma(g, "dg", out=WQKV[:, :].rearrange("p (dc o) -> p dc o", o=384),
                           in_=d_wqkvT[:, :].rearrange("(dc p) o -> p dc o", p=128))
                kb.gp(f_gp)
                tok_wqkv = kb.tok("dg")

                def f_sy(sy):
                    kb.wait(sy, experts_done)
                    kb.dma(sy, "dq", out=CSN[0:64, :], in_=d_cos[:, :])
                    kb.dma(sy, "dq", out=CSN[64:128, :], in_=d_sin[:, :])
                    kb.dma(sy, "dq", out=CMASK[:, :].rearrange("p (m t) -> p m t", t=512),
                           in_=d_cmask[:, :, :].rearrange("m p t -> p m t"))
                kb.sy(f_sy)
                tok_cm = kb.tok("dq")

                ret_toks = []
                for b in range(B):
                    hl_toks = [None, None]
                    first_mm = (b == 0)
                    proj_ev = [None, None]
                    vsp_toks = [None, None]
                    ppar = 0
                    for tb in range(16):
                        lpar = tb % 2
                        gtok = b * 8192 + tb * 512
                        r = gtok // TL
                        tcol = gtok % TL

                        def f_sy(sy, lpar=lpar, r=r, tcol=tcol, tb=tb,
                                 agw=(tok_ag if tb == 0 and b == 0 else None),
                                 cons=hl_toks[lpar]):
                            if agw is not None:
                                kb.wait(sy, agw)
                            kb.wait(sy, cons)
                            src = ag_out[:, :].rearrange("(r c p) t -> p r c t", p=128, c=NCH)[:, r, :, tcol:tcol + 512]
                            kb.dma(sy, "dq",
                                   out=HL[:, lpar * 4096:(lpar + 1) * 4096].rearrange(
                                       "p (c t) -> p c t", t=512),
                                   in_=src)
                        kb.sy(f_sy)
                        tok_hl = kb.tok("dq")

                        for m in (2, 0, 1):
                            def f_te(te, m=m, lpar=lpar, ppar=ppar, tok=tok_hl,
                                     w8=(tok_wqkv if first_mm else None), evp=proj_ev[ppar]):
                                if w8 is not None:
                                    kb.wait(te, w8)
                                kb.wait(te, tok)
                                kb.wait(te, evp)
                                for dc in range(NCH):
                                    mi = te.matmul(
                                        PQKV[:, ppar * 512:ppar * 512 + 512],
                                        WQKV[:, dc * 384 + m * 128: dc * 384 + m * 128 + 128],
                                        HL[:, lpar * 4096 + dc * 512: lpar * 4096 + dc * 512 + 512],
                                        start=(dc == 0), stop=(dc == NCH - 1))
                                kb.tag(mi, "mm")
                            kb.te(f_te)
                            tok_pm = kb.tok("mm")
                            first_mm = False

                            pos0 = tb * 512
                            if m == 2:
                                vpar = tb % 2

                                def f_ve(v, ppar=ppar, vpar=vpar, tok=tok_pm, vf=vsp_toks[vpar]):
                                    kb.wait(v, tok)
                                    kb.wait(v, vf)
                                    kb.tag(v.tensor_copy(VSP[:, vpar * 512:(vpar + 1) * 512],
                                                         PQKV[:, ppar * 512:ppar * 512 + 512]), "ev")
                                kb.ve(f_ve)
                                tok_vs = kb.tok("ev")

                                def f_sy(sy, pos0=pos0, vpar=vpar, tok=tok_vs):
                                    kb.wait(sy, tok)
                                    kb.dma(sy, "dq", out=d_vsp[:, pos0:pos0 + 512],
                                           in_=VSP[:, vpar * 512:(vpar + 1) * 512])
                                kb.sy(f_sy)
                                vsp_toks[vpar] = kb.tok("dq")
                                proj_ev[ppar] = tok_vs
                            else:
                                dst = [QS, KS][m]

                                def f_ve(v, m=m, ppar=ppar, dst=dst, pos0=pos0, tok=tok_pm,
                                         cm=(tok_cm if b == 0 and tb == 0 else None)):
                                    kb.wait(v, tok)
                                    kb.wait(v, cm)
                                    ps = PQKV[:, ppar * 512:ppar * 512 + 512]
                                    o = dst[:, pos0:pos0 + 512]
                                    cosb = CSN[0:64, pos0:pos0 + 512]
                                    sinb = CSN[64:128, pos0:pos0 + 512]
                                    pl = ps[0:64, :]
                                    ph = ps[64:128, :]
                                    v.tensor_mul(RSA[:, :], ph, sinb)
                                    v.tensor_mul(RSB[:, :], pl, cosb)
                                    v.drain()
                                    v.tensor_sub(o[0:64, :], RSB[:, :], RSA[:, :])
                                    v.tensor_mul(RSA[:, :], pl, sinb)
                                    v.tensor_mul(RSB[:, :], ph, cosb)
                                    v.drain()
                                    ti = v.tensor_add(o[64:128, :], RSA[:, :], RSB[:, :])
                                    kb.tag(ti, "ev")
                                kb.ve(f_ve)
                                proj_ev[ppar] = kb.tok("ev")
                            ppar ^= 1
                        hl_toks[lpar] = proj_ev[(ppar + 1) % 2]
                    tok_proj = kb.tok("ev")
                    tok_vlast = vsp_toks[1]

                    def f_sy(sy, t1=vsp_toks[0], t2=vsp_toks[1]):
                        kb.wait(sy, t1)
                        kb.wait(sy, t2)
                        for kc in range(64):
                            kb.dma(sy, "dq", out=VT[:, kc * 128:(kc + 1) * 128],
                                   in_=d_vsp[:, kc * 128:(kc + 1) * 128], transpose=True)
                    kb.sy(f_sy)
                    tok_vt = kb.tok("dq")

                    exp_par = [None, None]
                    u_cons = [None, None]
                    osb_toks = [None, None]
                    last_ret = None
                    prev_cs1 = None
                    prev_os = None
                    for qb in range(16):
                        kmax = 4 * (qb + 1)
                        av_tok = None
                        for kc in range(kmax):
                            spar = kc % 2
                            diag = kc >= 4 * qb

                            def f_te(te, qb=qb, kc=kc, spar=spar,
                                     vt=(tok_vt if qb == 0 and kc == 0 else None),
                                     ef=exp_par[spar]):
                                if vt is not None:
                                    kb.wait(te, vt)
                                kb.wait(te, ef)
                                kb.tag(te.matmul(PSS2[:, spar * 512:spar * 512 + 512],
                                                 KS[:, kc * 128:(kc + 1) * 128],
                                                 QS[:, qb * 512:qb * 512 + 512],
                                                 start=True, stop=True), "mm")
                            kb.te(f_te)
                            tok_s = kb.tok("mm")

                            def f_sc(scx, spar=spar, tok=tok_s, uc=u_cons[spar]):
                                kb.wait(scx, tok)
                                kb.wait(scx, uc)
                                kb.tag(scx.activation(UB[:, spar * 512:(spar + 1) * 512],
                                                      PSS2[:, spar * 512:spar * 512 + 512],
                                                      AF.Exp, scale=1.0 / math.sqrt(128.0)), "ac")
                            kb.sc(f_sc)
                            tok_exp = kb.tok("ac")
                            exp_par[spar] = tok_exp

                            m_off = kc - 4 * qb

                            def f_ve(v, qb=qb, kc=kc, spar=spar, diag=diag, m_off=m_off,
                                     tok=tok_exp, pc1=(prev_cs1 if kc == 0 else None)):
                                kb.wait(v, tok)
                                kb.wait(v, pc1)
                                u = UB[:, spar * 512:(spar + 1) * 512]
                                if diag:
                                    v.tensor_mul(u, u, CMASK[:, m_off * 512:(m_off + 1) * 512])
                                    v.drain()
                                if kc == 0:
                                    ti = v.tensor_copy(AU[:, :], u)
                                else:
                                    ti = v.tensor_add(AU[:, :], AU[:, :], u)
                                v.drain()
                                kb.tag(ti, "ev")
                            kb.ve(f_ve)
                            tok_u = kb.tok("ev")

                            def f_te(te, qb=qb, kc=kc, spar=spar, kmax=kmax, tok=tok_u,
                                     pos=(prev_os if kc == 0 else None)):
                                kb.wait(te, tok)
                                kb.wait(te, pos)
                                kb.tag(te.matmul(PSO[:, :],
                                                 VT[:, kc * 128:(kc + 1) * 128],
                                                 UB[:, spar * 512:(spar + 1) * 512],
                                                 start=(kc == 0), stop=(kc == kmax - 1)), "mm")
                            kb.te(f_te)
                            av_tok = kb.tok("mm")
                            u_cons[spar] = av_tok

                        au_last = kb.tok("ev")

                        def f_te(te, tok=au_last):
                            kb.wait(te, tok)
                            kb.tag(te.matmul(CS1[0:1, :], ONESB[:, 0:1], AU[:, :],
                                             start=True, stop=True), "mm")
                        kb.te(f_te)
                        tok_cs1 = kb.tok("mm")
                        prev_cs1 = tok_cs1

                        def f_ve(v, tok=tok_cs1):
                            kb.wait(v, tok)
                            kb.tag(v.reciprocal(CSR[0:1, :], CS1[0:1, :]), "ev")
                        kb.ve(f_ve)
                        tok_cs = kb.tok("ev")

                        def f_te(te, tok=tok_cs):
                            kb.wait(te, tok)
                            kb.tag(te.matmul(PREC[:, :], ONESROWF[:, :], CSR[0:1, :],
                                             start=True, stop=True), "mm")
                        kb.te(f_te)
                        tok_rec = kb.tok("mm")

                        opar = qb % 2

                        def f_ve(v, qb=qb, opar=opar, tokr=tok_rec, toka=av_tok, of=osb_toks[opar]):
                            kb.wait(v, tokr)
                            kb.wait(v, toka)
                            kb.wait(v, of)
                            v.tensor_copy(RECS[:, :], PREC[:, :])
                            v.drain()
                            kb.tag(v.tensor_mul(OSB[:, opar * 512:(opar + 1) * 512],
                                                PSO[:, :], RECS[:, :]), "ev")
                        kb.ve(f_ve)
                        tok_os = kb.tok("ev")
                        prev_os = tok_os

                        def f_sy(sy, b=b, qb=qb, opar=opar, tok=tok_os):
                            kb.wait(sy, tok)
                            kb.dma(sy, "dq",
                                   out=ret_in[:, b * S + qb * 512: b * S + qb * 512 + 512],
                                   in_=OSB[:, opar * 512:(opar + 1) * 512])
                        kb.sy(f_sy)
                        osb_toks[opar] = kb.tok("dq")
                        last_ret = osb_toks[opar]
                    ret_toks.append(last_ret)

                def f_gp(g, t1=ret_toks[0], t2=ret_toks[1]):
                    kb.wait(g, t1)
                    kb.wait(g, t2)
                    kb.tag(g.collective_compute(
                        "AllGather", ALU.bypass, ins=[ret_in[:, :]], outs=[ret_out[:, :]],
                        replica_groups=[list(range(NC))]), "cc")
                kb.gp(f_gp)
                tok_ret = kb.tok("cc")

            if SUB <= 4:
                return
            if DBG:
                def f_dbg(sy):
                    sy.wait_ge(sems["ev"], C["ev"])
                    sy.wait_ge(sems["cc"], C["cc"])
                    kb.dma(sy, "dq", out=d_dbg_os[:, :], in_=ret_in[:, :])
                kb.sy(f_dbg)

            barrier()

            with contextlib.ExitStack() as wsx:
                WOT = wsx.enter_context(nc.sbuf_tensor(f"WOT{stg}", [128, 16 * 1024], BF16))
                ATT = wsx.enter_context(nc.sbuf_tensor(f"ATT{stg}", [128, 2 * 8 * 512], BF16))
                WSC = wsx.enter_context(nc.sbuf_tensor(f"WSC{stg}", [128, 2 * 512], F32))
                FFB = wsx.enter_context(nc.sbuf_tensor(f"FFB{stg}", [128, 2 * 8 * 512], BF16))
                PWO = wsx.enter_context(nc.psum_tensor(f"pwo{stg}", [128, 1024], F32))

                def f_gp(g):
                    kb.dma(g, "dg", out=WOT[:, :].rearrange("p (jc o) -> p jc o", o=1024),
                           in_=d_woutT[:, :].rearrange("(jc p) o -> p jc o", p=128))
                kb.gp(f_gp)
                tok_wot = kb.tok("dg")

                att_toks = [None, None]
                wpar = 0
                wo_ev = [None, None]
                for tb in range(4):
                    lpar = tb % 2

                    def f_sy(sy, tb=tb, lpar=lpar, tok=(tok_ret if tb == 0 else None),
                             cons=att_toks[lpar]):
                        if tok is not None:
                            kb.wait(sy, tok)
                        kb.wait(sy, cons)
                        pid = kb.pid(sy)
                        src = ret_out[:, :].rearrange("(jc p) t -> p jc t", p=128)
                        kb.dma(sy, "dq",
                               out=ATT[:, lpar * 4096:(lpar + 1) * 4096].rearrange(
                                   "p (jc t) -> p jc t", t=512),
                               in_=src[:, :, bass.ts(pid, TL)][:, :, tb * 512:tb * 512 + 512])
                        kb.dma(sy, "dq",
                               out=FFB[:, lpar * 4096:(lpar + 1) * 4096].rearrange(
                                   "p (jc t) -> p jc t", t=512),
                               in_=d_ffo[:, :].rearrange("(jc p) t -> p jc t", p=128)[:, :, tb * 512:tb * 512 + 512])
                    kb.sy(f_sy)
                    tok_att = kb.tok("dq")

                    for oc in range(NCH):
                        def f_te(te, tb=tb, oc=oc, lpar=lpar, wpar=wpar, tok=tok_att,
                                 w8=(tok_wot if tb == 0 and oc == 0 else None),
                                 evp=wo_ev[wpar]):
                            if w8 is not None:
                                kb.wait(te, w8)
                            kb.wait(te, tok)
                            kb.wait(te, evp)
                            for jc in range(16):
                                if jc < 8:
                                    rhs = ATT[:, lpar * 4096 + jc * 512: lpar * 4096 + jc * 512 + 512]
                                else:
                                    rhs = FFB[:, lpar * 4096 + (jc - 8) * 512: lpar * 4096 + (jc - 8) * 512 + 512]
                                mi = te.matmul(PWO[:, wpar * 512:wpar * 512 + 512],
                                               WOT[:, jc * 1024 + oc * 128: jc * 1024 + oc * 128 + 128],
                                               rhs, start=(jc == 0), stop=(jc == 15))
                            kb.tag(mi, "mm")
                        kb.te(f_te)
                        tok_wo = kb.tok("mm")

                        def f_sc(sc, wpar=wpar, tok=tok_wo):
                            kb.wait(sc, tok)
                            kb.tag(sc.activation(WSC[:, wpar * 512:wpar * 512 + 512],
                                                 PWO[:, wpar * 512:wpar * 512 + 512],
                                                 AF.Copy, scale=0.5), "ac")
                        kb.sc(f_sc)
                        tok_wsc = kb.tok("ac")

                        def f_ve(v, tb=tb, oc=oc, wpar=wpar, tok=tok_wsc):
                            kb.wait(v, tok)
                            xsl = X[:, oc * TL + tb * 512: oc * TL + tb * 512 + 512]
                            kb.tag(v.tensor_add(xsl, xsl,
                                                WSC[:, wpar * 512:wpar * 512 + 512]), "ev")
                        kb.ve(f_ve)
                        wo_ev[wpar] = kb.tok("ev")
                        wpar ^= 1
                    att_toks[lpar] = wo_ev[(wpar + 1) % 2]

            barrier()


        def emit_stage_l0attn(exi):
            ag_in, ag_out = d_ag_in[exi], d_ag_out[exi]
            ret_in, ret_out = d_ret_in[exi], d_ret_out[exi]

            with contextlib.ExitStack() as s2x:
                WQ2 = s2x.enter_context(nc.sbuf_tensor("WQ2", [128, NCH * 384], BF16))

                def f_gp(g):
                    kb.dma(g, "dg", out=WQ2[:, :].rearrange("p (dc o) -> p dc o", o=384),
                           in_=d_wqkv2[:, :].rearrange("(dc p) o -> p dc o", p=128))
                kb.gp(f_gp)
                tok_w2 = kb.tok("dg")

                stx0 = contextlib.ExitStack()
                HB = stx0.enter_context(nc.sbuf_tensor("HBs2", [128, NCH * TL], BF16))
                h_done = emit_ln("s2", HB)

                def f_sy(sy):
                    kb.wait(sy, h_done)
                    kb.dma(sy, "dq", out=ag_in[:, :].rearrange("(c p) t -> p c t", p=128),
                           in_=HB[:, :].rearrange("p (c t) -> p c t", t=TL))
                kb.sy(f_sy)
                tok_hstaged = kb.tok("dq")
                stx0.close()

                def f_gp(g):
                    kb.wait(g, tok_hstaged)
                    kb.tag(g.collective_compute(
                        "AllGather", ALU.bypass, ins=[ag_in[:, :]], outs=[ag_out[:, :]],
                        replica_groups=[list(range(NC))]), "cc")
                kb.gp(f_gp)
                tok_ag = kb.tok("cc")

                with contextlib.ExitStack() as a2x:
                    QSTK = a2x.enter_context(nc.sbuf_tensor("QSTK", [128, S], BF16))
                    KSTK = a2x.enter_context(nc.sbuf_tensor("KSTK", [128, S], BF16))
                    VT2 = a2x.enter_context(nc.sbuf_tensor("VT2", [128, S], BF16))
                    HL2 = a2x.enter_context(nc.sbuf_tensor("HL2", [128, 2 * 8 * 512], BF16))
                    UB2 = a2x.enter_context(nc.sbuf_tensor("UB2", [128, 2 * 512], BF16))
                    OS2 = a2x.enter_context(nc.sbuf_tensor("OS2", [128, 2 * 512], BF16))
                    SPB = a2x.enter_context(nc.sbuf_tensor("SPB", [128, 2 * 512], BF16))
                    SPV = a2x.enter_context(nc.sbuf_tensor("SPV", [128, 2 * 512], BF16))
                    SV1 = a2x.enter_context(nc.sbuf_tensor("SV1", [128, 8], F32))
                    SVT = a2x.enter_context(nc.sbuf_tensor("SVT", [128, 8], F32))
                    PQ2 = a2x.enter_context(nc.psum_tensor("pq2", [128, 1024], F32))
                    PSD = a2x.enter_context(nc.psum_tensor("psd2", [128, 1024], F32))
                    PSO2 = a2x.enter_context(nc.psum_tensor("pso2", [128, 512], F32))

                    def f_ve(v):
                        kb.tag(v.memset(SV1[:, :], 0.0), "ev")
                    kb.ve(f_ve)
                    tok_sv0 = kb.tok("ev")

                    hl_toks = [None, None]
                    proj_ev = [None, None]
                    spb_toks = [None, None]
                    spv_toks = [None, None]
                    ppar = 0
                    first_mm = True
                    for tb in range(32):
                        lpar = tb % 2
                        b = tb // 16
                        gtok = tb * 512
                        r = gtok // TL
                        tcol = gtok % TL
                        pos0 = gtok - b * 8192

                        def f_sy(sy, lpar=lpar, r=r, tcol=tcol,
                                 agw=(tok_ag if tb == 0 else None), cons=hl_toks[lpar]):
                            if agw is not None:
                                kb.wait(sy, agw)
                            kb.wait(sy, cons)
                            src = ag_out[:, :].rearrange("(r c p) t -> p r c t", p=128, c=NCH)[:, r, :, tcol:tcol + 512]
                            kb.dma(sy, "dq",
                                   out=HL2[:, lpar * 4096:(lpar + 1) * 4096].rearrange(
                                       "p (c t) -> p c t", t=512),
                                   in_=src)
                        kb.sy(f_sy)
                        tok_hl = kb.tok("dq")

                        for mi_ in range(3):
                            def f_te(te, mi_=mi_, lpar=lpar, ppar=ppar, tok=tok_hl,
                                     w8=(tok_w2 if first_mm else None), evp=proj_ev[ppar]):
                                if w8 is not None:
                                    kb.wait(te, w8)
                                kb.wait(te, tok)
                                kb.wait(te, evp)
                                for dc in range(NCH):
                                    mm_ = te.matmul(
                                        PQ2[:, ppar * 512:ppar * 512 + 512],
                                        WQ2[:, dc * 384 + mi_ * 128: dc * 384 + mi_ * 128 + 128],
                                        HL2[:, lpar * 4096 + dc * 512: lpar * 4096 + dc * 512 + 512],
                                        start=(dc == 0), stop=(dc == NCH - 1))
                                kb.tag(mm_, "mm")
                            kb.te(f_te)
                            tok_pm = kb.tok("mm")
                            first_mm = False

                            if mi_ == 2:
                                vpar = tb % 2

                                def f_ve(v, ppar=ppar, vpar=vpar, b=b, tok=tok_pm,
                                         vf=spv_toks[vpar], sv=(tok_sv0 if tb == 0 else None)):
                                    kb.wait(v, tok)
                                    kb.wait(v, vf)
                                    kb.wait(v, sv)
                                    ps = PQ2[:, ppar * 512:ppar * 512 + 512]
                                    v.tensor_copy(SPV[:, vpar * 512:(vpar + 1) * 512], ps)
                                    if b == 1:
                                        v.reduce_sum(SVT[0:64, 0:1], ps[0:64, :], axis=AX.X)
                                        v.reduce_sum(SVT[64:128, 1:2], ps[64:128, :], axis=AX.X)
                                        v.drain()
                                        v.tensor_copy(SVT[64:128, 2:3], SVT[0:64, 0:1])
                                        v.tensor_add(SV1[64:128, 0:1], SV1[64:128, 0:1],
                                                     SVT[64:128, 2:3])
                                        ti = v.tensor_add(SV1[64:128, 1:2], SV1[64:128, 1:2],
                                                          SVT[64:128, 1:2])
                                        kb.tag(ti, "ev")
                                    else:
                                        kb.tag(v.engine_nop(), "ev")
                                kb.ve(f_ve)
                                tok_vs = kb.tok("ev")

                                def f_sy(sy, gtok=gtok, vpar=vpar, tok=tok_vs):
                                    kb.wait(sy, tok)
                                    kb.dma(sy, "dq", out=d_vsp2[:, gtok:gtok + 512],
                                           in_=SPV[:, vpar * 512:(vpar + 1) * 512])
                                kb.sy(f_sy)
                                spv_toks[vpar] = kb.tok("dq")
                                proj_ev[ppar] = tok_vs
                            else:
                                dst = QSTK if mi_ == 0 else KSTK
                                neg = (mi_ == 0 and b == 1)
                                bpar = tb % 2

                                def f_sc(sc, ppar=ppar, b=b, pos0=pos0, dst=dst, neg=neg,
                                         tok=tok_pm):
                                    kb.wait(sc, tok)
                                    ps = PQ2[:, ppar * 512:ppar * 512 + 512]
                                    kb.tag(sc.activation(
                                        dst[b * 64:(b + 1) * 64, pos0:pos0 + 512],
                                        ps[0:64, :], AF.Copy,
                                        scale=(-1.0 if neg else 1.0)), "ac")
                                kb.sc(f_sc)
                                tok_sta = kb.tok("ac")

                                def f_ve(v, ppar=ppar, mi_=mi_, b=b, neg=neg, bpar=bpar,
                                         tok=tok_pm, bf=spb_toks[bpar]):
                                    kb.wait(v, tok)
                                    kb.wait(v, bf)
                                    ps = PQ2[:, ppar * 512:ppar * 512 + 512]
                                    kb.tag(v.tensor_copy(SPB[0:64, bpar * 512:bpar * 512 + 512],
                                                         ps[64:128, :]), "ev")
                                kb.ve(f_ve)
                                tok_spb = kb.tok("ev")

                                def f_sy(sy, mi_=mi_, gtok=gtok, bpar=bpar, tok=tok_spb,
                                         ts=tok_sta):
                                    kb.wait(sy, tok)
                                    kb.wait(sy, ts)
                                    kb.dma(sy, "dq",
                                           out=d_qkvb[mi_ * 64:(mi_ + 1) * 64, gtok:gtok + 512],
                                           in_=SPB[0:64, bpar * 512:bpar * 512 + 512])
                                kb.sy(f_sy)
                                spb_toks[bpar] = kb.tok("dq")
                                proj_ev[ppar] = spb_toks[bpar]
                            ppar ^= 1
                        hl_toks[lpar] = proj_ev[(ppar + 1) % 2]
                    proj_done = kb.tok("dq")

                    ret_last = None
                    for hh2 in range(2):
                        if hh2 == 1:
                            def f_sy(sy, tok=ret_last):
                                kb.wait(sy, tok)
                                for b in range(2):
                                    kb.dma(sy, "dq", out=QSTK[b * 64:(b + 1) * 64, :],
                                           in_=d_qkvb[0:64, b * S:(b + 1) * S])
                                    kb.dma(sy, "dq", out=KSTK[b * 64:(b + 1) * 64, :],
                                           in_=d_qkvb[64:128, b * S:(b + 1) * S])
                            kb.sy(f_sy)
                            tok_stkb = kb.tok("dq")

                            def f_sc(sc, tok=tok_stkb):
                                kb.wait(sc, tok)
                                kb.tag(sc.activation(QSTK[64:128, :], QSTK[64:128, :],
                                                     AF.Copy, scale=-1.0), "ac")
                            kb.sc(f_sc)
                            tok_stk = kb.tok("ac")
                        else:
                            tok_stk = proj_done

                        def f_sy(sy, hh2=hh2, tok=(spv_toks[0] if hh2 == 0 else None),
                                 tok2=(spv_toks[1] if hh2 == 0 else None), pu=ret_last):
                            kb.wait(sy, tok)
                            kb.wait(sy, tok2)
                            kb.wait(sy, pu)
                            for kc in range(64):
                                kb.dma(sy, "dq", out=VT2[:, kc * 128:kc * 128 + 64],
                                       in_=d_vsp2[hh2 * 64:(hh2 + 1) * 64, kc * 128:(kc + 1) * 128],
                                       transpose=True)
                                kb.dma(sy, "dq", out=VT2[:, kc * 128 + 64:kc * 128 + 128],
                                       in_=d_vsp2[hh2 * 64:(hh2 + 1) * 64, S + kc * 128:S + (kc + 1) * 128],
                                       transpose=True)
                        kb.sy(f_sy)
                        tok_vt = kb.tok("dq")

                        exp_par = [None, None]
                        u_cons = [None, None]
                        os_par = [None, None]
                        prev_os = None
                        for qb in range(16):
                            av_tok = None
                            for kc in range(64):
                                spar = kc % 2

                                def f_te(te, qb=qb, kc=kc, spar=spar,
                                         vt=(tok_vt if qb == 0 and kc == 0 else None),
                                         sk=(tok_stk if qb == 0 and kc == 0 else None),
                                         ef=exp_par[spar]):
                                    kb.wait(te, vt)
                                    kb.wait(te, sk)
                                    kb.wait(te, ef)
                                    kb.tag(te.matmul(PSD[:, spar * 512:spar * 512 + 512],
                                                     KSTK[:, kc * 128:(kc + 1) * 128],
                                                     QSTK[:, qb * 512:qb * 512 + 512],
                                                     start=True, stop=True), "mm")
                                kb.te(f_te)
                                tok_s = kb.tok("mm")

                                def f_sc(sc, spar=spar, tok=tok_s, uc=u_cons[spar]):
                                    kb.wait(sc, tok)
                                    kb.wait(sc, uc)
                                    kb.tag(sc.activation(UB2[:, spar * 512:(spar + 1) * 512],
                                                         PSD[:, spar * 512:spar * 512 + 512],
                                                         AF.Sigmoid, scale=1.0 / 32.0), "ac")
                                kb.sc(f_sc)
                                tok_p = kb.tok("ac")
                                exp_par[spar] = tok_p

                                def f_te(te, qb=qb, kc=kc, spar=spar, tok=tok_p,
                                         pos=(prev_os if kc == 0 else None)):
                                    kb.wait(te, tok)
                                    kb.wait(te, pos)
                                    kb.tag(te.matmul(PSO2[:, :],
                                                     VT2[:, kc * 128:(kc + 1) * 128],
                                                     UB2[:, spar * 512:(spar + 1) * 512],
                                                     start=(kc == 0), stop=(kc == 63)), "mm")
                                kb.te(f_te)
                                av_tok = kb.tok("mm")
                                u_cons[spar] = av_tok

                            opar = qb % 2

                            def f_ve(v, opar=opar, tok=av_tok, of=os_par[opar]):
                                kb.wait(v, tok)
                                kb.wait(v, of)
                                kb.tag(v.tensor_copy(OS2[0:64, opar * 512:opar * 512 + 512],
                                                     PSO2[0:64, :]), "ev")
                            kb.ve(f_ve)
                            tok_v0 = kb.tok("ev")

                            def f_sc(sc, hh2=hh2, opar=opar, tok=av_tok, of=os_par[opar]):
                                kb.wait(sc, tok)
                                kb.wait(sc, of)
                                kb.tag(sc.activation(OS2[64:128, opar * 512:opar * 512 + 512],
                                                     PSO2[64:128, :], AF.Identity,
                                                     scale=-1.0,
                                                     bias=SV1[64:128, hh2:hh2 + 1]), "ac")
                            kb.sc(f_sc)
                            tok_v1 = kb.tok("ac")

                            def f_sy(sy, hh2=hh2, qb=qb, opar=opar, t0=tok_v0, t1=tok_v1):
                                kb.wait(sy, t0)
                                kb.wait(sy, t1)
                                kb.dma(sy, "dq",
                                       out=ret_in[hh2 * 64:(hh2 + 1) * 64, qb * 512:qb * 512 + 512],
                                       in_=OS2[0:64, opar * 512:opar * 512 + 512])
                                kb.dma(sy, "dq",
                                       out=ret_in[hh2 * 64:(hh2 + 1) * 64, S + qb * 512:S + qb * 512 + 512],
                                       in_=OS2[64:128, opar * 512:opar * 512 + 512])
                            kb.sy(f_sy)
                            os_par[opar] = kb.tok("dq")
                            prev_os = os_par[opar]
                        ret_last = prev_os

                    def f_gp(g, tok=ret_last):
                        kb.wait(g, tok)
                        kb.tag(g.collective_compute(
                            "AllGather", ALU.bypass, ins=[ret_in[:, :]], outs=[ret_out[:, :]],
                            replica_groups=[list(range(NC))]), "cc")
                    kb.gp(f_gp)
                    tok_ret = kb.tok("cc")

            barrier()

            with contextlib.ExitStack() as w2x:
                WO2 = w2x.enter_context(nc.sbuf_tensor("WO2", [128, 8 * 1024], BF16))
                ATT2 = w2x.enter_context(nc.sbuf_tensor("ATT2", [128, 2 * 8 * 512], BF16))
                PWO2 = w2x.enter_context(nc.psum_tensor("pwo2", [128, 1024], F32))

                def f_gp(g):
                    kb.dma(g, "dg", out=WO2[:, :].rearrange("p (jc o) -> p jc o", o=1024),
                           in_=d_wo0[:, :].rearrange("(jc p) o -> p jc o", p=128))
                kb.gp(f_gp)
                tok_wo2 = kb.tok("dg")

                att_toks = [None, None]
                wpar = 0
                wo_ev = [None, None]
                for tb in range(4):
                    lpar = tb % 2

                    def f_sy(sy, tb=tb, lpar=lpar, tok=(tok_ret if tb == 0 else None),
                             cons=att_toks[lpar]):
                        kb.wait(sy, tok)
                        kb.wait(sy, cons)
                        pid = kb.pid(sy)
                        src = ret_out[:, :].rearrange("(jc p) t -> p jc t", p=128)
                        kb.dma(sy, "dq",
                               out=ATT2[:, lpar * 4096:(lpar + 1) * 4096].rearrange(
                                   "p (jc t) -> p jc t", t=512),
                               in_=src[:, :, bass.ts(pid, TL)][:, :, tb * 512:tb * 512 + 512])
                    kb.sy(f_sy)
                    tok_att = kb.tok("dq")

                    for oc in range(NCH):
                        def f_te(te, tb=tb, oc=oc, lpar=lpar, wpar=wpar, tok=tok_att,
                                 w8=(tok_wo2 if tb == 0 and oc == 0 else None),
                                 evp=wo_ev[wpar]):
                            kb.wait(te, w8)
                            kb.wait(te, tok)
                            kb.wait(te, evp)
                            for jc in range(8):
                                mm_ = te.matmul(
                                    PWO2[:, wpar * 512:wpar * 512 + 512],
                                    WO2[:, jc * 1024 + oc * 128: jc * 1024 + oc * 128 + 128],
                                    ATT2[:, lpar * 4096 + jc * 512: lpar * 4096 + jc * 512 + 512],
                                    start=(jc == 0), stop=(jc == 7))
                            kb.tag(mm_, "mm")
                        kb.te(f_te)
                        tok_wo = kb.tok("mm")

                        def f_ve(v, tb=tb, oc=oc, wpar=wpar, tok=tok_wo):
                            kb.wait(v, tok)
                            xsl = X[:, oc * TL + tb * 512: oc * TL + tb * 512 + 512]
                            kb.tag(v.tensor_add(xsl, xsl,
                                                PWO2[:, wpar * 512:wpar * 512 + 512]), "ev")
                        kb.ve(f_ve)
                        wo_ev[wpar] = kb.tok("ev")
                        wpar ^= 1
                    att_toks[lpar] = wo_ev[(wpar + 1) % 2]

            barrier()

        def emit_stage_l1ffn(stg):
            with contextlib.ExitStack() as fx:
                W1Q = fx.enter_context(nc.sbuf_tensor(f"W1Q{stg}", [128, 8 * 4096], F8))
                W2Q = fx.enter_context(nc.sbuf_tensor(f"W2Q{stg}", [128, 32 * 1024], F8))
                BQF = fx.enter_context(nc.sbuf_tensor(f"BQF{stg}", [128, 48], F32))
                BQ8 = fx.enter_context(nc.sbuf_tensor(f"BQ8{stg}", [128, 48], F8))
                X8 = fx.enter_context(nc.sbuf_tensor(f"X8{stg}", [128, NCH * TL], F8))

                stx0 = contextlib.ExitStack()
                WST = stx0.enter_context(nc.sbuf_tensor(f"WST{stg}", [128, 2 * 2048], F32))

                def f_sy(sy):
                    kb.dma(sy, "dq", out=BQF[:, 0:32], in_=d_b1c[:, :])
                    kb.dma(sy, "dq", out=BQF[:, 32:40], in_=d_b2c[:, :])
                kb.sy(f_sy)
                tok_bld = kb.tok("dq")

                def f_ve(v, tok=tok_bld):
                    kb.wait(v, tok)
                    v.tensor_copy(BQ8[:, 0:40], BQF[:, 0:40])
                    v.drain()
                    kb.tag(v.tensor_copy(BQF[:, 0:40], BQ8[:, 0:40]), "ev")
                kb.ve(f_ve)
                tok_bq = kb.tok("ev")

                wld = [None, None]
                wcast = [None, None]
                chunks = [("w1", dc, hf) for dc in range(8) for hf in range(2)]
                chunks += [("w2", g4, 0) for g4 in range(16)]
                for ci, (kind, i0, i1) in enumerate(chunks):
                    par = ci % 2

                    def f_gp(g, kind=kind, i0=i0, i1=i1, par=par, cons=wcast[par]):
                        kb.wait(g, cons)
                        if kind == "w1":
                            kb.dma(g, "dg", out=WST[:, par * 2048:(par + 1) * 2048],
                                   in_=d_w1T[i0 * 128:(i0 + 1) * 128,
                                             i1 * 2048:(i1 + 1) * 2048])
                        else:
                            kb.dma(g, "dg",
                                   out=WST[:, par * 2048:(par + 1) * 2048].rearrange(
                                       "p (c o) -> p c o", o=1024),
                                   in_=d_w2T[i0 * 256:(i0 + 1) * 256, :].rearrange(
                                       "(c p) o -> p c o", p=128))
                    kb.gp(f_gp)
                    wld[par] = kb.tok("dg")

                    def f_ve(v, kind=kind, i0=i0, i1=i1, par=par, tok=wld[par]):
                        kb.wait(v, tok)
                        if kind == "w1":
                            dst = W1Q[:, i0 * 4096 + i1 * 2048: i0 * 4096 + (i1 + 1) * 2048]
                        else:
                            dst = W2Q[:, i0 * 2048:(i0 + 1) * 2048]
                        kb.tag(v.tensor_copy(dst, WST[:, par * 2048:(par + 1) * 2048]), "ev")
                    kb.ve(f_ve)
                    wcast[par] = kb.tok("ev")
                tok_wq = kb.tok("ev")

                h_done = emit_ln(f"s{stg}", X8)
                stx0.close()

                with contextlib.ExitStack() as ex:
                    H8 = ex.enter_context(nc.sbuf_tensor(f"H8{stg}", [128, 32 * 512], F8))
                    H16 = ex.enter_context(nc.sbuf_tensor(f"H16{stg}", [128, 2 * 512], F16))
                    Y16 = ex.enter_context(nc.sbuf_tensor(f"Y16{stg}", [128, 2 * 512], F16))
                    WSC = ex.enter_context(nc.sbuf_tensor(f"WSCq{stg}", [128, 2 * 512], F32))
                    MSKT = ex.enter_context(nc.sbuf_tensor(f"MSKT{stg}", [128, 8 * 512], BF16))
                    PH = ex.enter_context(nc.psum_tensor(f"ph{stg}", [128, 1024], F32))
                    PY = ex.enter_context(nc.psum_tensor(f"py{stg}", [128, 1024], F32))

                    t_h16 = [None, None]
                    t_h8w = [None, None]
                    t_yact = [None, None]
                    t_ymul = [None, None]
                    t_yadd = [None, None]
                    t_msk = None
                    t_ymm_all = None
                    for tb in range(4):
                        def f_sy(sy, tb=tb, cons=(t_ymul[1] if tb else None),
                                 wq=(tok_wq if tb == 0 else None),
                                 hd=(h_done if tb == 0 else None)):
                            kb.wait(sy, cons)
                            kb.wait(sy, wq)
                            kb.wait(sy, hd)
                            kb.dma(sy, "dq",
                                   out=MSKT[:, :].rearrange("p (c t) -> p c t", t=512),
                                   in_=d_mskT[:, :].rearrange("(c p) t -> p c t", p=128)[
                                       :, :, tb * 512:(tb + 1) * 512])
                        kb.sy(f_sy)
                        t_msk = kb.tok("dq")

                        for oc2 in range(32):
                            par = oc2 % 2

                            def f_te(te, tb=tb, oc2=oc2, par=par,
                                     w8=(tok_wq if tb == 0 and oc2 == 0 else None),
                                     hd=(h_done if tb == 0 and oc2 == 0 else None),
                                     evp=t_h16[par]):
                                kb.wait(te, w8)
                                kb.wait(te, hd)
                                kb.wait(te, evp)
                                for dc in range(NCH):
                                    mm_ = te.matmul(
                                        PH[:, par * 512:par * 512 + 512],
                                        W1Q[:, dc * 4096 + oc2 * 128: dc * 4096 + oc2 * 128 + 128],
                                        X8[:, dc * TL + tb * 512: dc * TL + tb * 512 + 512],
                                        start=(dc == 0), stop=(dc == NCH - 1))
                                kb.tag(mm_, "mm")
                            kb.te(f_te)
                            t_hmm = kb.tok("mm")

                            def f_sc(sc, tb=tb, oc2=oc2, par=par, tok=t_hmm, cons=t_h8w[par],
                                     bq=(tok_bq if tb == 0 and oc2 == 0 else None)):
                                kb.wait(sc, bq)
                                kb.wait(sc, tok)
                                kb.wait(sc, cons)
                                kb.tag(sc.activation(H16[:, par * 512:(par + 1) * 512],
                                                     PH[:, par * 512:par * 512 + 512],
                                                     AF.Relu, scale=1.0,
                                                     bias=BQF[:, oc2:oc2 + 1]), "ac")
                            kb.sc(f_sc)
                            t_h16[par] = kb.tok("ac")

                            def f_ve(v, oc2=oc2, par=par, tok=t_h16[par],
                                     ycons=(t_ymm_all if oc2 == 0 else None)):
                                kb.wait(v, tok)
                                kb.wait(v, ycons)
                                kb.tag(v.tensor_copy(H8[:, oc2 * 512:(oc2 + 1) * 512],
                                                     H16[:, par * 512:(par + 1) * 512]), "ev")
                            kb.ve(f_ve)
                            t_h8w[par] = kb.tok("ev")
                        t_h8_last = t_h8w[31 % 2]

                        for oc in range(8):
                            ypar = oc % 2

                            def f_te(te, oc=oc, ypar=ypar,
                                     h8=(t_h8_last if oc == 0 else None),
                                     evp=t_yact[ypar]):
                                kb.wait(te, h8)
                                kb.wait(te, evp)
                                for hc in range(32):
                                    mm_ = te.matmul(
                                        PY[:, ypar * 512:ypar * 512 + 512],
                                        W2Q[:, hc * 1024 + oc * 128: hc * 1024 + oc * 128 + 128],
                                        H8[:, hc * 512:(hc + 1) * 512],
                                        start=(hc == 0), stop=(hc == 31))
                                kb.tag(mm_, "mm")
                            kb.te(f_te)
                            t_ymm = kb.tok("mm")

                            def f_sc(sc, oc=oc, ypar=ypar, tok=t_ymm, cons=t_ymul[ypar]):
                                kb.wait(sc, tok)
                                kb.wait(sc, cons)
                                kb.tag(sc.activation(Y16[:, ypar * 512:(ypar + 1) * 512],
                                                     PY[:, ypar * 512:ypar * 512 + 512],
                                                     AF.Identity, scale=1.0,
                                                     bias=BQF[:, 32 + oc:33 + oc]), "ac")
                            kb.sc(f_sc)
                            t_yact[ypar] = kb.tok("ac")

                            def f_ve(v, oc=oc, ypar=ypar, tok=t_yact[ypar],
                                     mk=t_msk, c1=t_yadd[ypar]):
                                kb.wait(v, tok)
                                kb.wait(v, mk)
                                kb.wait(v, c1)
                                kb.tag(v.tensor_mul(
                                    WSC[:, ypar * 512:(ypar + 1) * 512],
                                    Y16[:, ypar * 512:(ypar + 1) * 512],
                                    MSKT[:, oc * 512: oc * 512 + 512]), "ev")
                            kb.ve(f_ve)
                            t_ymul[ypar] = kb.tok("ev")

                            def f_gp(g, tb=tb, oc=oc, ypar=ypar, tok=t_ymul[ypar]):
                                kb.wait(g, tok)
                                xsl = X[:, oc * TL + tb * 512: oc * TL + tb * 512 + 512]
                                kb.tag(g.tensor_tensor(xsl, xsl,
                                                       WSC[:, ypar * 512:(ypar + 1) * 512],
                                                       ALU.add), "gv")
                            kb.gp(f_gp)
                            t_yadd[ypar] = kb.tok("gv")
                        t_ymm_all = t_ymm

            barrier()

        def emit_stage_l1attn(exi):
            ag_in, ag_out = d_ag_in[exi], d_ag_out[exi]
            ret_in, ret_out = d_ret_in[exi], d_ret_out[exi]

            with contextlib.ExitStack() as a5x:
                H5 = a5x.enter_context(nc.sbuf_tensor("H5", [128, NCH * TL], BF16))
                Z512 = a5x.enter_context(nc.sbuf_tensor("Z512", [128, 512], F32))
                W1LC = a5x.enter_context(nc.sbuf_tensor("W1LC", [128, 8], F32))

                def f_sy(sy):
                    kb.dma(sy, "dq", out=W1LC[:, :], in_=d_w1lc[:, :])
                kb.sy(f_sy)
                tok_w1l = kb.tok("dq")

                def f_ve(v, tok=tok_w1l):
                    kb.wait(v, tok)
                    kb.tag(v.memset(Z512[:, :], 0.0), "ev")
                kb.ve(f_ve)
                tok_z5 = kb.tok("ev")

                stx0 = contextlib.ExitStack()
                STAT5 = stx0.enter_context(nc.sbuf_tensor("STAT5", [1, 1536], F32))
                ONE6R = stx0.enter_context(nc.sbuf_tensor("ONE6R", [1, 512], F32))
                RAR = stx0.enter_context(nc.sbuf_tensor("RAR", [128, TL], F32))
                RBR = stx0.enter_context(nc.sbuf_tensor("RBR", [128, TL], F32))
                PREP5 = stx0.enter_context(nc.psum_tensor("prep5", [128, 1024], F32))

                def f_ve(v):
                    kb.tag(v.memset(ONE6R[:, :], 1.0 + 1e-6), "ev")
                kb.ve(f_ve)
                tok_one6 = kb.tok("ev")

                ex_state = {"prev": None}

                def extra_rows5(tb, tok_rows):
                    t0 = tb * 512

                    def f_ve(v, tok=tok_rows, t1=(tok_one6 if tb == 0 else None)):
                        kb.wait(v, tok)
                        kb.wait(v, t1)
                        rs = STAT[0:1, 512:1024]
                        v.tensor_mul(STAT5[0:1, 0:512], rs, rs)
                        v.drain()
                        v.tensor_mul(STAT5[0:1, 512:1024], STAT5[0:1, 0:512], EPSR[0:1, :])
                        v.drain()
                        kb.tag(v.tensor_sub(STAT5[0:1, 0:512], ONE6R[0:1, :],
                                            STAT5[0:1, 512:1024]), "ev")
                    kb.ve(f_ve)
                    t_a = kb.tok("ev")

                    def f_sc(sc, tok=t_a):
                        kb.wait(sc, tok)
                        kb.tag(sc.activation(STAT5[0:1, 512:1024], STAT5[0:1, 0:512],
                                             AF.Sqrt), "ac")
                    kb.sc(f_sc)
                    t_b = kb.tok("ac")

                    def f_ve(v, tok=t_b):
                        kb.wait(v, tok)
                        v.reciprocal(STAT5[0:1, 0:512], STAT5[0:1, 512:1024])
                        v.drain()
                        v.tensor_mul(STAT5[0:1, 512:1024], STAT5[0:1, 0:512],
                                     STAT[0:1, 512:1024])
                        ti = v.tensor_mul(STAT5[0:1, 1024:1536], STAT5[0:1, 0:512],
                                          STAT[0:1, 1536:2048])
                        v.drain()
                        kb.tag(ti, "ev")
                    kb.ve(f_ve)
                    t_c = kb.tok("ev")

                    def f_te(te, tok=t_c, prev=ex_state["prev"]):
                        kb.wait(te, tok)
                        kb.wait(te, prev)
                        te.matmul(PREP5[:, 0:512], ONESROWF[:, :], STAT5[0:1, 512:1024],
                                  start=True, stop=True)
                        kb.tag(te.matmul(PREP5[:, 512:1024], ONESROWF[:, :],
                                         STAT5[0:1, 1024:1536], start=True, stop=True),
                               "mm")
                    kb.te(f_te)
                    t_d = kb.tok("mm")

                    def f_ve(v, t0=t0, tok=t_d):
                        kb.wait(v, tok)
                        v.tensor_copy(RAR[:, t0:t0 + 512], PREP5[:, 0:512])
                        kb.tag(v.tensor_copy(RBR[:, t0:t0 + 512], PREP5[:, 512:1024]),
                               "ev")
                    kb.ve(f_ve)
                    ex_state["prev"] = kb.tok("ev")
                    return ex_state["prev"]

                h_done = emit_ln("s5", H5, extra_rows=extra_rows5,
                                 apply5=(RAR, RBR, W1LC, Z512))
                stx0.close()

                def f_sy(sy, tok=h_done):
                    kb.wait(sy, tok)
                    kb.wait(sy, tok_z5)
                    kb.dma(sy, "dq", out=ag_in[:, :].rearrange("(c p) t -> p c t", p=128),
                           in_=H5[:, :].rearrange("p (c t) -> p c t", t=TL))
                kb.sy(f_sy)
                tok_hstg = kb.tok("dq")
                if DBG:
                    def f_sy(sy, tok=h_done):
                        kb.wait(sy, tok)
                        kb.dma(sy, "dq", out=d_dbg_h5[:, :], in_=H5[:, :])
                    kb.sy(f_sy)

                def f_gp(g, tok=tok_hstg):
                    kb.wait(g, tok)
                    kb.tag(g.collective_compute(
                        "AllGather", ALU.bypass, ins=[ag_in[:, :]], outs=[ag_out[:, :]],
                        replica_groups=[list(range(NC))]), "cc")
                kb.gp(f_gp)
                tok_ag5 = kb.tok("cc")

                with contextlib.ExitStack() as p5x:
                    W5S = p5x.enter_context(nc.sbuf_tensor("W5S", [128, 8 * 576], F32))
                    W51 = p5x.enter_context(nc.sbuf_tensor("W51", [128, 8 * 576], BF16))
                    W52 = p5x.enter_context(nc.sbuf_tensor("W52", [128, 8 * 576], BF16))
                    MU5 = p5x.enter_context(nc.sbuf_tensor("MU5", [128, 40], F32))
                    OM5 = p5x.enter_context(nc.sbuf_tensor("OM5", [128, 40], F32))
                    W0C = p5x.enter_context(nc.sbuf_tensor("W0C", [128, 1], F32))
                    WB5 = p5x.enter_context(nc.sbuf_tensor("WB5", [64, 128], BF16))
                    C10T = p5x.enter_context(nc.sbuf_tensor("C10T", [128, 512], F32))
                    HL5 = p5x.enter_context(nc.sbuf_tensor("HL5", [128, 2 * 8 * 513], BF16))
                    TA5 = p5x.enter_context(nc.sbuf_tensor("TA5", [64, 2 * 512], BF16))
                    SPR5 = p5x.enter_context(nc.sbuf_tensor("SPR5", [128, 2 * 512], BF16))
                    SPG5 = p5x.enter_context(nc.sbuf_tensor("SPG5", [128, 2 * 512], BF16))
                    RTF5 = p5x.enter_context(nc.sbuf_tensor("RTF5", [128, 2 * 512], F32))
                    PPRJ = p5x.enter_context(nc.psum_tensor("pprj", [128, 1024], F32))
                    PLB = p5x.enter_context(nc.psum_tensor("plb", [128, 1024], F32))

                    def f_gp(g, hd=h_done):
                        kb.wait(g, hd)
                        kb.dma(g, "dg", out=W5S[:, :].rearrange("p (dc o) -> p dc o", o=576),
                               in_=d_w5T[:, :].rearrange("(dc p) o -> p dc o", p=128))
                        kb.dma(g, "dg", out=WB5[:, :], in_=d_wb5[:, :])
                    kb.gp(f_gp)
                    tok_w5 = kb.tok("dg")

                    def f_sy(sy, hd=h_done):
                        kb.wait(sy, hd)
                        kb.dma(sy, "dq", out=MU5[:, :], in_=d_mu5c[:, :])
                        kb.dma(sy, "dq", out=W0C[:, :], in_=d_w0c[:, :])
                    kb.sy(f_sy)
                    tok_mu = kb.tok("dq")

                    def f_ve(v, tw=tok_w5, tm=tok_mu, tz=tok_z5):
                        kb.wait(v, tw)
                        kb.wait(v, tm)
                        kb.wait(v, tz)
                        v.memset(OM5[:, :], 1.0)
                        v.memset(C10T[:, :], 9.0)
                        v.drain()
                        v.tensor_sub(OM5[:, :], OM5[:, :], MU5[:, :])
                        v.drain()
                        ti = None
                        for dc in range(NCH):
                            for m in range(5):
                                wdt = 128 if m < 4 else 64
                                off = dc * 576 + m * 128
                                mcol = dc * 5 + m
                                v.scalar_tensor_tensor(
                                    W51[:, off:off + wdt], W5S[:, off:off + wdt],
                                    OM5[:, mcol:mcol + 1], Z512[:, 0:wdt],
                                    ALU.mult, ALU.add)
                                ti = v.scalar_tensor_tensor(
                                    W52[:, off:off + wdt], W5S[:, off:off + wdt],
                                    MU5[:, mcol:mcol + 1], Z512[:, 0:wdt],
                                    ALU.mult, ALU.add)
                        kb.tag(ti, "ev")
                    kb.ve(f_ve)
                    tok_wsc = kb.tok("ev")

                    hl_toks = [None, None]
                    prj_ev = [None, None]
                    lb_ev = [None, None]
                    sp_toks = [None, None]
                    g_toks = [None, None]
                    rt_toks = [None, None]
                    ta_toks = [None, None]
                    ppar = 0
                    lpar_lb = 0
                    for qb in range(32):
                        lpar = qb % 2
                        gtok = qb * 512
                        r_ = gtok // TL
                        tcol = gtok % TL

                        def f_sy(sy, qb=qb, lpar=lpar, r_=r_, tcol=tcol,
                                 agw=(tok_ag5 if qb == 0 else None), cons=hl_toks[lpar]):
                            kb.wait(sy, agw)
                            kb.wait(sy, cons)
                            base = lpar * 4104
                            hv = HL5[:, base:base + 4104].rearrange("p (c t) -> p c t", t=513)
                            src = ag_out[:, :].rearrange("(r c p) t -> p r c t", p=128, c=NCH)
                            kb.dma(sy, "dq", out=hv[:, :, 1:513],
                                   in_=src[:, r_, :, tcol:tcol + 512])
                            gm1 = gtok - 1
                            if gtok not in (0, 8192):
                                kb.dma(sy, "dq", out=hv[:, :, 0:1],
                                       in_=src[:, gm1 // TL, :, (gm1 % TL):(gm1 % TL) + 1],
                                       allow_slow_non_contiguous=True)
                        kb.sy(f_sy)
                        tok_hl = kb.tok("dq")

                        if gtok in (0, 8192):
                            def f_ve(v, lpar=lpar, tok=tok_hl):
                                kb.wait(v, tok)
                                base = lpar * 4104
                                hv = HL5[:, base:base + 4104].rearrange(
                                    "p (c t) -> p c t", t=513)
                                kb.tag(v.memset(hv[:, :, 0:1], 0.0), "ev")
                            kb.ve(f_ve)
                            tok_hl = (tok_hl, kb.tok("ev"))
                        else:
                            tok_hl = (tok_hl, None)

                        for m in range(5):
                            wdt = 128 if m < 4 else 64

                            def f_te(te, m=m, lpar=lpar, ppar=ppar, tok=tok_hl,
                                     w8=(tok_wsc if qb == 0 and m == 0 else None),
                                     evp=prj_ev[ppar]):
                                kb.wait(te, w8)
                                kb.wait(te, tok[0])
                                kb.wait(te, tok[1])
                                kb.wait(te, evp)
                                base = lpar * 4104
                                hv = HL5[:, base:base + 4104].rearrange(
                                    "p (c t) -> p c t", t=513)
                                wdt = 128 if m < 4 else 64
                                out = PPRJ[0:wdt, ppar * 512:ppar * 512 + 512]
                                for dc in range(NCH):
                                    off = dc * 576 + m * 128
                                    te.matmul(out, W51[:, off:off + wdt],
                                              hv[:, dc, 1:513], start=(dc == 0), stop=False)
                                    mm_ = te.matmul(out, W52[:, off:off + wdt],
                                                    hv[:, dc, 0:512], start=False,
                                                    stop=(dc == NCH - 1))
                                kb.tag(mm_, "mm")
                            kb.te(f_te)
                            tok_pm = kb.tok("mm")

                            if m < 3:
                                spar = (qb * 3 + m) % 2
                                dsp = (d_r5, d_k5, d_v5)[m]

                                def f_ve(v, ppar=ppar, spar=spar, tok=tok_pm,
                                         cons=sp_toks[spar]):
                                    kb.wait(v, tok)
                                    kb.wait(v, cons)
                                    kb.tag(v.tensor_copy(
                                        SPR5[:, spar * 512:(spar + 1) * 512],
                                        PPRJ[:, ppar * 512:ppar * 512 + 512]), "ev")
                                kb.ve(f_ve)
                                tok_sp = kb.tok("ev")
                                prj_ev[ppar] = tok_sp

                                def f_gp(g, gtok=gtok, spar=spar, dsp=dsp, tok=tok_sp):
                                    kb.wait(g, tok)
                                    kb.dma(g, "dg", out=dsp[:, gtok:gtok + 512],
                                           in_=SPR5[:, spar * 512:(spar + 1) * 512])
                                kb.gp(f_gp)
                                sp_toks[spar] = kb.tok("dg")
                            elif m == 3:
                                gpar = qb % 2

                                def f_sc(sc, ppar=ppar, gpar=gpar, tok=tok_pm,
                                         cons=g_toks[gpar]):
                                    kb.wait(sc, tok)
                                    kb.wait(sc, cons)
                                    kb.tag(sc.activation(
                                        SPG5[:, gpar * 512:(gpar + 1) * 512],
                                        PPRJ[:, ppar * 512:ppar * 512 + 512],
                                        AF.Silu), "ac")
                                kb.sc(f_sc)
                                tok_g = kb.tok("ac")
                                prj_ev[ppar] = tok_g

                                def f_gp(g, gtok=gtok, gpar=gpar, tok=tok_g):
                                    kb.wait(g, tok)
                                    kb.dma(g, "dg", out=d_g5[:, gtok:gtok + 512],
                                           in_=SPG5[:, gpar * 512:(gpar + 1) * 512])
                                kb.gp(f_gp)
                                g_toks[gpar] = kb.tok("dg")
                            else:
                                tpar = qb % 2

                                def f_sc(sc, ppar=ppar, tpar=tpar, tok=tok_pm,
                                         cons=ta_toks[tpar]):
                                    kb.wait(sc, tok)
                                    kb.wait(sc, cons)
                                    kb.tag(sc.activation(
                                        TA5[0:64, tpar * 512:(tpar + 1) * 512],
                                        PPRJ[0:64, ppar * 512:ppar * 512 + 512],
                                        AF.Tanh), "ac")
                                kb.sc(f_sc)
                                tok_ta = kb.tok("ac")
                                prj_ev[ppar] = tok_ta

                                def f_te(te, tpar=tpar, tok=tok_ta,
                                         wb=(tok_w5 if qb == 0 else None),
                                         evp=lb_ev[tpar]):
                                    kb.wait(te, wb)
                                    kb.wait(te, tok)
                                    kb.wait(te, evp)
                                    kb.tag(te.matmul(
                                        PLB[:, tpar * 512:tpar * 512 + 512], WB5[:, :],
                                        TA5[0:64, tpar * 512:(tpar + 1) * 512],
                                        start=True, stop=True), "mm")
                                kb.te(f_te)
                                tok_lb = kb.tok("mm")
                                ta_toks[tpar] = tok_lb

                                def f_sc(sc, tpar=tpar, tok=tok_lb, cons=rt_toks[tpar]):
                                    kb.wait(sc, tok)
                                    kb.wait(sc, cons)
                                    kb.tag(sc.activation(
                                        RTF5[:, tpar * 512:(tpar + 1) * 512],
                                        PLB[:, tpar * 512:tpar * 512 + 512],
                                        AF.Exp, scale=1.0, bias=W0C[:, 0:1]), "ac")
                                kb.sc(f_sc)
                                tok_rt = kb.tok("ac")
                                lb_ev[tpar] = tok_rt

                                def f_ve(v, tpar=tpar, tok=tok_rt):
                                    kb.wait(v, tok)
                                    sl = RTF5[:, tpar * 512:(tpar + 1) * 512]
                                    kb.tag(v.tensor_tensor(sl, sl, C10T[:, :],
                                                           ALU.min), "ev")
                                kb.ve(f_ve)
                                tok_cl = kb.tok("ev")

                                def f_gp(g, gtok=gtok, tpar=tpar, tok=tok_cl):
                                    kb.wait(g, tok)
                                    kb.dma(g, "dg", out=d_rate5[:, gtok:gtok + 512],
                                           in_=RTF5[:, tpar * 512:(tpar + 1) * 512])
                                kb.gp(f_gp)
                                rt_toks[tpar] = kb.tok("dg")
                            ppar ^= 1
                        hl_toks[lpar] = prj_ev[(ppar + 1) % 2]
                    proj5_done_dg = kb.tok("dg")
                    proj5_done_dq = kb.tok("dq")

                SWW = 512
                with contextlib.ExitStack() as s5x:
                    RATE5 = s5x.enter_context(nc.sbuf_tensor("RATE5", [128, 2 * SWW], F32))
                    REV5 = s5x.enter_context(nc.sbuf_tensor("REV5", [128, SWW], F32))
                    FS = s5x.enter_context(nc.sbuf_tensor("FS", [128, 5 * SWW], F32))
                    RSC = s5x.enter_context(nc.sbuf_tensor("RSC", [128, SWW], F32))
                    RSF = s5x.enter_context(nc.sbuf_tensor("RSF", [128, 5 * SWW], F32))
                    KD8X = s5x.enter_context(nc.sbuf_tensor("KD8X", [128, SWW], F32))
                    QP = s5x.enter_context(nc.sbuf_tensor("QP", [128, 5 * SWW], BF16))
                    KP8 = s5x.enter_context(nc.sbuf_tensor("KP8", [128, SWW], BF16))
                    KGP = s5x.enter_context(nc.sbuf_tensor("KGP", [128, 4 * SWW], BF16))
                    KST = s5x.enter_context(nc.sbuf_tensor("KST", [128, SWW], BF16))
                    UKB = s5x.enter_context(nc.sbuf_tensor("UKB", [128, SWW], BF16))
                    RSW5 = s5x.enter_context(nc.sbuf_tensor("RSW5", [128, SWW], BF16))
                    KSW5 = s5x.enter_context(nc.sbuf_tensor("KSW5", [128, SWW], BF16))
                    GSW5 = s5x.enter_context(nc.sbuf_tensor("GSW5", [128, SWW], BF16))
                    VTM5 = s5x.enter_context(nc.sbuf_tensor("VTM5", [128, 4 * 128], BF16))
                    KTM5 = s5x.enter_context(nc.sbuf_tensor("KTM5", [128, 2 * 128], BF16))
                    S5 = s5x.enter_context(nc.sbuf_tensor("S5", [128, 128], F32))
                    S5B = s5x.enter_context(nc.sbuf_tensor("S5B", [128, 128], BF16))
                    AT5F = s5x.enter_context(nc.sbuf_tensor("AT5F", [128, 128], F32))
                    GSCR = s5x.enter_context(nc.sbuf_tensor("GSCR", [128, 128], F32))
                    SCR5 = s5x.enter_context(nc.sbuf_tensor("SCR5", [128, 128], F32))
                    AT5B = s5x.enter_context(nc.sbuf_tensor("AT5B", [128, 2 * 128], BF16))
                    MSK5 = s5x.enter_context(nc.sbuf_tensor("MSK5", [128, 7 * 128], F32))
                    SEG5 = s5x.enter_context(nc.sbuf_tensor("SEG5", [128, 5 * SWW], F32))
                    UC5 = s5x.enter_context(nc.sbuf_tensor("UC5", [128, 4], F32))
                    DDsc = s5x.enter_context(nc.sbuf_tensor("DDsc", [128, 4], F32))
                    OS5 = s5x.enter_context(nc.sbuf_tensor("OS5", [128, SWW], BF16))
                    SQ5 = s5x.enter_context(nc.sbuf_tensor("SQ5", [128, SWW], BF16))
                    OT5 = s5x.enter_context(nc.sbuf_tensor("OT5", [128, SWW], BF16))
                    MREPS = s5x.enter_context(nc.sbuf_tensor("MREPS", [128, SWW], F32))
                    RREPS = s5x.enter_context(nc.sbuf_tensor("RREPS", [128, SWW], F32))
                    TGN = s5x.enter_context(nc.sbuf_tensor("TGN", [128, SWW], F32))
                    GR = s5x.enter_context(nc.sbuf_tensor("GR", [1, 4 * SWW], F32))
                    PSC = s5x.enter_context(nc.psum_tensor("psc5", [128, 1024], F32))
                    PSO5 = s5x.enter_context(nc.psum_tensor("pso5", [128, 256], F32))
                    PSADD = s5x.enter_context(nc.psum_tensor("psadd5", [128, 128], F32))
                    PGNS = s5x.enter_context(nc.psum_tensor("pgns5", [1, SWW], F32))
                    PGNR = s5x.enter_context(nc.psum_tensor("pgnr5", [128, SWW], F32))

                    def f_sy(sy, pd=proj5_done_dg, pq=proj5_done_dq):
                        kb.wait(sy, pd)
                        kb.wait(sy, pq)
                        kb.dma(sy, "dq", out=MSK5[:, :], in_=d_msk5[:, :])
                        kb.dma(sy, "dq", out=SEG5[:, :].rearrange("p (g t) -> p g t", t=SWW),
                               in_=d_seg5[:, :].rearrange("p (g t) -> p g t", t=1024)[:, :, 0:SWW])
                        kb.dma(sy, "dq", out=UC5[:, 0:1], in_=d_uc[:, :])
                        kb.dma(sy, "dq", out=UC5[:, 1:2], in_=d_gnwc[:, :])
                        kb.dma(sy, "dq", out=UC5[:, 2:3], in_=d_gnbc[:, :])
                    kb.sy(f_sy)
                    tok_cst5 = kb.tok("dq")

                    ld_cons = None
                    fac_cons = None
                    st_tok = None
                    os_cons = None
                    ret_toks = None
                    vtm_cons = None
                    ktm_cons = [None, None]
                    psc_free = None
                    pso_free = [None, None]
                    at5_free = [None, None]
                    sadd_free = None
                    gn_done = None
                    for b in range(2):
                        def f_ve(v, st=st_tok,
                                 pd=(proj5_done_dg if b == 0 else None)):
                            kb.wait(v, st)
                            kb.wait(v, pd)
                            kb.tag(v.memset(S5[:, :], 0.0), "ev")
                        kb.ve(f_ve)
                        st_tok = kb.tok("ev")

                        for sw in range(16):
                            W0 = b * S + sw * SWW
                            rpar = sw % 2

                            def f_sy(sy, W0=W0, rpar=rpar, cons=ld_cons,
                                     pd=(proj5_done_dg if b == 0 and sw == 0 else None),
                                     pq=(proj5_done_dq if b == 0 and sw == 0 else None)):
                                kb.wait(sy, pd)
                                kb.wait(sy, pq)
                                kb.wait(sy, cons)
                                kb.dma(sy, "dq", out=RATE5[:, rpar * SWW:(rpar + 1) * SWW],
                                       in_=d_rate5[:, W0:W0 + SWW])
                                kb.dma(sy, "dq", out=RSW5[:, :], in_=d_r5[:, W0:W0 + SWW])
                                kb.dma(sy, "dq", out=KSW5[:, :], in_=d_k5[:, W0:W0 + SWW])
                                kb.dma(sy, "dq", out=GSW5[:, :], in_=d_g5[:, W0:W0 + SWW])
                                for w in range(4):
                                    kb.dma(sy, "dq", out=VTM5[:, w * 128:(w + 1) * 128],
                                           in_=d_v5[:, W0 + w * 128:W0 + (w + 1) * 128],
                                           transpose=True)
                            kb.sy(f_sy)
                            tok_ld = kb.tok("dq")

                            def f_ve(v, rpar=rpar, tok=tok_ld, fc=fac_cons,
                                     tc=(tok_cst5 if b == 0 and sw == 0 else None)):
                                kb.wait(v, tok)
                                kb.wait(v, fc)
                                kb.wait(v, tc)
                                rt = RATE5[:, rpar * SWW:(rpar + 1) * SWW]
                                v.tensor_copy(REV5[:, :], rt[:, ::-1])
                                v.drain()
                                for g in range(5):
                                    v.tensor_tensor_scan(
                                        FS[:, g * SWW:(g + 1) * SWW],
                                        SEG5[:, g * SWW:(g + 1) * SWW], rt,
                                        0.0, ALU.mult, ALU.add)
                                ti = None
                                for g in range(5):
                                    v.tensor_tensor_scan(
                                        RSC[:, :], SEG5[:, g * SWW:(g + 1) * SWW],
                                        REV5[:, :], 0.0, ALU.mult, ALU.add)
                                    v.drain()
                                    ti = v.tensor_copy(RSF[:, g * SWW:(g + 1) * SWW],
                                                       RSC[:, ::-1])
                                    v.drain()
                                kb.tag(ti, "ev")
                            kb.ve(f_ve)
                            tok_scan = kb.tok("ev")

                            def f_sc(sc, tok=tok_scan):
                                kb.wait(sc, tok)
                                kb.tag(sc.activation(KD8X[:, :], FS[:, 0:SWW], AF.Exp), "ac")
                            kb.sc(f_sc)
                            tok_kd8x = kb.tok("ac")

                            def f_ve(v, rpar=rpar, tok=tok_kd8x):
                                kb.wait(v, tok)
                                rt = RATE5[:, rpar * SWW:(rpar + 1) * SWW]
                                v.tensor_mul(KP8[:, :], KSW5[:, :], KD8X[:, :])
                                ti = None
                                for g in range(5):
                                    v.tensor_sub(FS[:, g * SWW:(g + 1) * SWW],
                                                 FS[:, g * SWW:(g + 1) * SWW], rt)
                                    ti = v.tensor_sub(RSF[:, g * SWW:(g + 1) * SWW],
                                                      RSF[:, g * SWW:(g + 1) * SWW], rt)
                                kb.tag(ti, "ev")
                            kb.ve(f_ve)
                            tok_subs = kb.tok("ev")

                            def f_sc(sc, tok=tok_subs):
                                kb.wait(sc, tok)
                                for g in range(4):
                                    sc.activation(FS[:, g * SWW:(g + 1) * SWW],
                                                  FS[:, g * SWW:(g + 1) * SWW],
                                                  AF.Exp, scale=-1.0)
                                sc.activation(KD8X[:, :], FS[:, 4 * SWW:5 * SWW],
                                              AF.Exp, scale=-1.0)
                                ti = None
                                for g in range(5):
                                    ti = sc.activation(RSF[:, g * SWW:(g + 1) * SWW],
                                                       RSF[:, g * SWW:(g + 1) * SWW],
                                                       AF.Exp, scale=-1.0)
                                kb.tag(ti, "ac")
                            kb.sc(f_sc)
                            tok_exps = kb.tok("ac")

                            def f_ve(v, tok=tok_exps):
                                kb.wait(v, tok)
                                for g in range(4):
                                    v.tensor_mul(QP[:, g * SWW:(g + 1) * SWW], RSW5[:, :],
                                                 FS[:, g * SWW:(g + 1) * SWW])
                                v.tensor_mul(QP[:, 4 * SWW:5 * SWW], RSW5[:, :],
                                             KD8X[:, :])
                                for g in range(4):
                                    v.tensor_mul(KGP[:, g * SWW:(g + 1) * SWW], KSW5[:, :],
                                                 RSF[:, g * SWW:(g + 1) * SWW])
                                v.tensor_mul(KST[:, :], KSW5[:, :], RSF[:, 4 * SWW:5 * SWW])
                                v.drain()
                                ti = v.scalar_tensor_tensor(UKB[:, :], KSW5[:, :],
                                                            UC5[:, 0:1], Z512[:, :],
                                                            ALU.mult, ALU.add)
                                kb.tag(ti, "ev")
                            kb.ve(f_ve)
                            tok_fac = kb.tok("ev")

                            for w in range(4):
                                wpar = w % 2

                                def f_sy(sy, w=w, wpar=wpar, tok=tok_fac,
                                         cons=ktm_cons[wpar]):
                                    kb.wait(sy, tok)
                                    kb.wait(sy, cons)
                                    kb.dma(sy, "dq", out=KTM5[:, wpar * 128:(wpar + 1) * 128],
                                           in_=KST[:, w * 128:(w + 1) * 128], transpose=True)
                                kb.sy(f_sy)
                                tok_ktm = kb.tok("dq")

                                wsl = slice(w * 128, (w + 1) * 128)
                                opar = w % 2
                                at_toks = []
                                for h in range(2):
                                    hb = h * 64

                                    def f_te(te, h=h, hb=hb, wsl=wsl, tok=tok_fac,
                                             pf=psc_free, vt=(tok_ld if h == 0 else None)):
                                        kb.wait(te, tok)
                                        kb.wait(te, pf)
                                        kb.wait(te, vt)
                                        lhs = [(KP8, 0), (KGP, 0), (KGP, 1), (KGP, 2),
                                               (KGP, 3), (UKB, 0)]
                                        rhs = [0, 0, 1, 2, 3, None]
                                        mi = None
                                        for l in range(6):
                                            kt, ksl = lhs[l]
                                            ksl_ap = kt[hb:hb + 64,
                                                        ksl * SWW + w * 128: ksl * SWW + (w + 1) * 128]
                                            if rhs[l] is None:
                                                q_ap = RSW5[hb:hb + 64, wsl]
                                            else:
                                                q_ap = QP[hb:hb + 64,
                                                          rhs[l] * SWW + w * 128: rhs[l] * SWW + (w + 1) * 128]
                                            mi = te.matmul(PSC[:, l * 128:(l + 1) * 128],
                                                           ksl_ap, q_ap,
                                                           start=True, stop=True)
                                        kb.tag(mi, "mm")
                                    kb.te(f_te)
                                    tok_sc5 = kb.tok("mm")

                                    def f_ve(v, h=h, tok=tok_sc5,
                                             af=(at_toks[0] if h == 1 else at5_free[0]),
                                             avc=at5_free[1]):
                                        kb.wait(v, tok)
                                        kb.wait(v, af)
                                        kb.wait(v, avc)
                                        v.tensor_mul(AT5F[:, :], PSC[:, 0:128],
                                                     MSK5[:, 0:128])
                                        for l in range(1, 6):
                                            v.tensor_mul(GSCR[:, :],
                                                         PSC[:, l * 128:(l + 1) * 128],
                                                         MSK5[:, l * 128:(l + 1) * 128])
                                            v.drain()
                                            v.tensor_add(AT5F[:, :], AT5F[:, :],
                                                         GSCR[:, :])
                                        v.drain()
                                        kb.tag(v.tensor_copy(AT5B[:, h * 128:(h + 1) * 128],
                                                             AT5F[:, :]), "ev")
                                    kb.ve(f_ve)
                                    at_toks.append(kb.tok("ev"))
                                    at5_free[0] = at_toks[-1]
                                    psc_free = at_toks[-1]

                                def f_ve(v, st=st_tok):
                                    kb.wait(v, st)
                                    kb.tag(v.tensor_copy(S5B[:, :], S5[:, :]), "ev")
                                kb.ve(f_ve)
                                tok_s5b = kb.tok("ev")

                                def f_te(te, w=w, opar=opar, tok=tok_s5b, a0=at_toks[0],
                                         a1=at_toks[1], pf=pso_free[opar], tf=tok_fac):
                                    kb.wait(te, tok)
                                    kb.wait(te, a0)
                                    kb.wait(te, a1)
                                    kb.wait(te, pf)
                                    kb.wait(te, tf)
                                    osl = PSO5[:, opar * 128:(opar + 1) * 128]
                                    te.matmul(osl, S5B[:, :],
                                              QP[:, 4 * SWW + w * 128: 4 * SWW + (w + 1) * 128],
                                              start=True, stop=False)
                                    te.matmul(osl[0:64, :], VTM5[:, w * 128:w * 128 + 64],
                                              AT5B[:, 0:128], start=False, stop=True)
                                    kb.tag(te.matmul(osl[64:128, :],
                                                     VTM5[:, w * 128 + 64:(w + 1) * 128],
                                                     AT5B[:, 128:256], start=False,
                                                     stop=True), "mm")
                                kb.te(f_te)
                                tok_av = kb.tok("mm")
                                at5_free[1] = tok_av

                                def f_ve(v, w=w, opar=opar, tok=tok_av, oc=os_cons):
                                    kb.wait(v, tok)
                                    kb.wait(v, oc)
                                    kb.tag(v.tensor_copy(OS5[:, w * 128:(w + 1) * 128],
                                                         PSO5[:, opar * 128:(opar + 1) * 128]),
                                           "ev")
                                kb.ve(f_ve)
                                pso_free[opar] = kb.tok("ev")

                                def f_te(te, w=w, wpar=wpar, tok=tok_ktm, sf=sadd_free,
                                         vt=tok_ld):
                                    kb.wait(te, tok)
                                    kb.wait(te, sf)
                                    kb.wait(te, vt)
                                    kb.tag(te.matmul(PSADD[:, :],
                                                     KTM5[:, wpar * 128:(wpar + 1) * 128],
                                                     VTM5[:, w * 128:(w + 1) * 128],
                                                     start=True, stop=True), "mm")
                                kb.te(f_te)
                                tok_sadd = kb.tok("mm")

                                def f_sc(sc, w=w, rpar=rpar, tok=tok_scan):
                                    kb.wait(sc, tok)
                                    rt = RATE5[:, rpar * SWW:(rpar + 1) * SWW]
                                    sc.activation(DDsc[:, 1:2], rt[:, w * 128 + 127:w * 128 + 128],
                                                  AF.Copy, scale=-1.0)
                                    kb.tag(sc.activation(
                                        DDsc[:, 0:1],
                                        FS[:, 4 * SWW + w * 128 + 127: 4 * SWW + w * 128 + 128],
                                        AF.Exp, scale=-1.0, bias=DDsc[:, 1:2]), "ac")
                                kb.sc(f_sc)
                                tok_dd = kb.tok("ac")

                                def f_ve(v, tok=tok_sadd, td=tok_dd, ts=tok_s5b,
                                         st=st_tok):
                                    kb.wait(v, tok)
                                    kb.wait(v, td)
                                    kb.wait(v, ts)
                                    kb.wait(v, st)
                                    v.tensor_mul(SCR5[:, :], PSADD[:, :],
                                                 MSK5[:, 6 * 128:7 * 128])
                                    v.drain()
                                    kb.tag(v.scalar_tensor_tensor(
                                        S5[:, :], S5[:, :], DDsc[:, 0:1], SCR5[:, :],
                                        ALU.mult, ALU.add), "ev")
                                kb.ve(f_ve)
                                st_tok = kb.tok("ev")
                                sadd_free = st_tok
                                ktm_cons[wpar] = tok_sadd

                            fac_cons = kb.tok("mm")

                            def f_ve(v, pf=pso_free):
                                kb.wait(v, pf[0])
                                kb.wait(v, pf[1])
                                kb.tag(v.tensor_mul(SQ5[:, :], OS5[:, :], OS5[:, :]), "ev")
                            kb.ve(f_ve)
                            tok_sq5 = kb.tok("ev")

                            gn_rows = []
                            for qi in range(4):
                                srcb = (OS5, OS5, SQ5, SQ5)[qi]
                                hb = (0, 64, 0, 64)[qi]

                                def f_te(te, qi=qi, srcb=srcb, hb=hb, tok=tok_sq5,
                                         gp_=gn_done):
                                    kb.wait(te, tok)
                                    kb.wait(te, gp_)
                                    kb.tag(te.matmul(PGNS[0:1, :], ONESB[hb:hb + 64, 0:1],
                                                     srcb[hb:hb + 64, :],
                                                     start=True, stop=True), "mm")
                                kb.te(f_te)
                                tok_gmm = kb.tok("mm")

                                def f_sc(sc, qi=qi, tok=tok_gmm):
                                    kb.wait(sc, tok)
                                    kb.tag(sc.activation(GR[0:1, qi * SWW:(qi + 1) * SWW],
                                                         PGNS[0:1, :], AF.Copy,
                                                         scale=1.0 / 64.0), "ac")
                                kb.sc(f_sc)
                                gn_rows.append(kb.tok("ac"))
                                gn_done = gn_rows[-1]

                            def f_ve(v, tok=gn_rows[-1]):
                                kb.wait(v, tok)
                                for h in range(2):
                                    mrow = GR[0:1, h * SWW:(h + 1) * SWW]
                                    erow = GR[0:1, (2 + h) * SWW:(3 + h) * SWW]
                                    v.tensor_mul(TGN[0:1, :], mrow, mrow)
                                    v.drain()
                                    v.tensor_sub(erow, erow, TGN[0:1, :])
                                    v.drain()
                                    ti = v.tensor_add(erow, erow, EPSR[0:1, :])
                                    v.drain()
                                kb.tag(ti, "ev")
                            kb.ve(f_ve)
                            tok_var5 = kb.tok("ev")

                            def f_sc(sc, tok=tok_var5):
                                kb.wait(sc, tok)
                                sc.activation(GR[0:1, 2 * SWW:3 * SWW], GR[0:1, 2 * SWW:3 * SWW],
                                              AF.Sqrt)
                                kb.tag(sc.activation(GR[0:1, 3 * SWW:4 * SWW],
                                                     GR[0:1, 3 * SWW:4 * SWW], AF.Sqrt), "ac")
                            kb.sc(f_sc)
                            tok_sqr5 = kb.tok("ac")

                            def f_ve(v, tok=tok_sqr5):
                                kb.wait(v, tok)
                                v.reciprocal(GR[0:1, 2 * SWW:3 * SWW], GR[0:1, 2 * SWW:3 * SWW])
                                ti = v.reciprocal(GR[0:1, 3 * SWW:4 * SWW],
                                                  GR[0:1, 3 * SWW:4 * SWW])
                                v.drain()
                                kb.tag(ti, "ev")
                            kb.ve(f_ve)
                            tok_rst5 = kb.tok("ev")

                            def f_te(te, tok=tok_rst5):
                                kb.wait(te, tok)
                                te.matmul(PGNR[0:64, :], ONESROWF[0:1, 0:64],
                                          GR[0:1, 0:SWW], start=True, stop=True)
                                kb.tag(te.matmul(PGNR[64:128, :], ONESROWF[0:1, 0:64],
                                                 GR[0:1, SWW:2 * SWW], start=True,
                                                 stop=True), "mm")
                            kb.te(f_te)
                            tok_mrep = kb.tok("mm")

                            def f_ve(v, tok=tok_mrep):
                                kb.wait(v, tok)
                                kb.tag(v.tensor_copy(MREPS[:, :], PGNR[:, :]), "ev")
                            kb.ve(f_ve)
                            tok_mcp = kb.tok("ev")

                            def f_te(te, tok=tok_mcp):
                                kb.wait(te, tok)
                                te.matmul(PGNR[0:64, :], ONESROWF[0:1, 0:64],
                                          GR[0:1, 2 * SWW:3 * SWW], start=True, stop=True)
                                kb.tag(te.matmul(PGNR[64:128, :], ONESROWF[0:1, 0:64],
                                                 GR[0:1, 3 * SWW:4 * SWW], start=True,
                                                 stop=True), "mm")
                            kb.te(f_te)
                            tok_rrep = kb.tok("mm")

                            def f_ve(v, W0=W0, tok=tok_rrep):
                                kb.wait(v, tok)
                                v.tensor_copy(RREPS[:, :], PGNR[:, :])
                                v.drain()
                                v.tensor_sub(TGN[:, :], OS5[:, :], MREPS[:, :])
                                v.drain()
                                v.tensor_mul(TGN[:, :], TGN[:, :], RREPS[:, :])
                                v.drain()
                                ti = v.scalar_tensor_tensor(TGN[:, :], TGN[:, :],
                                                            UC5[:, 1:2], Z512[:, :],
                                                            ALU.mult, ALU.add)
                                kb.tag(ti, "ev")
                            kb.ve(f_ve)
                            tok_gn1 = kb.tok("ev")

                            def f_sc(sc, tok=tok_gn1):
                                kb.wait(sc, tok)
                                kb.tag(sc.activation(TGN[:, :], TGN[:, :], AF.Identity,
                                                     scale=1.0, bias=UC5[:, 2:3]), "ac")
                            kb.sc(f_sc)
                            tok_gn2 = kb.tok("ac")

                            def f_ve(v, tok=tok_gn2, rc=ret_toks):
                                kb.wait(v, tok)
                                kb.wait(v, rc)
                                kb.tag(v.tensor_mul(OT5[:, :], TGN[:, :], GSW5[:, :]), "ev")
                            kb.ve(f_ve)
                            tok_ot5 = kb.tok("ev")
                            os_cons = tok_ot5
                            gn_done = tok_ot5

                            def f_sy(sy, W0=W0, tok=tok_ot5, tsq=tok_sq5):
                                kb.wait(sy, tok)
                                kb.dma(sy, "dq", out=ret_in[:, W0:W0 + SWW], in_=OT5[:, :])
                                if DBG:
                                    kb.wait(sy, tsq)
                                    kb.dma(sy, "dq", out=d_dbg_os5[:, W0:W0 + SWW],
                                           in_=OS5[:, :])
                                    kb.dma(sy, "dq", out=d_dbg_ot5[:, W0:W0 + SWW],
                                           in_=OT5[:, :])
                            kb.sy(f_sy)
                            ret_toks = kb.tok("dq")
                            ld_cons = ret_toks

                    def f_gp(g, tok=ret_toks):
                        kb.wait(g, tok)
                        kb.tag(g.collective_compute(
                            "AllGather", ALU.bypass, ins=[ret_in[:, :]], outs=[ret_out[:, :]],
                            replica_groups=[list(range(NC))]), "cc")
                    kb.gp(f_gp)
                    tok_ret5 = kb.tok("cc")

            barrier()

            with contextlib.ExitStack() as w5x:
                XI = w5x.enter_context(nc.sbuf_tensor("XI5", [128, NCH * TL], BF16))
                w5y = contextlib.ExitStack()
                WO5 = w5y.enter_context(nc.sbuf_tensor("WO5", [128, 8 * 1024], BF16))
                ATT5 = w5y.enter_context(nc.sbuf_tensor("ATT5", [128, 2 * 8 * 512], BF16))
                TZ5 = w5y.enter_context(nc.sbuf_tensor("TZ5", [128, 2 * 512], F32))
                PWO5 = w5y.enter_context(nc.psum_tensor("pwo5", [128, 1024], F32))

                def f_gp(g):
                    kb.dma(g, "dg", out=WO5[:, :].rearrange("p (jc o) -> p jc o", o=1024),
                           in_=d_wo5[:, :].rearrange("(jc p) o -> p jc o", p=128))
                kb.gp(f_gp)
                tok_wo5 = kb.tok("dg")

                att_toks5 = [None, None]
                wpar5 = 0
                wo_ev5 = [None, None]
                for tb in range(4):
                    lpar = tb % 2

                    def f_sy(sy, tb=tb, lpar=lpar, tok=(tok_ret5 if tb == 0 else None),
                             cons=att_toks5[lpar]):
                        kb.wait(sy, tok)
                        kb.wait(sy, cons)
                        pid = kb.pid(sy)
                        src_ = ret_out[:, :].rearrange("(jc p) t -> p jc t", p=128)
                        kb.dma(sy, "dq",
                               out=ATT5[:, lpar * 4096:(lpar + 1) * 4096].rearrange(
                                   "p (jc t) -> p jc t", t=512),
                               in_=src_[:, :, bass.ts(pid, TL)][:, :, tb * 512:tb * 512 + 512])
                    kb.sy(f_sy)
                    tok_att5 = kb.tok("dq")

                    for oc in range(NCH):
                        def f_te(te, tb=tb, oc=oc, lpar=lpar, wpar=wpar5, tok=tok_att5,
                                 w8=(tok_wo5 if tb == 0 and oc == 0 else None),
                                 evp=wo_ev5[wpar5]):
                            kb.wait(te, w8)
                            kb.wait(te, tok)
                            kb.wait(te, evp)
                            for jc in range(8):
                                mm_ = te.matmul(
                                    PWO5[:, wpar * 512:wpar * 512 + 512],
                                    WO5[:, jc * 1024 + oc * 128: jc * 1024 + oc * 128 + 128],
                                    ATT5[:, lpar * 4096 + jc * 512: lpar * 4096 + jc * 512 + 512],
                                    start=(jc == 0), stop=(jc == 7))
                            kb.tag(mm_, "mm")
                        kb.te(f_te)
                        tok_wmm5 = kb.tok("mm")

                        def f_ve(v, tb=tb, oc=oc, wpar=wpar5, tok=tok_wmm5):
                            kb.wait(v, tok)
                            t0 = tb * 512
                            xs = X[:, oc * TL + t0: oc * TL + t0 + 512]
                            xisl = XI[:, oc * TL + t0: oc * TL + t0 + 512]
                            ps = PWO5[:, wpar * 512:wpar * 512 + 512]
                            tz = TZ5[:, wpar * 512:(wpar + 1) * 512]
                            v.tensor_mul(tz, xs, RSTDR[:, t0:t0 + 512])
                            v.drain()
                            v.tensor_sub(tz, tz, MRSR[:, t0:t0 + 512])
                            v.drain()
                            v.tensor_add(xisl, tz, ps)
                            v.tensor_add(xs, xs, tz)
                            v.drain()
                            kb.tag(v.tensor_add(xs, xs, ps), "ev")
                        kb.ve(f_ve)
                        wo_ev5[wpar5] = kb.tok("ev")
                        wpar5 ^= 1
                    att_toks5[lpar] = wo_ev5[(wpar5 + 1) % 2]
                w5y.close()
                if DBG:
                    def f_sy(sy, t0=wo_ev5[0], t1=wo_ev5[1]):
                        kb.wait(sy, t0)
                        kb.wait(sy, t1)
                        kb.dma(sy, "dq", out=d_dbg_xi5[:, :], in_=XI[:, :])
                    kb.sy(f_sy)

                with contextlib.ExitStack() as f5x:
                    R2R = f5x.enter_context(nc.sbuf_tensor("R2R", [128, TL], F32))
                    W2LC = f5x.enter_context(nc.sbuf_tensor("W2LC", [128, 8], F32))
                    Z5E = f5x.enter_context(nc.sbuf_tensor("Z5E", [128, 512], F32))
                    st5x = contextlib.ExitStack()
                    E6R = st5x.enter_context(nc.sbuf_tensor("E6R", [1, 512], F32))
                    SQ2 = st5x.enter_context(nc.sbuf_tensor("SQ2", [128, 2 * 512], BF16))
                    SROW2 = st5x.enter_context(nc.sbuf_tensor("SROW2", [1, 512], F32))
                    PST2 = st5x.enter_context(nc.psum_tensor("pst2", [1, 512], F32))
                    PRS2 = st5x.enter_context(nc.psum_tensor("prs2", [128, 512], F32))

                    def f_sy(sy, w0=wo_ev5[0], w1=wo_ev5[1]):
                        kb.wait(sy, w0)
                        kb.wait(sy, w1)
                        kb.dma(sy, "dq", out=W2LC[:, :], in_=d_w2lc[:, :])
                    kb.sy(f_sy)
                    tok_w2l = kb.tok("dq")

                    def f_ve(v, w0=wo_ev5[0], w1=wo_ev5[1]):
                        kb.wait(v, w0)
                        kb.wait(v, w1)
                        v.memset(E6R[:, :], 1e-6)
                        kb.tag(v.memset(Z5E[:, :], 0.0), "ev")
                    kb.ve(f_ve)
                    tok_e6 = kb.tok("ev")

                    sq_free = [None, None]
                    prs_free = None
                    for tb in range(4):
                        t0 = tb * 512
                        for c in range(NCH):
                            cpar = c % 2

                            def f_ve(v, c=c, tb=tb, cpar=cpar, sf=sq_free[cpar],
                                     xw=(wo_ev5[0] if c == 0 and tb == 0 else None),
                                     xw2=(wo_ev5[1] if c == 0 and tb == 0 else None)):
                                kb.wait(v, sf)
                                kb.wait(v, xw)
                                kb.wait(v, xw2)
                                xisl = XI[:, c * TL + tb * 512: c * TL + tb * 512 + 512]
                                kb.tag(v.tensor_mul(SQ2[:, cpar * 512:(cpar + 1) * 512],
                                                    xisl, xisl), "ev")
                            kb.ve(f_ve)
                            tok_sq2 = kb.tok("ev")

                            def f_te(te, c=c, cpar=cpar, tok=tok_sq2,
                                     pf=(prs_free if c == 0 else None)):
                                kb.wait(te, tok)
                                kb.wait(te, pf)
                                kb.tag(te.matmul(PST2[0:1, :], ONESB[:, :],
                                                 SQ2[:, cpar * 512:(cpar + 1) * 512],
                                                 start=(c == 0), stop=(c == NCH - 1)), "mm")
                            kb.te(f_te)
                            sq_free[cpar] = kb.tok("mm")
                        tok_st2 = kb.tok("mm")

                        def f_sc(sc, tok=tok_st2):
                            kb.wait(sc, tok)
                            kb.tag(sc.activation(SROW2[0:1, :], PST2[0:1, :], AF.Copy,
                                                 scale=1.0 / D), "ac")
                        kb.sc(f_sc)
                        tok_sr2 = kb.tok("ac")

                        def f_ve(v, tok=tok_sr2, te6=(tok_e6 if tb == 0 else None)):
                            kb.wait(v, tok)
                            kb.wait(v, te6)
                            kb.tag(v.tensor_add(SROW2[0:1, :], SROW2[0:1, :], E6R[0:1, :]),
                                   "ev")
                        kb.ve(f_ve)
                        tok_v2 = kb.tok("ev")

                        def f_sc(sc, tok=tok_v2):
                            kb.wait(sc, tok)
                            kb.tag(sc.activation(SROW2[0:1, :], SROW2[0:1, :], AF.Sqrt), "ac")
                        kb.sc(f_sc)
                        tok_s2 = kb.tok("ac")

                        def f_ve(v, tok=tok_s2):
                            kb.wait(v, tok)
                            ti = v.reciprocal(SROW2[0:1, :], SROW2[0:1, :])
                            v.drain()
                            kb.tag(ti, "ev")
                        kb.ve(f_ve)
                        tok_r2 = kb.tok("ev")

                        def f_te(te, tok=tok_r2, pf=prs_free):
                            kb.wait(te, tok)
                            kb.wait(te, pf)
                            kb.tag(te.matmul(PRS2[:, :], ONESROWF[:, :], SROW2[0:1, :],
                                             start=True, stop=True), "mm")
                        kb.te(f_te)
                        tok_rp2 = kb.tok("mm")

                        def f_ve(v, t0=t0, tok=tok_rp2):
                            kb.wait(v, tok)
                            kb.tag(v.tensor_copy(R2R[:, t0:t0 + 512], PRS2[:, :]), "ev")
                        kb.ve(f_ve)
                        prs_free = kb.tok("ev")
                    tok_r2r = prs_free
                    st5x.close()

                    with contextlib.ExitStack() as g5x:
                        H2H = g5x.enter_context(nc.sbuf_tensor("H2H", [128, 8 * 512], BF16))
                        TF2 = g5x.enter_context(nc.sbuf_tensor("TF2", [128, 2 * 512], F32))
                        GFB = g5x.enter_context(nc.sbuf_tensor("GFB", [128, 32 * 512], BF16))
                        W1FT = g5x.enter_context(nc.sbuf_tensor("W1FT", [128, 2 * 8 * 128], BF16))
                        W2FT = g5x.enter_context(nc.sbuf_tensor("W2FT", [128, 32 * 128], BF16))
                        PFH = g5x.enter_context(nc.psum_tensor("pfh5", [128, 1024], F32))
                        PFY = g5x.enter_context(nc.psum_tensor("pfy5", [128, 1024], F32))

                        tf_free = [None, None]
                        h2_done = None
                        w1_free = [None, None]
                        w2_free = [None, None]
                        pfh_free = [None, None]
                        pfy_free = [None, None]
                        gf_cons = None
                        xadd_last = None
                        for half in range(4):
                            t0 = half * 512
                            for c in range(NCH):
                                spar = c % 2

                                def f_ve(v, c=c, t0=t0, spar=spar,
                                         tr=(tok_r2r if half == 0 and c == 0 else None),
                                         tw=(tok_w2l if half == 0 and c == 0 else None),
                                         tz=(tok_e6 if half == 0 and c == 0 else None),
                                         tf=tf_free[spar], h2c=h2_done):
                                    kb.wait(v, tr)
                                    kb.wait(v, tw)
                                    kb.wait(v, tz)
                                    kb.wait(v, tf)
                                    if c < 2:
                                        kb.wait(v, h2c)
                                    xisl = XI[:, c * TL + t0: c * TL + t0 + 512]
                                    v.tensor_mul(TF2[:, spar * 512:(spar + 1) * 512],
                                                 xisl, R2R[:, t0:t0 + 512])
                                    v.drain()
                                    ti = v.scalar_tensor_tensor(
                                        H2H[:, c * 512:(c + 1) * 512],
                                        TF2[:, spar * 512:(spar + 1) * 512],
                                        W2LC[:, c:c + 1], Z5E[:, :],
                                        ALU.mult, ALU.add)
                                    kb.tag(ti, "ev")
                                kb.ve(f_ve)
                                tf_free[spar] = kb.tok("ev")
                            h2_done = tf_free[(NCH - 1) % 2]

                            for oc2 in range(32):
                                wpar = oc2 % 2

                                def f_gp(g, oc2=oc2, wpar=wpar, cons=w1_free[wpar],
                                         tr=(tok_r2r if half == 0 and oc2 == 0 else None)):
                                    kb.wait(g, cons)
                                    kb.wait(g, tr)
                                    kb.dma(g, "dg",
                                           out=W1FT[:, wpar * 1024:(wpar + 1) * 1024].rearrange(
                                               "p (dc o) -> p dc o", o=128),
                                           in_=d_fw1[:, oc2 * 128:(oc2 + 1) * 128].rearrange(
                                               "(dc p) o -> p dc o", p=128))
                                kb.gp(f_gp)
                                tok_w1t = kb.tok("dg")

                                def f_te(te, oc2=oc2, wpar=wpar, tok=tok_w1t, hd=h2_done,
                                         pf=pfh_free[wpar]):
                                    kb.wait(te, tok)
                                    kb.wait(te, hd)
                                    kb.wait(te, pf)
                                    for dc in range(NCH):
                                        mm_ = te.matmul(
                                            PFH[:, wpar * 512:(wpar + 1) * 512],
                                            W1FT[:, wpar * 1024 + dc * 128: wpar * 1024 + (dc + 1) * 128],
                                            H2H[:, dc * 512:(dc + 1) * 512],
                                            start=(dc == 0), stop=(dc == NCH - 1))
                                    kb.tag(mm_, "mm")
                                kb.te(f_te)
                                tok_fh = kb.tok("mm")
                                w1_free[wpar] = tok_fh

                                def f_sc(sc, oc2=oc2, wpar=wpar, tok=tok_fh, gc=gf_cons):
                                    kb.wait(sc, tok)
                                    if oc2 < 2:
                                        kb.wait(sc, gc)
                                    kb.tag(sc.activation(GFB[:, oc2 * 512:(oc2 + 1) * 512],
                                                         PFH[:, wpar * 512:(wpar + 1) * 512],
                                                         AF.Gelu), "ac")
                                kb.sc(f_sc)
                                pfh_free[wpar] = kb.tok("ac")
                            gf_last = pfh_free[31 % 2]

                            for oc in range(8):
                                ypar = oc % 2

                                def f_gp(g, oc=oc, cons=w2_free[0]):
                                    kb.wait(g, cons)
                                    kb.dma(g, "dg",
                                           out=W2FT[:, :].rearrange(
                                               "p (hc o) -> p hc o", o=128),
                                           in_=d_fw2[:, oc * 128:(oc + 1) * 128].rearrange(
                                               "(hc p) o -> p hc o", p=128))
                                kb.gp(f_gp)
                                tok_w2t = kb.tok("dg")

                                def f_te(te, oc=oc, ypar=ypar, tok=tok_w2t,
                                         gl=(gf_last if oc == 0 else None),
                                         pf=pfy_free[ypar]):
                                    kb.wait(te, tok)
                                    kb.wait(te, gl)
                                    kb.wait(te, pf)
                                    for hc in range(32):
                                        mm_ = te.matmul(
                                            PFY[:, ypar * 512:(ypar + 1) * 512],
                                            W2FT[:, hc * 128:(hc + 1) * 128],
                                            GFB[:, hc * 512:(hc + 1) * 512],
                                            start=(hc == 0), stop=(hc == 31))
                                    kb.tag(mm_, "mm")
                                kb.te(f_te)
                                tok_fy = kb.tok("mm")
                                w2_free[0] = tok_fy

                                def f_ve(v, oc=oc, t0=t0, ypar=ypar, tok=tok_fy):
                                    kb.wait(v, tok)
                                    xs = X[:, oc * TL + t0: oc * TL + t0 + 512]
                                    kb.tag(v.tensor_add(
                                        xs, xs, PFY[:, ypar * 512:(ypar + 1) * 512]), "ev")
                                kb.ve(f_ve)
                                pfy_free[ypar] = kb.tok("ev")
                                xadd_last = pfy_free[ypar]
                            gf_cons = xadd_last

            barrier()

        if NST >= 1:
            emit_stage_l0ffn(1, 0)
        if NST >= 2:
            emit_stage_l0attn(1)
        if NST >= 3:
            emit_stage_l0ffn(3, 2)
        if NST >= 4:
            emit_stage_l1ffn(4)
        if NST >= 5:
            emit_stage_l1attn(3)
        if NST >= 6:
            emit_stage_l1ffn(6)

        def f_out(sy):
            sy.wait_ge(sems["ev"], C["ev"])
            sy.wait_ge(sems["mm"], C["mm"])
            sy.wait_ge(sems["ac"], C["ac"])
            sy.wait_ge(sems["gv"], C["gv"])
            sy.wait_ge(sems["cc"], C["cc"])
            kb.dma(sy, "dq", out=d_xout[:, :].rearrange("(c p) t -> p c t", p=128),
                   in_=X[:, :].rearrange("p (c t) -> p c t", t=TL))
            sy.wait_ge(sems["dq"], C["dq"])
        kb.sy(f_out)

    return nc



def kernel(**inputs):
    gi = {k: np.asarray(v) for k, v in inputs.items()}
    x = gi["L0_attn_x"].astype(np.float32)
    xT = np.ascontiguousarray(x.reshape(NT, D).T)

    w_inT = np.ascontiguousarray(gi["L0_ffn_w_in"].astype(np.float32).T)
    w_outT = np.ascontiguousarray(gi["L0_ffn_w_out"].astype(np.float32).T)
    ktf = np.ascontiguousarray(gi["L0_ffn_k_ffwd"].astype(np.float32).transpose(0, 1, 3, 2))
    vf = np.ascontiguousarray(gi["L0_ffn_v_ffwd"].astype(np.float32))

    half = 64
    inv_freq = 1.0 / 10000.0 ** (np.arange(half, dtype=np.float64) / half)
    ang = np.arange(S, dtype=np.float64)[None, :] * inv_freq[:, None]
    cosr = np.cos(ang).astype(np.float32).astype(ml_dtypes.bfloat16)
    sinr = np.sin(ang).astype(np.float32).astype(ml_dtypes.bfloat16)

    cmask = np.zeros((4, 128, 512), np.float32)
    kk = np.arange(128)[:, None]
    qq = np.arange(512)[None, :]
    for m in range(4):
        cmask[m] = (qq >= kk + 128 * m).astype(np.float32)
    cmask = cmask.astype(ml_dtypes.bfloat16)

    w1T = np.ascontiguousarray(gi["L1_ffn_w1"].astype(np.float32).T)
    w2T = np.ascontiguousarray(gi["L1_ffn_w2"].astype(np.float32).T)
    b1c = np.ascontiguousarray(gi["L1_ffn_b1"].astype(np.float32).reshape(32, 128).T)
    b2c = np.ascontiguousarray(gi["L1_ffn_b2"].astype(np.float32).reshape(8, 128).T)

    import jax
    with jax.default_device(jax.devices("cpu")[0]):
        keep = np.asarray(jax.random.bernoulli(jax.random.key(42), 0.8, (B, S, D)))
    mskT_all = np.ascontiguousarray(
        np.where(keep, np.float32(0.625), np.float32(0.0)).reshape(NT, D).T
    ).astype(ml_dtypes.bfloat16)

    Wr5 = gi["L1_attn_Wr"].astype(np.float32)
    Wk5 = gi["L1_attn_Wk"].astype(np.float32)
    Wv5 = gi["L1_attn_Wv"].astype(np.float32)
    Wg5 = gi["L1_attn_Wg"].astype(np.float32)
    wA5 = gi["L1_attn_wA"].astype(np.float32)
    mu5c = np.empty((128, 40), np.float32)
    for mi, nm in enumerate(["mu_r", "mu_k", "mu_v", "mu_g", "mu_w"]):
        mu5c[:, mi::5] = gi["L1_attn_" + nm].astype(np.float32).reshape(8, 128).T
    u_flat = gi["L1_attn_u"].astype(np.float32).reshape(1024)
    gnw_flat = gi["L1_attn_gn_w"].astype(np.float32).reshape(1024)
    gnb_flat = gi["L1_attn_gn_b"].astype(np.float32).reshape(1024)
    w0_5 = gi["L1_attn_w0"].astype(np.float32)
    seg5 = np.ones((128, 5 * 1024), np.float32)
    for gix, g in enumerate((8, 16, 32, 64, 128)):
        t = np.arange(1024)
        seg5[:, gix * 1024:(gix + 1) * 1024] = (t % g != 0).astype(np.float32)[None, :]
    s_i = np.arange(128)[:, None]
    t_i = np.arange(128)[None, :]
    msk5 = np.zeros((128, 7 * 128), np.float32)
    msk5[:, 0 * 128:1 * 128] = ((s_i // 8 == t_i // 8) & (s_i < t_i))
    msk5[:, 1 * 128:2 * 128] = ((s_i // 16 == t_i // 16) & (s_i // 8 == t_i // 8 - 1))
    msk5[:, 2 * 128:3 * 128] = ((s_i // 32 == t_i // 32) & (s_i // 16 == t_i // 16 - 1))
    msk5[:, 3 * 128:4 * 128] = ((s_i // 64 == t_i // 64) & (s_i // 32 == t_i // 32 - 1))
    msk5[:, 4 * 128:5 * 128] = (s_i // 64 == t_i // 64 - 1)
    msk5[:, 5 * 128:6 * 128] = np.eye(128)
    msk5[:, 6 * 128:7 * 128] = ((s_i // 64) == (t_i // 64))

    nc = build_program()

    in_maps = []
    for c in range(NC):
        in_maps.append({
            "xin": np.ascontiguousarray(xT[:, c * TL:(c + 1) * TL]),
            "wqfT": np.ascontiguousarray(w_inT[:, 0:1024]),
            "wrtT": np.ascontiguousarray(w_inT[:, 4096:4112]),
            "wqkvT": np.ascontiguousarray(np.concatenate(
                [w_inT[:, 1024 + 128 * c:1024 + 128 * (c + 1)],
                 w_inT[:, 2048 + 128 * c:2048 + 128 * (c + 1)],
                 w_inT[:, 3072 + 128 * c:3072 + 128 * (c + 1)]], axis=1)),
            "woutT": w_outT,
            "wqkv2": np.ascontiguousarray(np.concatenate(
                [gi["L0_attn_w_q"].astype(np.float32)[:, 128 * c:128 * (c + 1)],
                 gi["L0_attn_w_k"].astype(np.float32)[:, 128 * c:128 * (c + 1)],
                 gi["L0_attn_w_v"].astype(np.float32)[:, 128 * c:128 * (c + 1)]], axis=1)),
            "wo0": np.ascontiguousarray(gi["L0_attn_W_o"].astype(np.float32)),
            "w1T": w1T, "w2T": w2T, "b1c": b1c, "b2c": b2c,
            "mskT": np.ascontiguousarray(mskT_all[:, c * TL:(c + 1) * TL]),
            "w5T": np.ascontiguousarray(np.concatenate(
                [Wr5[:, 128 * c:128 * (c + 1)], Wk5[:, 128 * c:128 * (c + 1)],
                 Wv5[:, 128 * c:128 * (c + 1)], Wg5[:, 128 * c:128 * (c + 1)],
                 wA5], axis=1)),
            "mu5c": mu5c,
            "w0c": np.ascontiguousarray(w0_5[128 * c:128 * (c + 1)].reshape(128, 1)),
            "wb5": np.ascontiguousarray(
                gi["L1_attn_wB"].astype(np.float32)[:, 128 * c:128 * (c + 1)]),
            "uc": np.ascontiguousarray(u_flat[128 * c:128 * (c + 1)].reshape(128, 1)),
            "gnwc": np.ascontiguousarray(gnw_flat[128 * c:128 * (c + 1)].reshape(128, 1)),
            "gnbc": np.ascontiguousarray(gnb_flat[128 * c:128 * (c + 1)].reshape(128, 1)),
            "w1lc": np.ascontiguousarray(
                gi["L1_attn_ln1_w"].astype(np.float32).reshape(8, 128).T),
            "w2lc": np.ascontiguousarray(
                gi["L1_attn_ln2_w"].astype(np.float32).reshape(8, 128).T),
            "wo5": np.ascontiguousarray(gi["L1_attn_Wo"].astype(np.float32)),
            "fw1": np.ascontiguousarray(gi["L1_attn_ffn_w1"].astype(np.float32)),
            "fw2": np.ascontiguousarray(gi["L1_attn_ffn_w2"].astype(np.float32)),
            "seg5": seg5,
            "msk5": msk5,
            "ktf": ktf,
            "vf": vf,
            "cosr": cosr,
            "sinr": sinr,
            "cmask": cmask,
            "eye": np.eye(128, dtype=np.float32).astype(ml_dtypes.bfloat16),
        })

    res = run_bass_kernel_spmd(nc, in_maps, core_ids=list(range(NC)))
    outT = np.concatenate([res.results[c]["xout"] for c in range(NC)], axis=1)
    return np.ascontiguousarray(outT.T).reshape(B, S, D)


# version 03 (final)
# speedup vs baseline: 1.0049x; 1.0049x over previous
import os
import math
import contextlib
import numpy as np
import ml_dtypes
import concourse.bass as bass
import concourse.mybir as mybir
from concourse.bass_utils import run_bass_kernel_spmd

F32 = mybir.dt.float32
BF16 = mybir.dt.bfloat16
F16 = mybir.dt.float16
F8 = mybir.dt.float8e4
FP8 = mybir.dt.float8e4
AF = mybir.ActivationFunctionType
ALU = mybir.AluOpType
AX = mybir.AxisListType

NST = int(os.environ.get("MK_STAGES", "6"))
DBG = int(os.environ.get("MK_DEBUG", "0"))
SUB = int(os.environ.get("MK_SUB", "9"))

NC, B, S, TL, NT, D, NCH = 8, 2, 8192, 2048, 16384, 1024, 8


class KB:

    def __init__(self, nc, block, sems):
        self.nc = nc
        self.block = block
        self.sems = sems
        self.cnt = {k: 0 for k in sems}
        self._pid = None

    def pid(self, eng):
        if self._pid is None:
            self._pid = eng.partition_id()
        return self._pid

    def te(self, f): self.block.tensor(f)
    def ve(self, f): self.block.vector(f)
    def sc(self, f): self.block.scalar(f)
    def gp(self, f): self.block.gpsimd(f)
    def sy(self, f): self.block.sync(f)

    def tag(self, instr, name, amt=1):
        instr.then_inc(self.sems[name], amt)
        self.cnt[name] += amt
        return (name, self.cnt[name])

    def dma(self, eng, name, **kw):
        return self.tag(eng.dma_start(**kw), name, 16)

    def wait(self, eng, tok):
        if tok is not None:
            eng.wait_ge(self.sems[tok[0]], tok[1])

    def tok(self, name):
        return (name, self.cnt[name])


def build_program():
    nc = bass.Bass("TRN2", target_bir_lowering=False)

    d_xin = nc.dram_tensor("xin", [D, TL], F32, kind="ExternalInput")
    d_xout = nc.dram_tensor("xout", [D, TL], F32, kind="ExternalOutput")
    d_wqfT = nc.dram_tensor("wqfT", [D, D], F32, kind="ExternalInput")
    d_wrtT = nc.dram_tensor("wrtT", [D, 16], F32, kind="ExternalInput")
    d_wqkvT = nc.dram_tensor("wqkvT", [D, 384], F32, kind="ExternalInput")
    d_woutT = nc.dram_tensor("woutT", [2048, D], F32, kind="ExternalInput")
    d_ktf = nc.dram_tensor("ktf", [16, 8, 128, 256], F32, kind="ExternalInput")
    d_vf = nc.dram_tensor("vf", [16, 8, 256, 128], F32, kind="ExternalInput")
    d_wqkv2 = nc.dram_tensor("wqkv2", [D, 384], F32, kind="ExternalInput")
    d_w1T = nc.dram_tensor("w1T", [D, 4096], F32, kind="ExternalInput")
    d_w2T = nc.dram_tensor("w2T", [4096, D], F32, kind="ExternalInput")
    d_b1c = nc.dram_tensor("b1c", [128, 32], F32, kind="ExternalInput")
    d_b2c = nc.dram_tensor("b2c", [128, 8], F32, kind="ExternalInput")
    d_mskT = nc.dram_tensor("mskT", [D, TL], BF16, kind="ExternalInput")
    d_w5T = nc.dram_tensor("w5T", [D, 576], F32, kind="ExternalInput")
    d_mu5c = nc.dram_tensor("mu5c", [128, 40], F32, kind="ExternalInput")
    d_w0c = nc.dram_tensor("w0c", [128, 1], F32, kind="ExternalInput")
    d_wb5 = nc.dram_tensor("wb5", [64, 128], F32, kind="ExternalInput")
    d_uc = nc.dram_tensor("uc", [128, 1], F32, kind="ExternalInput")
    d_gnwc = nc.dram_tensor("gnwc", [128, 1], F32, kind="ExternalInput")
    d_gnbc = nc.dram_tensor("gnbc", [128, 1], F32, kind="ExternalInput")
    d_w1lc = nc.dram_tensor("w1lc", [128, 8], F32, kind="ExternalInput")
    d_w2lc = nc.dram_tensor("w2lc", [128, 8], F32, kind="ExternalInput")
    d_wo5 = nc.dram_tensor("wo5", [D, D], F32, kind="ExternalInput")
    d_fw1 = nc.dram_tensor("fw1", [D, 4096], F32, kind="ExternalInput")
    d_fw2 = nc.dram_tensor("fw2", [4096, D], F32, kind="ExternalInput")
    d_seg5 = nc.dram_tensor("seg5", [128, 5 * 1024], F32, kind="ExternalInput")
    d_msk5 = nc.dram_tensor("msk5", [128, 7 * 128], F32, kind="ExternalInput")
    _k5 = "ExternalOutput" if DBG else "Internal"
    d_r5 = nc.dram_tensor("r5", [128, NT], BF16, kind=_k5)
    d_k5 = nc.dram_tensor("k5", [128, NT], BF16, kind=_k5)
    d_v5 = nc.dram_tensor("v5", [128, NT], BF16, kind=_k5)
    d_g5 = nc.dram_tensor("g5", [128, NT], BF16, kind=_k5)
    d_rate5 = nc.dram_tensor("rate5", [128, NT], F32, kind=_k5)
    if DBG:
        d_dbg_h5 = nc.dram_tensor("dbg_h5", [128, NCH * TL], BF16, kind="ExternalOutput")
        d_dbg_os5 = nc.dram_tensor("dbg_os5", [128, NT], BF16, kind="ExternalOutput")
        d_dbg_ot5 = nc.dram_tensor("dbg_ot5", [128, NT], BF16, kind="ExternalOutput")
        d_dbg_xi5 = nc.dram_tensor("dbg_xi5", [128, NCH * TL], BF16, kind="ExternalOutput")
    d_wo0 = nc.dram_tensor("wo0", [D, D], F32, kind="ExternalInput")
    d_qkvb = nc.dram_tensor("qkvb", [128, NT], BF16, kind="Internal")
    d_vsp2 = nc.dram_tensor("vsp2", [128, NT], BF16, kind="Internal")
    d_cos = nc.dram_tensor("cosr", [64, S], BF16, kind="ExternalInput")
    d_sin = nc.dram_tensor("sinr", [64, S], BF16, kind="ExternalInput")
    d_cmask = nc.dram_tensor("cmask", [4, 128, 512], BF16, kind="ExternalInput")

    d_ag_in = [nc.dram_tensor(f"ag{i}_in", [D, TL], BF16, kind="Internal") for i in range(4)]
    d_ag_out = [nc.dram_tensor(f"ag{i}_out", [NC * D, TL], BF16, kind="Internal") for i in range(4)]
    d_ret_in = [nc.dram_tensor(f"ret{i}_in", [128, NT], BF16, kind="Internal") for i in range(4)]
    d_ffo = nc.dram_tensor("ffo", [NCH * 128, TL], BF16, kind="Internal")
    d_gt = nc.dram_tensor("gtrow", [16, TL], BF16, kind="Internal")
    if DBG:
        d_dbg_hb = nc.dram_tensor("dbg_hb", [D, TL], BF16, kind="ExternalOutput")
        d_dbg_rstd = nc.dram_tensor("dbg_rstd", [128, TL], F32, kind="ExternalOutput")
        d_dbg_mrs = nc.dram_tensor("dbg_mrs", [128, TL], F32, kind="ExternalOutput")
        d_dbg_qf = nc.dram_tensor("dbg_qf", [128, NCH * TL], BF16, kind="ExternalOutput")
        d_dbg_gt = nc.dram_tensor("dbg_gt", [16, TL], BF16, kind="ExternalOutput")
        d_dbg_ffo = nc.dram_tensor("dbg_ffo", [NCH * 128, TL], BF16, kind="ExternalOutput")
        d_dbg_os = nc.dram_tensor("dbg_os", [128, NT], BF16, kind="ExternalOutput")
    d_eye = nc.dram_tensor("eye", [128, 128], BF16, kind="ExternalInput")
    d_vsp = nc.dram_tensor("vsp", [128, S], BF16, kind="Internal")
    d_ret_out = [nc.dram_tensor(f"ret{i}_out", [NC * 128, NT], BF16, kind="Internal") for i in range(4)]

    with contextlib.ExitStack() as top:
        ec = top.enter_context
        s_dq = ec(nc.semaphore("k_dq"))
        s_dg = ec(nc.semaphore("k_dg"))
        s_cc = ec(nc.semaphore("k_cc"))
        s_mm = ec(nc.semaphore("k_mm"))
        s_ev = ec(nc.semaphore("k_ev"))
        s_ac = ec(nc.semaphore("k_ac"))
        s_gv = ec(nc.semaphore("k_gv"))
        s_ph = ec(nc.semaphore("k_ph"))
        X = ec(nc.sbuf_tensor("X", [128, NCH * TL], F32))
        CPOOL = ec(nc.sbuf_tensor("CPOOL", [128, 8], F32))
        ONESB = ec(nc.sbuf_tensor("ONESB", [128, 1], BF16))
        ONESROW = ec(nc.sbuf_tensor("ONESROW", [1, 128], BF16))
        ONESROWF = ec(nc.sbuf_tensor("ONESROWF", [1, 128], F32))
        STAT = ec(nc.sbuf_tensor("STAT", [1, 2048], F32))
        RSTDR = ec(nc.sbuf_tensor("RSTDR", [128, TL], F32))
        MRSR = ec(nc.sbuf_tensor("MRSR", [128, TL], F32))
        IDE = ec(nc.sbuf_tensor("IDE", [128, 128], BF16))
        EPSR = ec(nc.sbuf_tensor("EPSR", [1, 512], F32))
        ZERO16 = ec(nc.sbuf_tensor("ZERO16", [128, 16], F32))
        block = ec(nc.Block())
        sems = {"dq": s_dq, "dg": s_dg, "cc": s_cc, "mm": s_mm,
                "ev": s_ev, "ac": s_ac, "gv": s_gv, "ph": s_ph}
        kb = KB(nc, block, sems)
        C = kb.cnt

        def f_sy(sy):
            kb.dma(sy, "dq", out=X[:, :].rearrange("p (c t) -> p c t", t=TL),
                   in_=d_xin[:, :].rearrange("(c p) t -> p c t", p=128))
            kb.dma(sy, "dq", out=IDE[:, :], in_=d_eye[:, :])
        kb.sy(f_sy)
        tok_init = kb.tok("dq")

        def f_ve(v):
            kb.wait(v, tok_init)
            v.memset(CPOOL[:, 0:1], 0.5)
            v.memset(CPOOL[:, 1:2], -1e30)
            v.memset(CPOOL[:, 2:3], 1.0)
            v.memset(ONESB[:, :], 1.0)
            v.memset(EPSR[:, :], 1e-5)
            v.memset(ZERO16[:, :], 0.0)
            v.memset(ONESROW[:, :], 1.0)
            kb.tag(v.memset(ONESROWF[:, :], 1.0), "ev")
        kb.ve(f_ve)
        tok_consts = kb.tok("ev")

        def barrier():
            target = C["ph"] + 5

            def f(eng):
                eng.nop().then_inc(sems["ph"], 1)
                eng.wait_ge(sems["ph"], target)
            for emit in (kb.te, kb.ve, kb.sc, kb.gp, kb.sy):
                emit(f)
            C["ph"] += 5

        def emit_ln(tag, HB, extra_apply=None, extra_rows=None, apply5=None):
            with contextlib.ExitStack() as lnst:
                XB = lnst.enter_context(nc.sbuf_tensor(f"lnxb_{tag}", [128, 2 * 512], BF16))
                SQ = lnst.enter_context(nc.sbuf_tensor(f"lnsq_{tag}", [128, 2 * 512], BF16))
                TT = lnst.enter_context(nc.sbuf_tensor(f"lntt_{tag}", [128, 512], F32))
                PST = lnst.enter_context(nc.psum_tensor(f"lnst_{tag}", [1, 1024], F32))
                PREP = lnst.enter_context(nc.psum_tensor(f"lnrep_{tag}", [128, 1024], F32))
                tok_rep_done = None
                tok_rep_mm_prev = None
                tok_extra_prev = None
                cast_free = [None, None]
                for tb in range(4):
                    t0 = tb * 512
                    tok_cast = None
                    for c in range(NCH):
                        xs = X[:, c * TL + t0: c * TL + t0 + 512]
                        cpar = c % 2

                        def f_ve(v, xs=xs, c=c, tb=tb, cpar=cpar, cf=cast_free[cpar]):
                            if c == 0:
                                kb.wait(v, tok_consts)
                                if tb > 0:
                                    kb.wait(v, tok_stats_read)
                            kb.wait(v, cf)
                            v.tensor_copy(XB[:, cpar * 512:(cpar + 1) * 512], xs)
                            kb.tag(v.tensor_mul(SQ[:, cpar * 512:(cpar + 1) * 512], xs, xs), "ev")
                        kb.ve(f_ve)
                        tok_cast = kb.tok("ev")

                        def f_te(te, c=c, cpar=cpar, tok_cast=tok_cast):
                            kb.wait(te, tok_cast)
                            te.matmul(PST[0:1, 0:512], ONESB[:, :],
                                      XB[:, cpar * 512:(cpar + 1) * 512],
                                      start=(c == 0), stop=(c == NCH - 1))
                            mi = te.matmul(PST[0:1, 512:1024], ONESB[:, :],
                                           SQ[:, cpar * 512:(cpar + 1) * 512],
                                           start=(c == 0), stop=(c == NCH - 1))
                            kb.tag(mi, "mm")
                        kb.te(f_te)
                        cast_free[cpar] = kb.tok("mm")
                    tok_stats_mm = kb.tok("mm")

                    def f_sc(sc, t0=t0, tok=tok_stats_mm, trm=tok_rep_mm_prev,
                             trx=tok_extra_prev):
                        kb.wait(sc, tok)
                        kb.wait(sc, trm)
                        kb.wait(sc, trx)
                        sc.activation(STAT[0:1, 0:512], PST[0:1, 0:512], AF.Copy,
                                      scale=1.0 / D)
                        kb.tag(sc.activation(STAT[0:1, 1024:1536], PST[0:1, 512:1024],
                                             AF.Copy, scale=1.0 / D), "ac")
                    kb.sc(f_sc)
                    tok_me2 = kb.tok("ac")

                    def f_ve(v, t0=t0, tok=tok_me2):
                        kb.wait(v, tok)
                        m = STAT[0:1, 0:512]
                        vv = STAT[0:1, 1024:1536]
                        sc2 = STAT[0:1, 1536:2048]
                        v.tensor_mul(sc2, m, m)
                        v.drain()
                        v.tensor_sub(vv, vv, sc2)
                        v.drain()
                        ti = v.tensor_add(vv, vv, EPSR[0:1, :])
                        v.drain()
                        kb.tag(ti, "ev")
                    kb.ve(f_ve)
                    tok_stats_read = kb.tok("ev")
                    tok_var = tok_stats_read

                    def f_sc(sc, t0=t0, tok=tok_var):
                        kb.wait(sc, tok)
                        kb.tag(sc.activation(STAT[0:1, 1536:2048],
                                             STAT[0:1, 1024:1536],
                                             AF.Sqrt), "ac")
                    kb.sc(f_sc)
                    tok_sqrt = kb.tok("ac")

                    def f_ve(v, t0=t0, tok=tok_sqrt):
                        kb.wait(v, tok)
                        rs = STAT[0:1, 512:1024]
                        mrs = STAT[0:1, 1536:2048]
                        v.reciprocal(rs, mrs)
                        v.drain()
                        ti = v.tensor_mul(mrs, STAT[0:1, 0:512], rs)
                        v.drain()
                        kb.tag(ti, "ev")
                    kb.ve(f_ve)
                    tok_rows = kb.tok("ev")

                    def f_te(te, t0=t0, tok=tok_rows, tok_prev=tok_rep_done):
                        kb.wait(te, tok)
                        kb.wait(te, tok_prev)
                        te.matmul(PREP[:, 0:512], ONESROWF[:, :],
                                  STAT[0:1, 512:1024], start=True, stop=True)
                        kb.tag(te.matmul(PREP[:, 512:1024], ONESROWF[:, :],
                                         STAT[0:1, 1536:2048],
                                         start=True, stop=True), "mm")
                    kb.te(f_te)
                    tok_rep_mm = kb.tok("mm")
                    tok_rep_mm_prev = tok_rep_mm

                    def f_ve(v, t0=t0, tok=tok_rep_mm):
                        kb.wait(v, tok)
                        v.tensor_copy(RSTDR[:, t0:t0 + 512], PREP[:, 0:512])
                        kb.tag(v.tensor_copy(MRSR[:, t0:t0 + 512], PREP[:, 512:1024]), "ev")
                    kb.ve(f_ve)
                    tok_rep_done = kb.tok("ev")
                    if extra_rows is not None:
                        tok_extra_prev = extra_rows(tb, tok_rows)

                for c in range(NCH):
                    for tb in range(4):
                        xs = X[:, c * TL + tb * 512: c * TL + tb * 512 + 512]
                        hs = HB[:, c * TL + tb * 512: c * TL + tb * 512 + 512]

                        if apply5 is not None:
                            RAR, RBR, W1LC, ZERO512 = apply5

                            def f_ve(v, xs=xs, hs=hs, c=c, tb=tb):
                                v.tensor_mul(TT[:, :], xs, RAR[:, tb * 512:tb * 512 + 512])
                                v.drain()
                                v.tensor_sub(TT[:, :], TT[:, :],
                                             RBR[:, tb * 512:tb * 512 + 512])
                                v.drain()
                                ti = v.scalar_tensor_tensor(
                                    hs, TT[:, :], W1LC[:, c:c + 1], ZERO512[:, :],
                                    ALU.mult, ALU.add)
                                if c == NCH - 1 and tb == 3:
                                    kb.tag(ti, "ev")
                            kb.ve(f_ve)
                            continue

                        def f_ve(v, xs=xs, hs=hs, c=c, tb=tb):
                            v.tensor_mul(TT[:, :], xs, RSTDR[:, tb * 512:tb * 512 + 512])
                            v.drain()
                            ti = v.tensor_sub(hs, TT[:, :], MRSR[:, tb * 512:tb * 512 + 512])
                            if extra_apply is not None:
                                extra_apply(v, c, tb, TT)
                            if c == NCH - 1 and tb == 3:
                                kb.tag(ti, "ev")
                        kb.ve(f_ve)
                h_done = kb.tok("ev")
            return h_done

        def emit_stage_l0ffn(stg, exi):
            ag_in, ag_out = d_ag_in[exi], d_ag_out[exi]
            ret_in, ret_out = d_ret_in[exi], d_ret_out[exi]

            with contextlib.ExitStack() as stx:
                QF = stx.enter_context(nc.sbuf_tensor(f"QF{stg}", [128, NCH * TL], BF16))
                GT = stx.enter_context(nc.sbuf_tensor(f"GT{stg}", [16, TL], BF16))
                qsx = contextlib.ExitStack()
                WQF = qsx.enter_context(nc.sbuf_tensor(f"WQF{stg}", [128, NCH * 1024], BF16))
                WRT = qsx.enter_context(nc.sbuf_tensor(f"WRT{stg}", [128, NCH * 16], BF16))
                RTT = qsx.enter_context(nc.sbuf_tensor(f"RTT{stg}", [128, 16], F32))
                MCOL = qsx.enter_context(nc.sbuf_tensor(f"MCOL{stg}", [128, 8], F32))
                EQT = qsx.enter_context(nc.sbuf_tensor(f"EQT{stg}", [128, 16], F32))
                SGT = qsx.enter_context(nc.sbuf_tensor(f"SGT{stg}", [128, 16], F32))
                GTM = qsx.enter_context(nc.sbuf_tensor(f"GTM{stg}", [128, 16], BF16))
                stx0 = contextlib.ExitStack()
                HB = stx0.enter_context(nc.sbuf_tensor(f"HBs{stg}", [128, NCH * TL], BF16))
                h_done = emit_ln(f"s{stg}", HB)
                if SUB <= 1:
                    stx0.close()
                    qsx.close()
                    return

                def f_sy(sy):
                    kb.wait(sy, h_done)
                    kb.dma(sy, "dq", out=ag_in[:, :].rearrange("(c p) t -> p c t", p=128),
                           in_=HB[:, :].rearrange("p (c t) -> p c t", t=TL))
                kb.sy(f_sy)
                tok_hstaged = kb.tok("dq")

                PP = qsx.enter_context(nc.psum_tensor(f"pp{stg}", [128, 1024], F32))
                PRT = qsx.enter_context(nc.psum_tensor(f"prt{stg}", [128, 1024], F32))
                PTR = qsx.enter_context(nc.psum_tensor(f"ptr{stg}", [16, 2048], BF16))

                if True:

                    def f_gp(g):
                        kb.dma(g, "dg", out=WQF[:, :].rearrange("p (dc o) -> p dc o", o=1024),
                               in_=d_wqfT[:, :].rearrange("(dc p) o -> p dc o", p=128))
                        kb.dma(g, "dg", out=WRT[:, :].rearrange("p (dc o) -> p dc o", o=16),
                               in_=d_wrtT[:, :].rearrange("(dc p) o -> p dc o", p=128))
                    kb.gp(f_gp)
                    tok_wqf = kb.tok("dg")

                    if DBG:
                        def f_dbgw(sy, tok=tok_wqf):
                            kb.wait(sy, tok)
                            kb.dma(sy, "dq", out=d_dbg_os[:, 0:8192], in_=WQF[:, :])
                        kb.sy(f_dbgw)

                    def f_gp(g):
                        kb.wait(g, tok_hstaged)
                        kb.tag(g.collective_compute(
                            "AllGather", ALU.bypass, ins=[ag_in[:, :]], outs=[ag_out[:, :]],
                            replica_groups=[list(range(NC))]), "cc")
                    kb.gp(f_gp)
                    tok_ag = kb.tok("cc")

                    ev_par = [None, None]
                    par = 0
                    first = True
                    for oc in range(NCH):
                        for tb in range(4):
                            w8 = tok_wqf if first else None
                            evp = ev_par[par]

                            def f_te(te, oc=oc, tb=tb, par=par, w8=w8, evp=evp):
                                if w8 is not None:
                                    kb.wait(te, w8)
                                    kb.wait(te, h_done)
                                kb.wait(te, evp)
                                for dc in range(NCH):
                                    mi = te.matmul(
                                        PP[:, par * 512:par * 512 + 512],
                                        WQF[:, dc * 1024 + oc * 128: dc * 1024 + oc * 128 + 128],
                                        HB[:, dc * TL + tb * 512: dc * TL + tb * 512 + 512],
                                        start=(dc == 0), stop=(dc == NCH - 1))
                                kb.tag(mi, "mm")
                            kb.te(f_te)
                            tok_mm = kb.tok("mm")

                            def f_ve(v, oc=oc, tb=tb, par=par, tok=tok_mm):
                                kb.wait(v, tok)
                                kb.tag(v.tensor_copy(
                                    QF[:, oc * TL + tb * 512: oc * TL + tb * 512 + 512],
                                    PP[:, par * 512:par * 512 + 512]), "ev")
                            kb.ve(f_ve)
                            ev_par[par] = kb.tok("ev")
                            par ^= 1
                            first = False
                    tok_qf_done_dbg = kb.tok("ev")

                    gate_ev = None
                    prt_free = [None, None]
                    ptr_free = [None, None]
                    prev_tpose = None
                    for tt in range(16):
                        tpar = tt % 2

                        def f_te(te, tt=tt, tpar=tpar, pf=prt_free[tpar]):
                            kb.wait(te, pf)
                            for dc in range(NCH):
                                mi = te.matmul(PRT[:, tpar * 512:tpar * 512 + 16],
                                               HB[:, dc * TL + tt * 128: dc * TL + tt * 128 + 128],
                                               WRT[:, dc * 16:(dc + 1) * 16],
                                               start=(dc == 0), stop=(dc == NCH - 1))
                            kb.tag(mi, "mm")
                        kb.te(f_te)
                        tok_prt = kb.tok("mm")

                        def f_ve(v, tt=tt, tpar=tpar, tok=tok_prt):
                            kb.wait(v, tok)
                            v.tensor_copy(RTT[:, 0:16], PRT[:, tpar * 512:tpar * 512 + 16])
                            v.drain()
                            ti = v.reduce_max(MCOL[:, 0:1], RTT[:, 0:16], axis=AX.X)
                            v.drain()
                            kb.tag(ti, "ev")
                        kb.ve(f_ve)
                        tok_rtt = kb.tok("ev")
                        prt_free[tpar] = tok_rtt

                        def f_sc(sc, tok=tok_rtt):
                            kb.wait(sc, tok)
                            kb.tag(sc.activation(EQT[:, :], RTT[:, 0:16], AF.Identity,
                                                 scale=-1.0, bias=MCOL[:, 0:1]), "ac")
                        kb.sc(f_sc)
                        tok_d1 = kb.tok("ac")

                        def f_ve(v, tok=tok_d1, ptp=prev_tpose):
                            kb.wait(v, tok)
                            kb.wait(v, ptp)
                            ti = v.tensor_tensor(EQT[:, :], EQT[:, :], ZERO16[:, :],
                                                 ALU.is_equal)
                            v.drain()
                            kb.tag(ti, "ev")
                        kb.ve(f_ve)
                        tok_eq = kb.tok("ev")

                        def f_sc(sc, tok=tok_eq):
                            kb.wait(sc, tok)
                            kb.tag(sc.activation(SGT[:, 0:16], EQT[:, :], AF.Copy,
                                                 scale=-1e30), "ac")
                        kb.sc(f_sc)
                        tok_t2 = kb.tok("ac")

                        def f_ve(v, tok=tok_t2):
                            kb.wait(v, tok)
                            v.tensor_add(EQT[:, :], SGT[:, 0:16], RTT[:, 0:16])
                            v.drain()
                            ti = v.reduce_max(MCOL[:, 0:1], EQT[:, :], axis=AX.X)
                            v.drain()
                            kb.tag(ti, "ev")
                        kb.ve(f_ve)
                        tok_m2 = kb.tok("ev")

                        def f_sc(sc, tok=tok_m2):
                            kb.wait(sc, tok)
                            sc.activation(EQT[:, :], RTT[:, 0:16], AF.Identity,
                                          scale=-1.0, bias=MCOL[:, 0:1])
                            kb.tag(sc.activation(SGT[:, 0:16], RTT[:, 0:16], AF.Sigmoid), "ac")
                        kb.sc(f_sc)
                        tok_d2 = kb.tok("ac")

                        def f_ve(v, tt=tt, tok=tok_d2):
                            kb.wait(v, tok)
                            v.tensor_tensor(EQT[:, :], EQT[:, :], ZERO16[:, :], ALU.is_le)
                            v.drain()
                            ti = v.tensor_mul(GTM[:, 0:16], EQT[:, :], SGT[:, 0:16])
                            v.drain()
                            if tt == 0 and DBG:
                                v.tensor_copy(MRSR[:, 0:16], RTT[:, 0:16])
                                v.tensor_copy(MRSR[:, 16:32], EQT[:, :])
                                v.tensor_copy(MRSR[:, 32:48], SGT[:, 0:16])
                                v.tensor_copy(MRSR[:, 48:64], GTM[:, 0:16])
                                v.tensor_copy(MRSR[:, 64:65], MCOL[:, 0:1])
                            kb.tag(ti, "ev")
                        kb.ve(f_ve)
                        tok_gtm = kb.tok("ev")

                        def f_te(te, tt=tt, tpar=tpar, tok=tok_gtm, pf=ptr_free[tpar]):
                            kb.wait(te, tok)
                            kb.wait(te, pf)
                            kb.tag(te.transpose(PTR[0:16, tpar * 1024:tpar * 1024 + 128],
                                                GTM[:, 0:16], IDE[:, :]), "mm")
                        kb.te(f_te)
                        tok_ptr = kb.tok("mm")
                        prev_tpose = tok_ptr

                        def f_ve(v, tt=tt, tpar=tpar, tok=tok_ptr):
                            kb.wait(v, tok)
                            kb.tag(v.tensor_copy(GT[0:16, tt * 128:(tt + 1) * 128],
                                                 PTR[0:16, tpar * 1024:tpar * 1024 + 128]), "ev")
                        kb.ve(f_ve)
                        ptr_free[tpar] = kb.tok("ev")
                    gates_done = kb.tok("ev")

                    def f_sy(sy, tok=gates_done):
                        kb.wait(sy, tok)
                        kb.dma(sy, "dq", out=d_gt[:, :], in_=GT[0:16, :])
                    kb.sy(f_sy)
                    gates_hbm = kb.tok("dq")

                if DBG:
                    def f_dbg0(sy):
                        sy.wait_ge(sems["ev"], C["ev"])
                        kb.dma(sy, "dq", out=d_dbg_hb[:, :], in_=ag_in[:, :])
                        kb.dma(sy, "dq", out=d_dbg_rstd[:, :], in_=RSTDR[:, :])
                        kb.dma(sy, "dq", out=d_dbg_mrs[:, :], in_=MRSR[:, :])
                        kb.dma(sy, "dq", out=d_dbg_gt[:, :], in_=d_gt[:, :])
                    kb.sy(f_dbg0)
                if SUB <= 2:
                    stx0.close()
                    return
                stx0.close()
                qsx.close()

                with contextlib.ExitStack() as esx:
                    KTE = esx.enter_context(nc.sbuf_tensor(f"KTE{stg}", [128, 2 * 4096], BF16))
                    VE = esx.enter_context(nc.sbuf_tensor(f"VE{stg}", [128, 2 * 4096], BF16))
                    GTB = esx.enter_context(nc.sbuf_tensor(f"GTB{stg}", [128, 16 * 512], BF16))
                    ASB = esx.enter_context(nc.sbuf_tensor(f"ASB{stg}", [128, 2 * 1024], BF16))
                    AGB = esx.enter_context(nc.sbuf_tensor(f"AGB{stg}", [128, 2 * 1024], BF16))
                    FB = esx.enter_context(nc.sbuf_tensor(f"FB{stg}", [128, 2 * 512], BF16))
                    GROW = esx.enter_context(nc.sbuf_tensor(f"GROW{stg}", [1, 16 * 512], BF16))
                    PSS = esx.enter_context(nc.psum_tensor(f"pss{stg}", [128, 2048], F32))
                    POFF = esx.enter_context(nc.psum_tensor(f"poff{stg}", [128, 512], F32))
                    PGR = esx.enter_context(nc.psum_tensor(f"pgr{stg}", [128, 1024], F32))

                    wtoks = {}
                    hh_done_toks = {}
                    fb_toks = [None, None]

                    def load_expw(g, tb, hh, par):
                        kb.dma(g, "dg",
                               out=KTE[:, par * 4096:(par + 1) * 4096].rearrange(
                                   "p (e s) -> p e s", s=256),
                               in_=d_ktf[:, hh].rearrange("e p s -> p e s"))
                        for sc_ in range(2):
                            tk = kb.dma(
                                g, "dg",
                                out=VE[:, par * 4096 + sc_ * 2048: par * 4096 + sc_ * 2048 + 2048].rearrange(
                                    "p (e d) -> p e d", d=128),
                                in_=d_vf[:, hh, sc_ * 128:(sc_ + 1) * 128, :].rearrange("e p d -> p e d"))
                        wtoks[(tb, hh)] = tk

                    def f_gp(g):
                        kb.wait(g, gates_done)
                        load_expw(g, 0, 0, 0)
                        load_expw(g, 0, 1, 1)
                    kb.gp(f_gp)
                    if DBG:
                        def f_dbgk(sy, tok=wtoks[(0, 1)]):
                            kb.wait(sy, tok)
                            kb.dma(sy, "dq", out=d_dbg_os[:, 8192:12288],
                                   in_=KTE[:, 0:4096])
                        kb.sy(f_dbgk)

                    gtb_done = None
                    grep_ev = [None, None]
                    grow_prev = None
                    for tb in range(4):
                        def f_sy(sy, tb=tb, tok=gates_hbm, pv=grow_prev):
                            kb.wait(sy, tok)
                            kb.wait(sy, pv)
                            kb.dma(sy, "dq",
                                   out=GROW[0:1, :].rearrange("p (e t) -> p e t", t=512),
                                   in_=d_gt[:, tb * 512:tb * 512 + 512])
                        kb.sy(f_sy)
                        tok_grow = kb.tok("dq")

                        gpar = 0
                        for e in range(16):
                            def f_te(te, e=e, tb=tb, gpar=gpar,
                                     gd=(tok_grow if e == 0 else None),
                                     evp=grep_ev[gpar]):
                                if gd is not None:
                                    kb.wait(te, gd)
                                kb.wait(te, evp)
                                kb.tag(te.matmul(PGR[:, gpar * 512:gpar * 512 + 512],
                                                 ONESROW[:, :],
                                                 GROW[0:1, e * 512:(e + 1) * 512],
                                                 start=True, stop=True), "mm")
                            kb.te(f_te)
                            tok_g = kb.tok("mm")

                            def f_ve(v, e=e, gpar=gpar, tok=tok_g):
                                kb.wait(v, tok)
                                kb.tag(v.tensor_copy(GTB[:, e * 512:(e + 1) * 512],
                                                     PGR[:, gpar * 512:gpar * 512 + 512]), "ev")
                            kb.ve(f_ve)
                            grep_ev[gpar] = kb.tok("ev")
                            gpar ^= 1
                        gtb_done = kb.tok("ev")
                        grow_prev = gtb_done
                        if DBG and tb == 0:
                            def f_dbgg(sy, tok=gtb_done):
                                kb.wait(sy, tok)
                                kb.dma(sy, "dq", out=d_dbg_os[:, 12288:16384],
                                       in_=GTB[:, 0:4096])
                            kb.sy(f_dbgg)

                        for hh in range(8):
                            par = hh % 2

                            spar_toks = [None, None]
                            mul_toks = [None, None]
                            omm_tok = None
                            for e in range(16):
                                epar = e % 2
                                wtok = wtoks[(tb, hh)] if e == 0 else None

                                def f_te(te, e=e, tb=tb, hh=hh, par=par, epar=epar,
                                         wtok=wtok, gf=spar_toks[epar],
                                         gtbd=(gtb_done if e == 0 else None)):
                                    if wtok is not None:
                                        kb.wait(te, wtok)
                                    if gtbd is not None:
                                        kb.wait(te, gtbd)
                                    kb.wait(te, gf)
                                    for scc in range(2):
                                        mi = te.matmul(
                                            PSS[:, epar * 1024 + scc * 512: epar * 1024 + scc * 512 + 512],
                                            KTE[:, par * 4096 + e * 256 + scc * 128: par * 4096 + e * 256 + scc * 128 + 128],
                                            QF[:, hh * TL + tb * 512: hh * TL + tb * 512 + 512],
                                            start=True, stop=True)
                                    kb.tag(mi, "mm")
                                kb.te(f_te)
                                tok_smm = kb.tok("mm")

                                def f_sc(scx, epar=epar, tok=tok_smm, mf=mul_toks[epar]):
                                    kb.wait(scx, tok)
                                    if mf is not None:
                                        kb.wait(scx, mf[0])
                                        kb.wait(scx, mf[1])
                                    kb.tag(scx.activation(ASB[:, epar * 1024:(epar + 1) * 1024],
                                                          PSS[:, epar * 1024:(epar + 1) * 1024],
                                                          AF.Gelu), "ac")
                                kb.sc(f_sc)
                                tok_gelu = kb.tok("ac")
                                spar_toks[epar] = tok_gelu

                                def f_ve(v, e=e, epar=epar, tok=tok_gelu):
                                    kb.wait(v, tok)
                                    kb.tag(v.tensor_mul(AGB[:, epar * 1024:epar * 1024 + 512],
                                                        ASB[:, epar * 1024:epar * 1024 + 512],
                                                        GTB[:, e * 512:(e + 1) * 512]), "ev")
                                kb.ve(f_ve)
                                tok_mv = kb.tok("ev")

                                def f_gp(g, e=e, epar=epar, tok=tok_gelu):
                                    kb.wait(g, tok)
                                    kb.tag(g.tensor_mul(AGB[:, epar * 1024 + 512:epar * 1024 + 1024],
                                                        ASB[:, epar * 1024 + 512:epar * 1024 + 1024],
                                                        GTB[:, e * 512:(e + 1) * 512]), "gv")
                                kb.gp(f_gp)
                                tok_mg = kb.tok("gv")
                                mul_toks[epar] = (tok_mv, tok_mg)
                                if DBG and tb == 0 and hh == 0 and e == 0:
                                    def f_dbga(v, tv=tok_mv, tg=tok_mg):
                                        kb.wait(v, tv)
                                        kb.wait(v, tg)
                                        v.tensor_copy(MRSR[:, 0:1024], ASB[:, 0:1024])
                                        kb.tag(v.tensor_copy(MRSR[:, 1024:2048], AGB[:, 0:1024]), "ev")
                                    kb.ve(f_dbga)

                                prev_ffo = None
                                if e == 0:
                                    pk = (tb, hh - 1) if hh >= 1 else (((tb - 1, 7)) if tb > 0 else None)
                                    prev_ffo = hh_done_toks.get(pk) if pk else None

                                def f_te(te, e=e, par=par, epar=epar, tv=tok_mv, tg=tok_mg, pf=prev_ffo):
                                    kb.wait(te, tv)
                                    kb.wait(te, tg)
                                    kb.wait(te, pf)
                                    for scc in range(2):
                                        mi = te.matmul(
                                            POFF[:, :],
                                            VE[:, par * 4096 + scc * 2048 + e * 128: par * 4096 + scc * 2048 + e * 128 + 128],
                                            AGB[:, epar * 1024 + scc * 512: epar * 1024 + scc * 512 + 512],
                                            start=(e == 0 and scc == 0), stop=(e == 15 and scc == 1))
                                    kb.tag(mi, "mm")
                                kb.te(f_te)
                                omm_tok = kb.tok("mm")

                            fpar = (tb * 8 + hh) % 2

                            def f_ve(v, hh=hh, tb=tb, fpar=fpar, tok=omm_tok, fbf=fb_toks[fpar]):
                                kb.wait(v, tok)
                                kb.wait(v, fbf)
                                kb.tag(v.tensor_copy(FB[:, fpar * 512:(fpar + 1) * 512],
                                                     POFF[:, :]), "ev")
                            kb.ve(f_ve)
                            tok_fb = kb.tok("ev")
                            hh_done_toks[(tb, hh)] = tok_fb

                            nxt = (tb, hh + 2) if hh + 2 < 8 else ((tb + 1, hh + 2 - 8) if tb + 1 < 4 else None)
                            if nxt is not None:
                                def f_gp(g, nxt=nxt, par=par, pu=omm_tok):
                                    kb.wait(g, pu)
                                    load_expw(g, nxt[0], nxt[1], par)
                                kb.gp(f_gp)

                            def f_sy(sy, hh=hh, tb=tb, fpar=fpar, tok=tok_fb):
                                kb.wait(sy, tok)
                                kb.dma(sy, "dq",
                                       out=d_ffo[hh * 128:(hh + 1) * 128, tb * 512:tb * 512 + 512],
                                       in_=FB[:, fpar * 512:(fpar + 1) * 512])
                            kb.sy(f_sy)
                            fb_toks[fpar] = kb.tok("dq")
                    experts_done = kb.tok("dq")
                    if DBG:
                        def f_dbg(sy, tok=experts_done):
                            kb.wait(sy, tok)
                            kb.wait(sy, tok_qf_done_dbg)
                            kb.dma(sy, "dq", out=d_dbg_qf[:, :], in_=QF[:, :])
                        kb.sy(f_dbg)

            if DBG:
                def f_dbgf(sy, tok=experts_done):
                    kb.wait(sy, tok)
                    sy.wait_ge(sems["ev"], C["ev"])
                    kb.dma(sy, "dq", out=d_dbg_ffo[:, :], in_=d_ffo[:, :])
                    kb.dma(sy, "dq", out=d_dbg_mrs[:, :], in_=MRSR[:, :])
                kb.sy(f_dbgf)
            if SUB <= 3:
                return
            with contextlib.ExitStack() as asx:
                CSN = asx.enter_context(nc.sbuf_tensor(f"CSN{stg}", [128, S], BF16))
                CMASK = asx.enter_context(nc.sbuf_tensor(f"CMASK{stg}", [128, 4 * 512], BF16))
                WQKV = asx.enter_context(nc.sbuf_tensor(f"WQKV{stg}", [128, NCH * 384], BF16))
                QS = asx.enter_context(nc.sbuf_tensor(f"QS{stg}", [128, S], BF16))
                KS = asx.enter_context(nc.sbuf_tensor(f"KS{stg}", [128, S], BF16))
                VT = asx.enter_context(nc.sbuf_tensor(f"VT{stg}", [128, S], BF16))
                HL = asx.enter_context(nc.sbuf_tensor(f"HL{stg}", [128, 2 * 8 * 512], BF16))
                UB = asx.enter_context(nc.sbuf_tensor(f"UB{stg}", [128, 2 * 512], BF16))
                AU = asx.enter_context(nc.sbuf_tensor(f"AU{stg}", [128, 512], BF16))
                CSR = asx.enter_context(nc.sbuf_tensor(f"CSR{stg}", [1, 512], F32))
                RECS = asx.enter_context(nc.sbuf_tensor(f"RECS{stg}", [128, 512], F32))
                RSA = asx.enter_context(nc.sbuf_tensor(f"RSA{stg}", [64, 512], F32))
                RSB = asx.enter_context(nc.sbuf_tensor(f"RSB{stg}", [64, 512], F32))
                VSP = asx.enter_context(nc.sbuf_tensor(f"VSP{stg}", [128, 2 * 512], BF16))
                OSB = asx.enter_context(nc.sbuf_tensor(f"OSB{stg}", [128, 2 * 512], BF16))
                PQKV = asx.enter_context(nc.psum_tensor(f"pqkv{stg}", [128, 1024], F32))
                PSS2 = asx.enter_context(nc.psum_tensor(f"psS{stg}", [128, 1024], F32))
                PSO = asx.enter_context(nc.psum_tensor(f"psO{stg}", [128, 512], F32))
                PREC = asx.enter_context(nc.psum_tensor(f"prec{stg}", [128, 512], F32))
                CS1 = asx.enter_context(nc.psum_tensor(f"cs1{stg}", [1, 512], F32))

                def f_gp(g):
                    kb.wait(g, experts_done)
                    kb.dma(g, "dg", out=WQKV[:, :].rearrange("p (dc o) -> p dc o", o=384),
                           in_=d_wqkvT[:, :].rearrange("(dc p) o -> p dc o", p=128))
                kb.gp(f_gp)
                tok_wqkv = kb.tok("dg")

                def f_sy(sy):
                    kb.wait(sy, experts_done)
                    kb.dma(sy, "dq", out=CSN[0:64, :], in_=d_cos[:, :])
                    kb.dma(sy, "dq", out=CSN[64:128, :], in_=d_sin[:, :])
                    kb.dma(sy, "dq", out=CMASK[:, :].rearrange("p (m t) -> p m t", t=512),
                           in_=d_cmask[:, :, :].rearrange("m p t -> p m t"))
                kb.sy(f_sy)
                tok_cm = kb.tok("dq")

                ret_toks = []
                for b in range(B):
                    hl_toks = [None, None]
                    first_mm = (b == 0)
                    proj_ev = [None, None]
                    vsp_toks = [None, None]
                    ppar = 0
                    for tb in range(16):
                        lpar = tb % 2
                        gtok = b * 8192 + tb * 512
                        r = gtok // TL
                        tcol = gtok % TL

                        def f_sy(sy, lpar=lpar, r=r, tcol=tcol, tb=tb,
                                 agw=(tok_ag if tb == 0 and b == 0 else None),
                                 cons=hl_toks[lpar]):
                            if agw is not None:
                                kb.wait(sy, agw)
                            kb.wait(sy, cons)
                            src = ag_out[:, :].rearrange("(r c p) t -> p r c t", p=128, c=NCH)[:, r, :, tcol:tcol + 512]
                            kb.dma(sy, "dq",
                                   out=HL[:, lpar * 4096:(lpar + 1) * 4096].rearrange(
                                       "p (c t) -> p c t", t=512),
                                   in_=src)
                        kb.sy(f_sy)
                        tok_hl = kb.tok("dq")

                        for m in (2, 0, 1):
                            def f_te(te, m=m, lpar=lpar, ppar=ppar, tok=tok_hl,
                                     w8=(tok_wqkv if first_mm else None), evp=proj_ev[ppar]):
                                if w8 is not None:
                                    kb.wait(te, w8)
                                kb.wait(te, tok)
                                kb.wait(te, evp)
                                for dc in range(NCH):
                                    mi = te.matmul(
                                        PQKV[:, ppar * 512:ppar * 512 + 512],
                                        WQKV[:, dc * 384 + m * 128: dc * 384 + m * 128 + 128],
                                        HL[:, lpar * 4096 + dc * 512: lpar * 4096 + dc * 512 + 512],
                                        start=(dc == 0), stop=(dc == NCH - 1))
                                kb.tag(mi, "mm")
                            kb.te(f_te)
                            tok_pm = kb.tok("mm")
                            first_mm = False

                            pos0 = tb * 512
                            if m == 2:
                                vpar = tb % 2

                                def f_ve(v, ppar=ppar, vpar=vpar, tok=tok_pm, vf=vsp_toks[vpar]):
                                    kb.wait(v, tok)
                                    kb.wait(v, vf)
                                    kb.tag(v.tensor_copy(VSP[:, vpar * 512:(vpar + 1) * 512],
                                                         PQKV[:, ppar * 512:ppar * 512 + 512]), "ev")
                                kb.ve(f_ve)
                                tok_vs = kb.tok("ev")

                                def f_sy(sy, pos0=pos0, vpar=vpar, tok=tok_vs):
                                    kb.wait(sy, tok)
                                    kb.dma(sy, "dq", out=d_vsp[:, pos0:pos0 + 512],
                                           in_=VSP[:, vpar * 512:(vpar + 1) * 512])
                                kb.sy(f_sy)
                                vsp_toks[vpar] = kb.tok("dq")
                                proj_ev[ppar] = tok_vs
                            else:
                                dst = [QS, KS][m]

                                def f_ve(v, m=m, ppar=ppar, dst=dst, pos0=pos0, tok=tok_pm,
                                         cm=(tok_cm if b == 0 and tb == 0 else None)):
                                    kb.wait(v, tok)
                                    kb.wait(v, cm)
                                    ps = PQKV[:, ppar * 512:ppar * 512 + 512]
                                    o = dst[:, pos0:pos0 + 512]
                                    cosb = CSN[0:64, pos0:pos0 + 512]
                                    sinb = CSN[64:128, pos0:pos0 + 512]
                                    pl = ps[0:64, :]
                                    ph = ps[64:128, :]
                                    v.tensor_mul(RSA[:, :], ph, sinb)
                                    v.tensor_mul(RSB[:, :], pl, cosb)
                                    v.drain()
                                    v.tensor_sub(o[0:64, :], RSB[:, :], RSA[:, :])
                                    v.tensor_mul(RSA[:, :], pl, sinb)
                                    v.tensor_mul(RSB[:, :], ph, cosb)
                                    v.drain()
                                    ti = v.tensor_add(o[64:128, :], RSA[:, :], RSB[:, :])
                                    kb.tag(ti, "ev")
                                kb.ve(f_ve)
                                proj_ev[ppar] = kb.tok("ev")
                            ppar ^= 1
                        hl_toks[lpar] = proj_ev[(ppar + 1) % 2]
                    tok_proj = kb.tok("ev")
                    tok_vlast = vsp_toks[1]

                    def f_sy(sy, t1=vsp_toks[0], t2=vsp_toks[1]):
                        kb.wait(sy, t1)
                        kb.wait(sy, t2)
                        for kc in range(64):
                            kb.dma(sy, "dq", out=VT[:, kc * 128:(kc + 1) * 128],
                                   in_=d_vsp[:, kc * 128:(kc + 1) * 128], transpose=True)
                    kb.sy(f_sy)
                    tok_vt = kb.tok("dq")

                    exp_par = [None, None]
                    u_cons = [None, None]
                    osb_toks = [None, None]
                    last_ret = None
                    prev_cs1 = None
                    prev_os = None
                    for qb in range(16):
                        kmax = 4 * (qb + 1)
                        av_tok = None
                        for kc in range(kmax):
                            spar = kc % 2
                            diag = kc >= 4 * qb

                            def f_te(te, qb=qb, kc=kc, spar=spar,
                                     vt=(tok_vt if qb == 0 and kc == 0 else None),
                                     ef=exp_par[spar]):
                                if vt is not None:
                                    kb.wait(te, vt)
                                kb.wait(te, ef)
                                kb.tag(te.matmul(PSS2[:, spar * 512:spar * 512 + 512],
                                                 KS[:, kc * 128:(kc + 1) * 128],
                                                 QS[:, qb * 512:qb * 512 + 512],
                                                 start=True, stop=True), "mm")
                            kb.te(f_te)
                            tok_s = kb.tok("mm")

                            def f_sc(scx, spar=spar, tok=tok_s, uc=u_cons[spar]):
                                kb.wait(scx, tok)
                                kb.wait(scx, uc)
                                kb.tag(scx.activation(UB[:, spar * 512:(spar + 1) * 512],
                                                      PSS2[:, spar * 512:spar * 512 + 512],
                                                      AF.Exp, scale=1.0 / math.sqrt(128.0)), "ac")
                            kb.sc(f_sc)
                            tok_exp = kb.tok("ac")
                            exp_par[spar] = tok_exp

                            m_off = kc - 4 * qb

                            def f_ve(v, qb=qb, kc=kc, spar=spar, diag=diag, m_off=m_off,
                                     tok=tok_exp, pc1=(prev_cs1 if kc == 0 else None)):
                                kb.wait(v, tok)
                                kb.wait(v, pc1)
                                u = UB[:, spar * 512:(spar + 1) * 512]
                                if diag:
                                    v.tensor_mul(u, u, CMASK[:, m_off * 512:(m_off + 1) * 512])
                                    v.drain()
                                if kc == 0:
                                    ti = v.tensor_copy(AU[:, :], u)
                                else:
                                    ti = v.tensor_add(AU[:, :], AU[:, :], u)
                                v.drain()
                                kb.tag(ti, "ev")
                            kb.ve(f_ve)
                            tok_u = kb.tok("ev")

                            def f_te(te, qb=qb, kc=kc, spar=spar, kmax=kmax, tok=tok_u,
                                     pos=(prev_os if kc == 0 else None)):
                                kb.wait(te, tok)
                                kb.wait(te, pos)
                                kb.tag(te.matmul(PSO[:, :],
                                                 VT[:, kc * 128:(kc + 1) * 128],
                                                 UB[:, spar * 512:(spar + 1) * 512],
                                                 start=(kc == 0), stop=(kc == kmax - 1)), "mm")
                            kb.te(f_te)
                            av_tok = kb.tok("mm")
                            u_cons[spar] = av_tok

                        au_last = kb.tok("ev")

                        def f_te(te, tok=au_last):
                            kb.wait(te, tok)
                            kb.tag(te.matmul(CS1[0:1, :], ONESB[:, 0:1], AU[:, :],
                                             start=True, stop=True), "mm")
                        kb.te(f_te)
                        tok_cs1 = kb.tok("mm")
                        prev_cs1 = tok_cs1

                        def f_ve(v, tok=tok_cs1):
                            kb.wait(v, tok)
                            kb.tag(v.reciprocal(CSR[0:1, :], CS1[0:1, :]), "ev")
                        kb.ve(f_ve)
                        tok_cs = kb.tok("ev")

                        def f_te(te, tok=tok_cs):
                            kb.wait(te, tok)
                            kb.tag(te.matmul(PREC[:, :], ONESROWF[:, :], CSR[0:1, :],
                                             start=True, stop=True), "mm")
                        kb.te(f_te)
                        tok_rec = kb.tok("mm")

                        opar = qb % 2

                        def f_ve(v, qb=qb, opar=opar, tokr=tok_rec, toka=av_tok, of=osb_toks[opar]):
                            kb.wait(v, tokr)
                            kb.wait(v, toka)
                            kb.wait(v, of)
                            v.tensor_copy(RECS[:, :], PREC[:, :])
                            v.drain()
                            kb.tag(v.tensor_mul(OSB[:, opar * 512:(opar + 1) * 512],
                                                PSO[:, :], RECS[:, :]), "ev")
                        kb.ve(f_ve)
                        tok_os = kb.tok("ev")
                        prev_os = tok_os

                        def f_sy(sy, b=b, qb=qb, opar=opar, tok=tok_os):
                            kb.wait(sy, tok)
                            kb.dma(sy, "dq",
                                   out=ret_in[:, b * S + qb * 512: b * S + qb * 512 + 512],
                                   in_=OSB[:, opar * 512:(opar + 1) * 512])
                        kb.sy(f_sy)
                        osb_toks[opar] = kb.tok("dq")
                        last_ret = osb_toks[opar]
                    ret_toks.append(last_ret)

                def f_gp(g, t1=ret_toks[0], t2=ret_toks[1]):
                    kb.wait(g, t1)
                    kb.wait(g, t2)
                    kb.tag(g.collective_compute(
                        "AllGather", ALU.bypass, ins=[ret_in[:, :]], outs=[ret_out[:, :]],
                        replica_groups=[list(range(NC))]), "cc")
                kb.gp(f_gp)
                tok_ret = kb.tok("cc")

            if SUB <= 4:
                return
            if DBG:
                def f_dbg(sy):
                    sy.wait_ge(sems["ev"], C["ev"])
                    sy.wait_ge(sems["cc"], C["cc"])
                    kb.dma(sy, "dq", out=d_dbg_os[:, :], in_=ret_in[:, :])
                kb.sy(f_dbg)

            barrier()

            with contextlib.ExitStack() as wsx:
                WOT = wsx.enter_context(nc.sbuf_tensor(f"WOT{stg}", [128, 16 * 1024], BF16))
                ATT = wsx.enter_context(nc.sbuf_tensor(f"ATT{stg}", [128, 2 * 8 * 512], BF16))
                WSC = wsx.enter_context(nc.sbuf_tensor(f"WSC{stg}", [128, 2 * 512], F32))
                FFB = wsx.enter_context(nc.sbuf_tensor(f"FFB{stg}", [128, 2 * 8 * 512], BF16))
                PWO = wsx.enter_context(nc.psum_tensor(f"pwo{stg}", [128, 1024], F32))

                def f_gp(g):
                    kb.dma(g, "dg", out=WOT[:, :].rearrange("p (jc o) -> p jc o", o=1024),
                           in_=d_woutT[:, :].rearrange("(jc p) o -> p jc o", p=128))
                kb.gp(f_gp)
                tok_wot = kb.tok("dg")

                att_toks = [None, None]
                wpar = 0
                wo_ev = [None, None]
                for tb in range(4):
                    lpar = tb % 2

                    def f_sy(sy, tb=tb, lpar=lpar, tok=(tok_ret if tb == 0 else None),
                             cons=att_toks[lpar]):
                        if tok is not None:
                            kb.wait(sy, tok)
                        kb.wait(sy, cons)
                        pid = kb.pid(sy)
                        src = ret_out[:, :].rearrange("(jc p) t -> p jc t", p=128)
                        kb.dma(sy, "dq",
                               out=ATT[:, lpar * 4096:(lpar + 1) * 4096].rearrange(
                                   "p (jc t) -> p jc t", t=512),
                               in_=src[:, :, bass.ts(pid, TL)][:, :, tb * 512:tb * 512 + 512])
                        kb.dma(sy, "dq",
                               out=FFB[:, lpar * 4096:(lpar + 1) * 4096].rearrange(
                                   "p (jc t) -> p jc t", t=512),
                               in_=d_ffo[:, :].rearrange("(jc p) t -> p jc t", p=128)[:, :, tb * 512:tb * 512 + 512])
                    kb.sy(f_sy)
                    tok_att = kb.tok("dq")

                    for oc in range(NCH):
                        def f_te(te, tb=tb, oc=oc, lpar=lpar, wpar=wpar, tok=tok_att,
                                 w8=(tok_wot if tb == 0 and oc == 0 else None),
                                 evp=wo_ev[wpar]):
                            if w8 is not None:
                                kb.wait(te, w8)
                            kb.wait(te, tok)
                            kb.wait(te, evp)
                            for jc in range(16):
                                if jc < 8:
                                    rhs = ATT[:, lpar * 4096 + jc * 512: lpar * 4096 + jc * 512 + 512]
                                else:
                                    rhs = FFB[:, lpar * 4096 + (jc - 8) * 512: lpar * 4096 + (jc - 8) * 512 + 512]
                                mi = te.matmul(PWO[:, wpar * 512:wpar * 512 + 512],
                                               WOT[:, jc * 1024 + oc * 128: jc * 1024 + oc * 128 + 128],
                                               rhs, start=(jc == 0), stop=(jc == 15))
                            kb.tag(mi, "mm")
                        kb.te(f_te)
                        tok_wo = kb.tok("mm")

                        def f_sc(sc, wpar=wpar, tok=tok_wo):
                            kb.wait(sc, tok)
                            kb.tag(sc.activation(WSC[:, wpar * 512:wpar * 512 + 512],
                                                 PWO[:, wpar * 512:wpar * 512 + 512],
                                                 AF.Copy, scale=0.5), "ac")
                        kb.sc(f_sc)
                        tok_wsc = kb.tok("ac")

                        def f_ve(v, tb=tb, oc=oc, wpar=wpar, tok=tok_wsc):
                            kb.wait(v, tok)
                            xsl = X[:, oc * TL + tb * 512: oc * TL + tb * 512 + 512]
                            kb.tag(v.tensor_add(xsl, xsl,
                                                WSC[:, wpar * 512:wpar * 512 + 512]), "ev")
                        kb.ve(f_ve)
                        wo_ev[wpar] = kb.tok("ev")
                        wpar ^= 1
                    att_toks[lpar] = wo_ev[(wpar + 1) % 2]

            barrier()


        def emit_stage_l0attn(exi):
            ag_in, ag_out = d_ag_in[exi], d_ag_out[exi]
            ret_in, ret_out = d_ret_in[exi], d_ret_out[exi]

            with contextlib.ExitStack() as s2x:
                WQ2 = s2x.enter_context(nc.sbuf_tensor("WQ2", [128, NCH * 384], BF16))

                def f_gp(g):
                    kb.dma(g, "dg", out=WQ2[:, :].rearrange("p (dc o) -> p dc o", o=384),
                           in_=d_wqkv2[:, :].rearrange("(dc p) o -> p dc o", p=128))
                kb.gp(f_gp)
                tok_w2 = kb.tok("dg")

                stx0 = contextlib.ExitStack()
                HB = stx0.enter_context(nc.sbuf_tensor("HBs2", [128, NCH * TL], BF16))
                h_done = emit_ln("s2", HB)

                def f_sy(sy):
                    kb.wait(sy, h_done)
                    kb.dma(sy, "dq", out=ag_in[:, :].rearrange("(c p) t -> p c t", p=128),
                           in_=HB[:, :].rearrange("p (c t) -> p c t", t=TL))
                kb.sy(f_sy)
                tok_hstaged = kb.tok("dq")
                stx0.close()

                def f_gp(g):
                    kb.wait(g, tok_hstaged)
                    kb.tag(g.collective_compute(
                        "AllGather", ALU.bypass, ins=[ag_in[:, :]], outs=[ag_out[:, :]],
                        replica_groups=[list(range(NC))]), "cc")
                kb.gp(f_gp)
                tok_ag = kb.tok("cc")

                with contextlib.ExitStack() as a2x:
                    QSTK = a2x.enter_context(nc.sbuf_tensor("QSTK", [128, S], BF16))
                    KSTK = a2x.enter_context(nc.sbuf_tensor("KSTK", [128, S], BF16))
                    VT2 = a2x.enter_context(nc.sbuf_tensor("VT2", [128, S], BF16))
                    HL2 = a2x.enter_context(nc.sbuf_tensor("HL2", [128, 2 * 8 * 512], BF16))
                    UB2 = a2x.enter_context(nc.sbuf_tensor("UB2", [128, 2 * 512], BF16))
                    OS2 = a2x.enter_context(nc.sbuf_tensor("OS2", [128, 2 * 512], BF16))
                    SPB = a2x.enter_context(nc.sbuf_tensor("SPB", [128, 2 * 512], BF16))
                    SPV = a2x.enter_context(nc.sbuf_tensor("SPV", [128, 2 * 512], BF16))
                    SV1 = a2x.enter_context(nc.sbuf_tensor("SV1", [128, 8], F32))
                    SVT = a2x.enter_context(nc.sbuf_tensor("SVT", [128, 8], F32))
                    PQ2 = a2x.enter_context(nc.psum_tensor("pq2", [128, 1024], F32))
                    PSD = a2x.enter_context(nc.psum_tensor("psd2", [128, 1024], F32))
                    PSO2 = a2x.enter_context(nc.psum_tensor("pso2", [128, 512], F32))

                    def f_ve(v):
                        kb.tag(v.memset(SV1[:, :], 0.0), "ev")
                    kb.ve(f_ve)
                    tok_sv0 = kb.tok("ev")

                    hl_toks = [None, None]
                    proj_ev = [None, None]
                    spb_toks = [None, None]
                    spv_toks = [None, None]
                    ppar = 0
                    first_mm = True
                    for tb in range(32):
                        lpar = tb % 2
                        b = tb // 16
                        gtok = tb * 512
                        r = gtok // TL
                        tcol = gtok % TL
                        pos0 = gtok - b * 8192

                        def f_sy(sy, lpar=lpar, r=r, tcol=tcol,
                                 agw=(tok_ag if tb == 0 else None), cons=hl_toks[lpar]):
                            if agw is not None:
                                kb.wait(sy, agw)
                            kb.wait(sy, cons)
                            src = ag_out[:, :].rearrange("(r c p) t -> p r c t", p=128, c=NCH)[:, r, :, tcol:tcol + 512]
                            kb.dma(sy, "dq",
                                   out=HL2[:, lpar * 4096:(lpar + 1) * 4096].rearrange(
                                       "p (c t) -> p c t", t=512),
                                   in_=src)
                        kb.sy(f_sy)
                        tok_hl = kb.tok("dq")

                        for mi_ in range(3):
                            def f_te(te, mi_=mi_, lpar=lpar, ppar=ppar, tok=tok_hl,
                                     w8=(tok_w2 if first_mm else None), evp=proj_ev[ppar]):
                                if w8 is not None:
                                    kb.wait(te, w8)
                                kb.wait(te, tok)
                                kb.wait(te, evp)
                                for dc in range(NCH):
                                    mm_ = te.matmul(
                                        PQ2[:, ppar * 512:ppar * 512 + 512],
                                        WQ2[:, dc * 384 + mi_ * 128: dc * 384 + mi_ * 128 + 128],
                                        HL2[:, lpar * 4096 + dc * 512: lpar * 4096 + dc * 512 + 512],
                                        start=(dc == 0), stop=(dc == NCH - 1))
                                kb.tag(mm_, "mm")
                            kb.te(f_te)
                            tok_pm = kb.tok("mm")
                            first_mm = False

                            if mi_ == 2:
                                vpar = tb % 2

                                def f_ve(v, ppar=ppar, vpar=vpar, b=b, tok=tok_pm,
                                         vf=spv_toks[vpar], sv=(tok_sv0 if tb == 0 else None)):
                                    kb.wait(v, tok)
                                    kb.wait(v, vf)
                                    kb.wait(v, sv)
                                    ps = PQ2[:, ppar * 512:ppar * 512 + 512]
                                    v.tensor_copy(SPV[:, vpar * 512:(vpar + 1) * 512], ps)
                                    if b == 1:
                                        v.reduce_sum(SVT[0:64, 0:1], ps[0:64, :], axis=AX.X)
                                        v.reduce_sum(SVT[64:128, 1:2], ps[64:128, :], axis=AX.X)
                                        v.drain()
                                        v.tensor_copy(SVT[64:128, 2:3], SVT[0:64, 0:1])
                                        v.tensor_add(SV1[64:128, 0:1], SV1[64:128, 0:1],
                                                     SVT[64:128, 2:3])
                                        ti = v.tensor_add(SV1[64:128, 1:2], SV1[64:128, 1:2],
                                                          SVT[64:128, 1:2])
                                        kb.tag(ti, "ev")
                                    else:
                                        kb.tag(v.engine_nop(), "ev")
                                kb.ve(f_ve)
                                tok_vs = kb.tok("ev")

                                def f_sy(sy, gtok=gtok, vpar=vpar, tok=tok_vs):
                                    kb.wait(sy, tok)
                                    kb.dma(sy, "dq", out=d_vsp2[:, gtok:gtok + 512],
                                           in_=SPV[:, vpar * 512:(vpar + 1) * 512])
                                kb.sy(f_sy)
                                spv_toks[vpar] = kb.tok("dq")
                                proj_ev[ppar] = tok_vs
                            else:
                                dst = QSTK if mi_ == 0 else KSTK
                                neg = (mi_ == 0 and b == 1)
                                bpar = tb % 2

                                def f_sc(sc, ppar=ppar, b=b, pos0=pos0, dst=dst, neg=neg,
                                         tok=tok_pm):
                                    kb.wait(sc, tok)
                                    ps = PQ2[:, ppar * 512:ppar * 512 + 512]
                                    kb.tag(sc.activation(
                                        dst[b * 64:(b + 1) * 64, pos0:pos0 + 512],
                                        ps[0:64, :], AF.Copy,
                                        scale=(-1.0 if neg else 1.0)), "ac")
                                kb.sc(f_sc)
                                tok_sta = kb.tok("ac")

                                def f_ve(v, ppar=ppar, mi_=mi_, b=b, neg=neg, bpar=bpar,
                                         tok=tok_pm, bf=spb_toks[bpar]):
                                    kb.wait(v, tok)
                                    kb.wait(v, bf)
                                    ps = PQ2[:, ppar * 512:ppar * 512 + 512]
                                    kb.tag(v.tensor_copy(SPB[0:64, bpar * 512:bpar * 512 + 512],
                                                         ps[64:128, :]), "ev")
                                kb.ve(f_ve)
                                tok_spb = kb.tok("ev")

                                def f_sy(sy, mi_=mi_, gtok=gtok, bpar=bpar, tok=tok_spb,
                                         ts=tok_sta):
                                    kb.wait(sy, tok)
                                    kb.wait(sy, ts)
                                    kb.dma(sy, "dq",
                                           out=d_qkvb[mi_ * 64:(mi_ + 1) * 64, gtok:gtok + 512],
                                           in_=SPB[0:64, bpar * 512:bpar * 512 + 512])
                                kb.sy(f_sy)
                                spb_toks[bpar] = kb.tok("dq")
                                proj_ev[ppar] = spb_toks[bpar]
                            ppar ^= 1
                        hl_toks[lpar] = proj_ev[(ppar + 1) % 2]
                    proj_done = kb.tok("dq")

                    ret_last = None
                    for hh2 in range(2):
                        if hh2 == 1:
                            def f_sy(sy, tok=ret_last):
                                kb.wait(sy, tok)
                                for b in range(2):
                                    kb.dma(sy, "dq", out=QSTK[b * 64:(b + 1) * 64, :],
                                           in_=d_qkvb[0:64, b * S:(b + 1) * S])
                                    kb.dma(sy, "dq", out=KSTK[b * 64:(b + 1) * 64, :],
                                           in_=d_qkvb[64:128, b * S:(b + 1) * S])
                            kb.sy(f_sy)
                            tok_stkb = kb.tok("dq")

                            def f_sc(sc, tok=tok_stkb):
                                kb.wait(sc, tok)
                                kb.tag(sc.activation(QSTK[64:128, :], QSTK[64:128, :],
                                                     AF.Copy, scale=-1.0), "ac")
                            kb.sc(f_sc)
                            tok_stk = kb.tok("ac")
                        else:
                            tok_stk = proj_done

                        def f_sy(sy, hh2=hh2, tok=(spv_toks[0] if hh2 == 0 else None),
                                 tok2=(spv_toks[1] if hh2 == 0 else None), pu=ret_last):
                            kb.wait(sy, tok)
                            kb.wait(sy, tok2)
                            kb.wait(sy, pu)
                            for kc in range(64):
                                kb.dma(sy, "dq", out=VT2[:, kc * 128:kc * 128 + 64],
                                       in_=d_vsp2[hh2 * 64:(hh2 + 1) * 64, kc * 128:(kc + 1) * 128],
                                       transpose=True)
                                kb.dma(sy, "dq", out=VT2[:, kc * 128 + 64:kc * 128 + 128],
                                       in_=d_vsp2[hh2 * 64:(hh2 + 1) * 64, S + kc * 128:S + (kc + 1) * 128],
                                       transpose=True)
                        kb.sy(f_sy)
                        tok_vt = kb.tok("dq")

                        exp_par = [None, None]
                        u_cons = [None, None]
                        os_par = [None, None]
                        prev_os = None
                        for qb in range(16):
                            av_tok = None
                            for kc in range(64):
                                spar = kc % 2

                                def f_te(te, qb=qb, kc=kc, spar=spar,
                                         vt=(tok_vt if qb == 0 and kc == 0 else None),
                                         sk=(tok_stk if qb == 0 and kc == 0 else None),
                                         ef=exp_par[spar]):
                                    kb.wait(te, vt)
                                    kb.wait(te, sk)
                                    kb.wait(te, ef)
                                    kb.tag(te.matmul(PSD[:, spar * 512:spar * 512 + 512],
                                                     KSTK[:, kc * 128:(kc + 1) * 128],
                                                     QSTK[:, qb * 512:qb * 512 + 512],
                                                     start=True, stop=True), "mm")
                                kb.te(f_te)
                                tok_s = kb.tok("mm")

                                def f_sc(sc, spar=spar, tok=tok_s, uc=u_cons[spar]):
                                    kb.wait(sc, tok)
                                    kb.wait(sc, uc)
                                    kb.tag(sc.activation(UB2[:, spar * 512:(spar + 1) * 512],
                                                         PSD[:, spar * 512:spar * 512 + 512],
                                                         AF.Sigmoid, scale=1.0 / 32.0), "ac")
                                kb.sc(f_sc)
                                tok_p = kb.tok("ac")
                                exp_par[spar] = tok_p

                                def f_te(te, qb=qb, kc=kc, spar=spar, tok=tok_p,
                                         pos=(prev_os if kc == 0 else None)):
                                    kb.wait(te, tok)
                                    kb.wait(te, pos)
                                    kb.tag(te.matmul(PSO2[:, :],
                                                     VT2[:, kc * 128:(kc + 1) * 128],
                                                     UB2[:, spar * 512:(spar + 1) * 512],
                                                     start=(kc == 0), stop=(kc == 63)), "mm")
                                kb.te(f_te)
                                av_tok = kb.tok("mm")
                                u_cons[spar] = av_tok

                            opar = qb % 2

                            def f_ve(v, opar=opar, tok=av_tok, of=os_par[opar]):
                                kb.wait(v, tok)
                                kb.wait(v, of)
                                kb.tag(v.tensor_copy(OS2[0:64, opar * 512:opar * 512 + 512],
                                                     PSO2[0:64, :]), "ev")
                            kb.ve(f_ve)
                            tok_v0 = kb.tok("ev")

                            def f_sc(sc, hh2=hh2, opar=opar, tok=av_tok, of=os_par[opar]):
                                kb.wait(sc, tok)
                                kb.wait(sc, of)
                                kb.tag(sc.activation(OS2[64:128, opar * 512:opar * 512 + 512],
                                                     PSO2[64:128, :], AF.Identity,
                                                     scale=-1.0,
                                                     bias=SV1[64:128, hh2:hh2 + 1]), "ac")
                            kb.sc(f_sc)
                            tok_v1 = kb.tok("ac")

                            def f_sy(sy, hh2=hh2, qb=qb, opar=opar, t0=tok_v0, t1=tok_v1):
                                kb.wait(sy, t0)
                                kb.wait(sy, t1)
                                kb.dma(sy, "dq",
                                       out=ret_in[hh2 * 64:(hh2 + 1) * 64, qb * 512:qb * 512 + 512],
                                       in_=OS2[0:64, opar * 512:opar * 512 + 512])
                                kb.dma(sy, "dq",
                                       out=ret_in[hh2 * 64:(hh2 + 1) * 64, S + qb * 512:S + qb * 512 + 512],
                                       in_=OS2[64:128, opar * 512:opar * 512 + 512])
                            kb.sy(f_sy)
                            os_par[opar] = kb.tok("dq")
                            prev_os = os_par[opar]
                        ret_last = prev_os

                    def f_gp(g, tok=ret_last):
                        kb.wait(g, tok)
                        kb.tag(g.collective_compute(
                            "AllGather", ALU.bypass, ins=[ret_in[:, :]], outs=[ret_out[:, :]],
                            replica_groups=[list(range(NC))]), "cc")
                    kb.gp(f_gp)
                    tok_ret = kb.tok("cc")

            barrier()

            with contextlib.ExitStack() as w2x:
                WO2 = w2x.enter_context(nc.sbuf_tensor("WO2", [128, 8 * 1024], BF16))
                ATT2 = w2x.enter_context(nc.sbuf_tensor("ATT2", [128, 2 * 8 * 512], BF16))
                PWO2 = w2x.enter_context(nc.psum_tensor("pwo2", [128, 1024], F32))

                def f_gp(g):
                    kb.dma(g, "dg", out=WO2[:, :].rearrange("p (jc o) -> p jc o", o=1024),
                           in_=d_wo0[:, :].rearrange("(jc p) o -> p jc o", p=128))
                kb.gp(f_gp)
                tok_wo2 = kb.tok("dg")

                att_toks = [None, None]
                wpar = 0
                wo_ev = [None, None]
                for tb in range(4):
                    lpar = tb % 2

                    def f_sy(sy, tb=tb, lpar=lpar, tok=(tok_ret if tb == 0 else None),
                             cons=att_toks[lpar]):
                        kb.wait(sy, tok)
                        kb.wait(sy, cons)
                        pid = kb.pid(sy)
                        src = ret_out[:, :].rearrange("(jc p) t -> p jc t", p=128)
                        kb.dma(sy, "dq",
                               out=ATT2[:, lpar * 4096:(lpar + 1) * 4096].rearrange(
                                   "p (jc t) -> p jc t", t=512),
                               in_=src[:, :, bass.ts(pid, TL)][:, :, tb * 512:tb * 512 + 512])
                    kb.sy(f_sy)
                    tok_att = kb.tok("dq")

                    for oc in range(NCH):
                        def f_te(te, tb=tb, oc=oc, lpar=lpar, wpar=wpar, tok=tok_att,
                                 w8=(tok_wo2 if tb == 0 and oc == 0 else None),
                                 evp=wo_ev[wpar]):
                            kb.wait(te, w8)
                            kb.wait(te, tok)
                            kb.wait(te, evp)
                            for jc in range(8):
                                mm_ = te.matmul(
                                    PWO2[:, wpar * 512:wpar * 512 + 512],
                                    WO2[:, jc * 1024 + oc * 128: jc * 1024 + oc * 128 + 128],
                                    ATT2[:, lpar * 4096 + jc * 512: lpar * 4096 + jc * 512 + 512],
                                    start=(jc == 0), stop=(jc == 7))
                            kb.tag(mm_, "mm")
                        kb.te(f_te)
                        tok_wo = kb.tok("mm")

                        def f_ve(v, tb=tb, oc=oc, wpar=wpar, tok=tok_wo):
                            kb.wait(v, tok)
                            xsl = X[:, oc * TL + tb * 512: oc * TL + tb * 512 + 512]
                            kb.tag(v.tensor_add(xsl, xsl,
                                                PWO2[:, wpar * 512:wpar * 512 + 512]), "ev")
                        kb.ve(f_ve)
                        wo_ev[wpar] = kb.tok("ev")
                        wpar ^= 1
                    att_toks[lpar] = wo_ev[(wpar + 1) % 2]

            barrier()

        def emit_stage_l1ffn(stg):
            with contextlib.ExitStack() as fx:
                W1Q = fx.enter_context(nc.sbuf_tensor(f"W1Q{stg}", [128, 8 * 4096], F8))
                W2Q = fx.enter_context(nc.sbuf_tensor(f"W2Q{stg}", [128, 32 * 1024], F8))
                BQF = fx.enter_context(nc.sbuf_tensor(f"BQF{stg}", [128, 48], F32))
                BQ8 = fx.enter_context(nc.sbuf_tensor(f"BQ8{stg}", [128, 48], F8))
                X8 = fx.enter_context(nc.sbuf_tensor(f"X8{stg}", [128, NCH * TL], F8))

                stx0 = contextlib.ExitStack()
                WST = stx0.enter_context(nc.sbuf_tensor(f"WST{stg}", [128, 2 * 2048], F32))

                def f_sy(sy):
                    kb.dma(sy, "dq", out=BQF[:, 0:32], in_=d_b1c[:, :])
                    kb.dma(sy, "dq", out=BQF[:, 32:40], in_=d_b2c[:, :])
                kb.sy(f_sy)
                tok_bld = kb.tok("dq")

                def f_ve(v, tok=tok_bld):
                    kb.wait(v, tok)
                    v.tensor_copy(BQ8[:, 0:40], BQF[:, 0:40])
                    v.drain()
                    kb.tag(v.tensor_copy(BQF[:, 0:40], BQ8[:, 0:40]), "ev")
                kb.ve(f_ve)
                tok_bq = kb.tok("ev")

                wld = [None, None]
                wcast = [None, None]
                chunks = [("w1", dc, hf) for dc in range(8) for hf in range(2)]
                chunks += [("w2", g4, 0) for g4 in range(16)]
                for ci, (kind, i0, i1) in enumerate(chunks):
                    par = ci % 2

                    def f_gp(g, kind=kind, i0=i0, i1=i1, par=par, cons=wcast[par]):
                        kb.wait(g, cons)
                        if kind == "w1":
                            kb.dma(g, "dg", out=WST[:, par * 2048:(par + 1) * 2048],
                                   in_=d_w1T[i0 * 128:(i0 + 1) * 128,
                                             i1 * 2048:(i1 + 1) * 2048])
                        else:
                            kb.dma(g, "dg",
                                   out=WST[:, par * 2048:(par + 1) * 2048].rearrange(
                                       "p (c o) -> p c o", o=1024),
                                   in_=d_w2T[i0 * 256:(i0 + 1) * 256, :].rearrange(
                                       "(c p) o -> p c o", p=128))
                    kb.gp(f_gp)
                    wld[par] = kb.tok("dg")

                    def f_ve(v, kind=kind, i0=i0, i1=i1, par=par, tok=wld[par]):
                        kb.wait(v, tok)
                        if kind == "w1":
                            dst = W1Q[:, i0 * 4096 + i1 * 2048: i0 * 4096 + (i1 + 1) * 2048]
                        else:
                            dst = W2Q[:, i0 * 2048:(i0 + 1) * 2048]
                        kb.tag(v.tensor_copy(dst, WST[:, par * 2048:(par + 1) * 2048]), "ev")
                    kb.ve(f_ve)
                    wcast[par] = kb.tok("ev")
                tok_wq = kb.tok("ev")

                h_done = emit_ln(f"s{stg}", X8)
                stx0.close()

                with contextlib.ExitStack() as ex:
                    H8 = ex.enter_context(nc.sbuf_tensor(f"H8{stg}", [128, 32 * 512], F8))
                    H16 = ex.enter_context(nc.sbuf_tensor(f"H16{stg}", [128, 2 * 512], F16))
                    Y16 = ex.enter_context(nc.sbuf_tensor(f"Y16{stg}", [128, 2 * 512], F16))
                    WSC = ex.enter_context(nc.sbuf_tensor(f"WSCq{stg}", [128, 2 * 512], F32))
                    MSKT = ex.enter_context(nc.sbuf_tensor(f"MSKT{stg}", [128, 8 * 512], BF16))
                    PH = ex.enter_context(nc.psum_tensor(f"ph{stg}", [128, 1024], F32))
                    PY = ex.enter_context(nc.psum_tensor(f"py{stg}", [128, 1024], F32))

                    t_h16 = [None, None]
                    t_h8w = [None, None]
                    t_yact = [None, None]
                    t_ymul = [None, None]
                    t_yadd = [None, None]
                    t_msk = None
                    t_ymm_all = None
                    for tb in range(4):
                        def f_sy(sy, tb=tb, cons=(t_ymul[1] if tb else None),
                                 wq=(tok_wq if tb == 0 else None),
                                 hd=(h_done if tb == 0 else None)):
                            kb.wait(sy, cons)
                            kb.wait(sy, wq)
                            kb.wait(sy, hd)
                            kb.dma(sy, "dq",
                                   out=MSKT[:, :].rearrange("p (c t) -> p c t", t=512),
                                   in_=d_mskT[:, :].rearrange("(c p) t -> p c t", p=128)[
                                       :, :, tb * 512:(tb + 1) * 512])
                        kb.sy(f_sy)
                        t_msk = kb.tok("dq")

                        for oc2 in range(32):
                            par = oc2 % 2

                            def f_te(te, tb=tb, oc2=oc2, par=par,
                                     w8=(tok_wq if tb == 0 and oc2 == 0 else None),
                                     hd=(h_done if tb == 0 and oc2 == 0 else None),
                                     evp=t_h16[par]):
                                kb.wait(te, w8)
                                kb.wait(te, hd)
                                kb.wait(te, evp)
                                for dc in range(NCH):
                                    mm_ = te.matmul(
                                        PH[:, par * 512:par * 512 + 512],
                                        W1Q[:, dc * 4096 + oc2 * 128: dc * 4096 + oc2 * 128 + 128],
                                        X8[:, dc * TL + tb * 512: dc * TL + tb * 512 + 512],
                                        start=(dc == 0), stop=(dc == NCH - 1))
                                kb.tag(mm_, "mm")
                            kb.te(f_te)
                            t_hmm = kb.tok("mm")

                            def f_sc(sc, tb=tb, oc2=oc2, par=par, tok=t_hmm, cons=t_h8w[par],
                                     bq=(tok_bq if tb == 0 and oc2 == 0 else None)):
                                kb.wait(sc, bq)
                                kb.wait(sc, tok)
                                kb.wait(sc, cons)
                                kb.tag(sc.activation(H16[:, par * 512:(par + 1) * 512],
                                                     PH[:, par * 512:par * 512 + 512],
                                                     AF.Relu, scale=1.0,
                                                     bias=BQF[:, oc2:oc2 + 1]), "ac")
                            kb.sc(f_sc)
                            t_h16[par] = kb.tok("ac")

                            def f_ve(v, oc2=oc2, par=par, tok=t_h16[par],
                                     ycons=(t_ymm_all if oc2 == 0 else None)):
                                kb.wait(v, tok)
                                kb.wait(v, ycons)
                                kb.tag(v.tensor_copy(H8[:, oc2 * 512:(oc2 + 1) * 512],
                                                     H16[:, par * 512:(par + 1) * 512]), "ev")
                            kb.ve(f_ve)
                            t_h8w[par] = kb.tok("ev")
                        t_h8_last = t_h8w[31 % 2]

                        for oc in range(8):
                            ypar = oc % 2

                            def f_te(te, oc=oc, ypar=ypar,
                                     h8=(t_h8_last if oc == 0 else None),
                                     evp=t_yact[ypar]):
                                kb.wait(te, h8)
                                kb.wait(te, evp)
                                for hc in range(32):
                                    mm_ = te.matmul(
                                        PY[:, ypar * 512:ypar * 512 + 512],
                                        W2Q[:, hc * 1024 + oc * 128: hc * 1024 + oc * 128 + 128],
                                        H8[:, hc * 512:(hc + 1) * 512],
                                        start=(hc == 0), stop=(hc == 31))
                                kb.tag(mm_, "mm")
                            kb.te(f_te)
                            t_ymm = kb.tok("mm")

                            def f_sc(sc, oc=oc, ypar=ypar, tok=t_ymm, cons=t_ymul[ypar]):
                                kb.wait(sc, tok)
                                kb.wait(sc, cons)
                                kb.tag(sc.activation(Y16[:, ypar * 512:(ypar + 1) * 512],
                                                     PY[:, ypar * 512:ypar * 512 + 512],
                                                     AF.Identity, scale=1.0,
                                                     bias=BQF[:, 32 + oc:33 + oc]), "ac")
                            kb.sc(f_sc)
                            t_yact[ypar] = kb.tok("ac")

                            def f_ve(v, oc=oc, ypar=ypar, tok=t_yact[ypar],
                                     mk=t_msk, c1=t_yadd[ypar]):
                                kb.wait(v, tok)
                                kb.wait(v, mk)
                                kb.wait(v, c1)
                                kb.tag(v.tensor_mul(
                                    WSC[:, ypar * 512:(ypar + 1) * 512],
                                    Y16[:, ypar * 512:(ypar + 1) * 512],
                                    MSKT[:, oc * 512: oc * 512 + 512]), "ev")
                            kb.ve(f_ve)
                            t_ymul[ypar] = kb.tok("ev")

                            def f_gp(g, tb=tb, oc=oc, ypar=ypar, tok=t_ymul[ypar]):
                                kb.wait(g, tok)
                                xsl = X[:, oc * TL + tb * 512: oc * TL + tb * 512 + 512]
                                kb.tag(g.tensor_tensor(xsl, xsl,
                                                       WSC[:, ypar * 512:(ypar + 1) * 512],
                                                       ALU.add), "gv")
                            kb.gp(f_gp)
                            t_yadd[ypar] = kb.tok("gv")
                        t_ymm_all = t_ymm

            barrier()

        def emit_stage_l1attn(exi):
            ag_in, ag_out = d_ag_in[exi], d_ag_out[exi]
            ret_in, ret_out = d_ret_in[exi], d_ret_out[exi]

            with contextlib.ExitStack() as a5x:
                H5 = a5x.enter_context(nc.sbuf_tensor("H5", [128, NCH * TL], BF16))
                Z512 = a5x.enter_context(nc.sbuf_tensor("Z512", [128, 512], F32))
                W1LC = a5x.enter_context(nc.sbuf_tensor("W1LC", [128, 8], F32))

                def f_sy(sy):
                    kb.dma(sy, "dq", out=W1LC[:, :], in_=d_w1lc[:, :])
                kb.sy(f_sy)
                tok_w1l = kb.tok("dq")

                def f_ve(v, tok=tok_w1l):
                    kb.wait(v, tok)
                    kb.tag(v.memset(Z512[:, :], 0.0), "ev")
                kb.ve(f_ve)
                tok_z5 = kb.tok("ev")

                stx0 = contextlib.ExitStack()
                STAT5 = stx0.enter_context(nc.sbuf_tensor("STAT5", [1, 1536], F32))
                ONE6R = stx0.enter_context(nc.sbuf_tensor("ONE6R", [1, 512], F32))
                RAR = stx0.enter_context(nc.sbuf_tensor("RAR", [128, TL], F32))
                RBR = stx0.enter_context(nc.sbuf_tensor("RBR", [128, TL], F32))
                PREP5 = stx0.enter_context(nc.psum_tensor("prep5", [128, 1024], F32))

                def f_ve(v):
                    kb.tag(v.memset(ONE6R[:, :], 1.0 + 1e-6), "ev")
                kb.ve(f_ve)
                tok_one6 = kb.tok("ev")

                ex_state = {"prev": None}

                def extra_rows5(tb, tok_rows):
                    t0 = tb * 512

                    def f_ve(v, tok=tok_rows, t1=(tok_one6 if tb == 0 else None)):
                        kb.wait(v, tok)
                        kb.wait(v, t1)
                        rs = STAT[0:1, 512:1024]
                        v.tensor_mul(STAT5[0:1, 0:512], rs, rs)
                        v.drain()
                        v.tensor_mul(STAT5[0:1, 512:1024], STAT5[0:1, 0:512], EPSR[0:1, :])
                        v.drain()
                        kb.tag(v.tensor_sub(STAT5[0:1, 0:512], ONE6R[0:1, :],
                                            STAT5[0:1, 512:1024]), "ev")
                    kb.ve(f_ve)
                    t_a = kb.tok("ev")

                    def f_sc(sc, tok=t_a):
                        kb.wait(sc, tok)
                        kb.tag(sc.activation(STAT5[0:1, 512:1024], STAT5[0:1, 0:512],
                                             AF.Sqrt), "ac")
                    kb.sc(f_sc)
                    t_b = kb.tok("ac")

                    def f_ve(v, tok=t_b):
                        kb.wait(v, tok)
                        v.reciprocal(STAT5[0:1, 0:512], STAT5[0:1, 512:1024])
                        v.drain()
                        v.tensor_mul(STAT5[0:1, 512:1024], STAT5[0:1, 0:512],
                                     STAT[0:1, 512:1024])
                        ti = v.tensor_mul(STAT5[0:1, 1024:1536], STAT5[0:1, 0:512],
                                          STAT[0:1, 1536:2048])
                        v.drain()
                        kb.tag(ti, "ev")
                    kb.ve(f_ve)
                    t_c = kb.tok("ev")

                    def f_te(te, tok=t_c, prev=ex_state["prev"]):
                        kb.wait(te, tok)
                        kb.wait(te, prev)
                        te.matmul(PREP5[:, 0:512], ONESROWF[:, :], STAT5[0:1, 512:1024],
                                  start=True, stop=True)
                        kb.tag(te.matmul(PREP5[:, 512:1024], ONESROWF[:, :],
                                         STAT5[0:1, 1024:1536], start=True, stop=True),
                               "mm")
                    kb.te(f_te)
                    t_d = kb.tok("mm")

                    def f_ve(v, t0=t0, tok=t_d):
                        kb.wait(v, tok)
                        v.tensor_copy(RAR[:, t0:t0 + 512], PREP5[:, 0:512])
                        kb.tag(v.tensor_copy(RBR[:, t0:t0 + 512], PREP5[:, 512:1024]),
                               "ev")
                    kb.ve(f_ve)
                    ex_state["prev"] = kb.tok("ev")
                    return ex_state["prev"]

                h_done = emit_ln("s5", H5, extra_rows=extra_rows5,
                                 apply5=(RAR, RBR, W1LC, Z512))
                stx0.close()

                def f_sy(sy, tok=h_done):
                    kb.wait(sy, tok)
                    kb.wait(sy, tok_z5)
                    kb.dma(sy, "dq", out=ag_in[:, :].rearrange("(c p) t -> p c t", p=128),
                           in_=H5[:, :].rearrange("p (c t) -> p c t", t=TL))
                kb.sy(f_sy)
                tok_hstg = kb.tok("dq")
                if DBG:
                    def f_sy(sy, tok=h_done):
                        kb.wait(sy, tok)
                        kb.dma(sy, "dq", out=d_dbg_h5[:, :], in_=H5[:, :])
                    kb.sy(f_sy)

                def f_gp(g, tok=tok_hstg):
                    kb.wait(g, tok)
                    kb.tag(g.collective_compute(
                        "AllGather", ALU.bypass, ins=[ag_in[:, :]], outs=[ag_out[:, :]],
                        replica_groups=[list(range(NC))]), "cc")
                kb.gp(f_gp)
                tok_ag5 = kb.tok("cc")

                with contextlib.ExitStack() as p5x:
                    W5S = p5x.enter_context(nc.sbuf_tensor("W5S", [128, 8 * 576], F32))
                    W51 = p5x.enter_context(nc.sbuf_tensor("W51", [128, 8 * 576], BF16))
                    W52 = p5x.enter_context(nc.sbuf_tensor("W52", [128, 8 * 576], BF16))
                    MU5 = p5x.enter_context(nc.sbuf_tensor("MU5", [128, 40], F32))
                    OM5 = p5x.enter_context(nc.sbuf_tensor("OM5", [128, 40], F32))
                    W0C = p5x.enter_context(nc.sbuf_tensor("W0C", [128, 1], F32))
                    WB5 = p5x.enter_context(nc.sbuf_tensor("WB5", [64, 128], BF16))
                    C10T = p5x.enter_context(nc.sbuf_tensor("C10T", [128, 512], F32))
                    HL5 = p5x.enter_context(nc.sbuf_tensor("HL5", [128, 2 * 8 * 513], BF16))
                    TA5 = p5x.enter_context(nc.sbuf_tensor("TA5", [64, 2 * 512], BF16))
                    SPR5 = p5x.enter_context(nc.sbuf_tensor("SPR5", [128, 2 * 512], BF16))
                    SPG5 = p5x.enter_context(nc.sbuf_tensor("SPG5", [128, 2 * 512], BF16))
                    RTF5 = p5x.enter_context(nc.sbuf_tensor("RTF5", [128, 2 * 512], F32))
                    PPRJ = p5x.enter_context(nc.psum_tensor("pprj", [128, 1024], F32))
                    PLB = p5x.enter_context(nc.psum_tensor("plb", [128, 1024], F32))

                    def f_gp(g, hd=h_done):
                        kb.wait(g, hd)
                        kb.dma(g, "dg", out=W5S[:, :].rearrange("p (dc o) -> p dc o", o=576),
                               in_=d_w5T[:, :].rearrange("(dc p) o -> p dc o", p=128))
                        kb.dma(g, "dg", out=WB5[:, :], in_=d_wb5[:, :])
                    kb.gp(f_gp)
                    tok_w5 = kb.tok("dg")

                    def f_sy(sy, hd=h_done):
                        kb.wait(sy, hd)
                        kb.dma(sy, "dq", out=MU5[:, :], in_=d_mu5c[:, :])
                        kb.dma(sy, "dq", out=W0C[:, :], in_=d_w0c[:, :])
                    kb.sy(f_sy)
                    tok_mu = kb.tok("dq")

                    def f_ve(v, tw=tok_w5, tm=tok_mu, tz=tok_z5):
                        kb.wait(v, tw)
                        kb.wait(v, tm)
                        kb.wait(v, tz)
                        v.memset(OM5[:, :], 1.0)
                        v.memset(C10T[:, :], 9.0)
                        v.drain()
                        v.tensor_sub(OM5[:, :], OM5[:, :], MU5[:, :])
                        v.drain()
                        ti = None
                        for dc in range(NCH):
                            for m in range(5):
                                wdt = 128 if m < 4 else 64
                                off = dc * 576 + m * 128
                                mcol = dc * 5 + m
                                v.scalar_tensor_tensor(
                                    W51[:, off:off + wdt], W5S[:, off:off + wdt],
                                    OM5[:, mcol:mcol + 1], Z512[:, 0:wdt],
                                    ALU.mult, ALU.add)
                                ti = v.scalar_tensor_tensor(
                                    W52[:, off:off + wdt], W5S[:, off:off + wdt],
                                    MU5[:, mcol:mcol + 1], Z512[:, 0:wdt],
                                    ALU.mult, ALU.add)
                        kb.tag(ti, "ev")
                    kb.ve(f_ve)
                    tok_wsc = kb.tok("ev")

                    hl_toks = [None, None]
                    prj_ev = [None, None]
                    lb_ev = [None, None]
                    sp_toks = [None, None]
                    g_toks = [None, None]
                    rt_toks = [None, None]
                    ta_toks = [None, None]
                    ppar = 0
                    lpar_lb = 0
                    for qb in range(32):
                        lpar = qb % 2
                        gtok = qb * 512
                        r_ = gtok // TL
                        tcol = gtok % TL

                        def f_sy(sy, qb=qb, lpar=lpar, r_=r_, tcol=tcol,
                                 agw=(tok_ag5 if qb == 0 else None), cons=hl_toks[lpar]):
                            kb.wait(sy, agw)
                            kb.wait(sy, cons)
                            base = lpar * 4104
                            hv = HL5[:, base:base + 4104].rearrange("p (c t) -> p c t", t=513)
                            src = ag_out[:, :].rearrange("(r c p) t -> p r c t", p=128, c=NCH)
                            kb.dma(sy, "dq", out=hv[:, :, 1:513],
                                   in_=src[:, r_, :, tcol:tcol + 512])
                            gm1 = gtok - 1
                            if gtok not in (0, 8192):
                                kb.dma(sy, "dq", out=hv[:, :, 0:1],
                                       in_=src[:, gm1 // TL, :, (gm1 % TL):(gm1 % TL) + 1],
                                       allow_slow_non_contiguous=True)
                        kb.sy(f_sy)
                        tok_hl = kb.tok("dq")

                        if gtok in (0, 8192):
                            def f_ve(v, lpar=lpar, tok=tok_hl):
                                kb.wait(v, tok)
                                base = lpar * 4104
                                hv = HL5[:, base:base + 4104].rearrange(
                                    "p (c t) -> p c t", t=513)
                                kb.tag(v.memset(hv[:, :, 0:1], 0.0), "ev")
                            kb.ve(f_ve)
                            tok_hl = (tok_hl, kb.tok("ev"))
                        else:
                            tok_hl = (tok_hl, None)

                        for m in range(5):
                            wdt = 128 if m < 4 else 64

                            def f_te(te, m=m, lpar=lpar, ppar=ppar, tok=tok_hl,
                                     w8=(tok_wsc if qb == 0 and m == 0 else None),
                                     evp=prj_ev[ppar]):
                                kb.wait(te, w8)
                                kb.wait(te, tok[0])
                                kb.wait(te, tok[1])
                                kb.wait(te, evp)
                                base = lpar * 4104
                                hv = HL5[:, base:base + 4104].rearrange(
                                    "p (c t) -> p c t", t=513)
                                wdt = 128 if m < 4 else 64
                                out = PPRJ[0:wdt, ppar * 512:ppar * 512 + 512]
                                for dc in range(NCH):
                                    off = dc * 576 + m * 128
                                    te.matmul(out, W51[:, off:off + wdt],
                                              hv[:, dc, 1:513], start=(dc == 0), stop=False)
                                    mm_ = te.matmul(out, W52[:, off:off + wdt],
                                                    hv[:, dc, 0:512], start=False,
                                                    stop=(dc == NCH - 1))
                                kb.tag(mm_, "mm")
                            kb.te(f_te)
                            tok_pm = kb.tok("mm")

                            if m < 3:
                                spar = (qb * 3 + m) % 2
                                dsp = (d_r5, d_k5, d_v5)[m]

                                def f_ve(v, ppar=ppar, spar=spar, tok=tok_pm,
                                         cons=sp_toks[spar]):
                                    kb.wait(v, tok)
                                    kb.wait(v, cons)
                                    kb.tag(v.tensor_copy(
                                        SPR5[:, spar * 512:(spar + 1) * 512],
                                        PPRJ[:, ppar * 512:ppar * 512 + 512]), "ev")
                                kb.ve(f_ve)
                                tok_sp = kb.tok("ev")
                                prj_ev[ppar] = tok_sp

                                def f_gp(g, gtok=gtok, spar=spar, dsp=dsp, tok=tok_sp):
                                    kb.wait(g, tok)
                                    kb.dma(g, "dg", out=dsp[:, gtok:gtok + 512],
                                           in_=SPR5[:, spar * 512:(spar + 1) * 512])
                                kb.gp(f_gp)
                                sp_toks[spar] = kb.tok("dg")
                            elif m == 3:
                                gpar = qb % 2

                                def f_sc(sc, ppar=ppar, gpar=gpar, tok=tok_pm,
                                         cons=g_toks[gpar]):
                                    kb.wait(sc, tok)
                                    kb.wait(sc, cons)
                                    kb.tag(sc.activation(
                                        SPG5[:, gpar * 512:(gpar + 1) * 512],
                                        PPRJ[:, ppar * 512:ppar * 512 + 512],
                                        AF.Silu), "ac")
                                kb.sc(f_sc)
                                tok_g = kb.tok("ac")
                                prj_ev[ppar] = tok_g

                                def f_gp(g, gtok=gtok, gpar=gpar, tok=tok_g):
                                    kb.wait(g, tok)
                                    kb.dma(g, "dg", out=d_g5[:, gtok:gtok + 512],
                                           in_=SPG5[:, gpar * 512:(gpar + 1) * 512])
                                kb.gp(f_gp)
                                g_toks[gpar] = kb.tok("dg")
                            else:
                                tpar = qb % 2

                                def f_sc(sc, ppar=ppar, tpar=tpar, tok=tok_pm,
                                         cons=ta_toks[tpar]):
                                    kb.wait(sc, tok)
                                    kb.wait(sc, cons)
                                    kb.tag(sc.activation(
                                        TA5[0:64, tpar * 512:(tpar + 1) * 512],
                                        PPRJ[0:64, ppar * 512:ppar * 512 + 512],
                                        AF.Tanh), "ac")
                                kb.sc(f_sc)
                                tok_ta = kb.tok("ac")
                                prj_ev[ppar] = tok_ta

                                def f_te(te, tpar=tpar, tok=tok_ta,
                                         wb=(tok_w5 if qb == 0 else None),
                                         evp=lb_ev[tpar]):
                                    kb.wait(te, wb)
                                    kb.wait(te, tok)
                                    kb.wait(te, evp)
                                    kb.tag(te.matmul(
                                        PLB[:, tpar * 512:tpar * 512 + 512], WB5[:, :],
                                        TA5[0:64, tpar * 512:(tpar + 1) * 512],
                                        start=True, stop=True), "mm")
                                kb.te(f_te)
                                tok_lb = kb.tok("mm")
                                ta_toks[tpar] = tok_lb

                                def f_sc(sc, tpar=tpar, tok=tok_lb, cons=rt_toks[tpar]):
                                    kb.wait(sc, tok)
                                    kb.wait(sc, cons)
                                    kb.tag(sc.activation(
                                        RTF5[:, tpar * 512:(tpar + 1) * 512],
                                        PLB[:, tpar * 512:tpar * 512 + 512],
                                        AF.Exp, scale=1.0, bias=W0C[:, 0:1]), "ac")
                                kb.sc(f_sc)
                                tok_rt = kb.tok("ac")
                                lb_ev[tpar] = tok_rt

                                def f_ve(v, tpar=tpar, tok=tok_rt):
                                    kb.wait(v, tok)
                                    sl = RTF5[:, tpar * 512:(tpar + 1) * 512]
                                    kb.tag(v.tensor_tensor(sl, sl, C10T[:, :],
                                                           ALU.min), "ev")
                                kb.ve(f_ve)
                                tok_cl = kb.tok("ev")

                                def f_gp(g, gtok=gtok, tpar=tpar, tok=tok_cl):
                                    kb.wait(g, tok)
                                    kb.dma(g, "dg", out=d_rate5[:, gtok:gtok + 512],
                                           in_=RTF5[:, tpar * 512:(tpar + 1) * 512])
                                kb.gp(f_gp)
                                rt_toks[tpar] = kb.tok("dg")
                            ppar ^= 1
                        hl_toks[lpar] = prj_ev[(ppar + 1) % 2]
                    proj5_done_dg = kb.tok("dg")
                    proj5_done_dq = kb.tok("dq")

                SWW = 512
                with contextlib.ExitStack() as s5x:
                    RATE5 = s5x.enter_context(nc.sbuf_tensor("RATE5", [128, 2 * SWW], F32))
                    REV5 = s5x.enter_context(nc.sbuf_tensor("REV5", [128, SWW], F32))
                    FS = s5x.enter_context(nc.sbuf_tensor("FS", [128, 5 * SWW], F32))
                    RSC = s5x.enter_context(nc.sbuf_tensor("RSC", [128, SWW], F32))
                    RSF = s5x.enter_context(nc.sbuf_tensor("RSF", [128, 5 * SWW], F32))
                    KD8X = s5x.enter_context(nc.sbuf_tensor("KD8X", [128, SWW], F32))
                    QP = s5x.enter_context(nc.sbuf_tensor("QP", [128, 5 * SWW], BF16))
                    KP8 = s5x.enter_context(nc.sbuf_tensor("KP8", [128, SWW], BF16))
                    KGP = s5x.enter_context(nc.sbuf_tensor("KGP", [128, 4 * SWW], BF16))
                    KST = s5x.enter_context(nc.sbuf_tensor("KST", [128, SWW], BF16))
                    UKB = s5x.enter_context(nc.sbuf_tensor("UKB", [128, SWW], BF16))
                    RSW5 = s5x.enter_context(nc.sbuf_tensor("RSW5", [128, SWW], BF16))
                    KSW5 = s5x.enter_context(nc.sbuf_tensor("KSW5", [128, SWW], BF16))
                    GSW5 = s5x.enter_context(nc.sbuf_tensor("GSW5", [128, SWW], BF16))
                    VTM5 = s5x.enter_context(nc.sbuf_tensor("VTM5", [128, 4 * 128], BF16))
                    KTM5 = s5x.enter_context(nc.sbuf_tensor("KTM5", [128, 2 * 128], BF16))
                    S5 = s5x.enter_context(nc.sbuf_tensor("S5", [128, 128], F32))
                    S5B = s5x.enter_context(nc.sbuf_tensor("S5B", [128, 128], BF16))
                    AT5F = s5x.enter_context(nc.sbuf_tensor("AT5F", [128, 128], F32))
                    GSCR = s5x.enter_context(nc.sbuf_tensor("GSCR", [128, 128], F32))
                    SCR5 = s5x.enter_context(nc.sbuf_tensor("SCR5", [128, 128], F32))
                    AT5B = s5x.enter_context(nc.sbuf_tensor("AT5B", [128, 2 * 128], BF16))
                    MSK5 = s5x.enter_context(nc.sbuf_tensor("MSK5", [128, 7 * 128], F32))
                    SEG5 = s5x.enter_context(nc.sbuf_tensor("SEG5", [128, 5 * SWW], F32))
                    UC5 = s5x.enter_context(nc.sbuf_tensor("UC5", [128, 4], F32))
                    DDsc = s5x.enter_context(nc.sbuf_tensor("DDsc", [128, 4], F32))
                    OS5 = s5x.enter_context(nc.sbuf_tensor("OS5", [128, SWW], BF16))
                    SQ5 = s5x.enter_context(nc.sbuf_tensor("SQ5", [128, SWW], BF16))
                    OT5 = s5x.enter_context(nc.sbuf_tensor("OT5", [128, SWW], BF16))
                    MREPS = s5x.enter_context(nc.sbuf_tensor("MREPS", [128, SWW], F32))
                    RREPS = s5x.enter_context(nc.sbuf_tensor("RREPS", [128, SWW], F32))
                    TGN = s5x.enter_context(nc.sbuf_tensor("TGN", [128, SWW], F32))
                    GR = s5x.enter_context(nc.sbuf_tensor("GR", [1, 4 * SWW], F32))
                    PSC = s5x.enter_context(nc.psum_tensor("psc5", [128, 1024], F32))
                    PSO5 = s5x.enter_context(nc.psum_tensor("pso5", [128, 256], F32))
                    PSADD = s5x.enter_context(nc.psum_tensor("psadd5", [128, 128], F32))
                    PGNS = s5x.enter_context(nc.psum_tensor("pgns5", [1, SWW], F32))
                    PGNR = s5x.enter_context(nc.psum_tensor("pgnr5", [128, SWW], F32))

                    def f_sy(sy, pd=proj5_done_dg, pq=proj5_done_dq):
                        kb.wait(sy, pd)
                        kb.wait(sy, pq)
                        kb.dma(sy, "dq", out=MSK5[:, :], in_=d_msk5[:, :])
                        kb.dma(sy, "dq", out=SEG5[:, :].rearrange("p (g t) -> p g t", t=SWW),
                               in_=d_seg5[:, :].rearrange("p (g t) -> p g t", t=1024)[:, :, 0:SWW])
                        kb.dma(sy, "dq", out=UC5[:, 0:1], in_=d_uc[:, :])
                        kb.dma(sy, "dq", out=UC5[:, 1:2], in_=d_gnwc[:, :])
                        kb.dma(sy, "dq", out=UC5[:, 2:3], in_=d_gnbc[:, :])
                    kb.sy(f_sy)
                    tok_cst5 = kb.tok("dq")

                    ld_cons = None
                    fac_cons = None
                    st_tok = None
                    os_cons = None
                    ret_toks = None
                    vtm_cons = None
                    ktm_cons = [None, None]
                    psc_free = None
                    pso_free = [None, None]
                    at5_free = [None, None]
                    sadd_free = None
                    gn_done = None
                    for b in range(2):
                        def f_ve(v, st=st_tok,
                                 pd=(proj5_done_dg if b == 0 else None)):
                            kb.wait(v, st)
                            kb.wait(v, pd)
                            kb.tag(v.memset(S5[:, :], 0.0), "ev")
                        kb.ve(f_ve)
                        st_tok = kb.tok("ev")

                        for sw in range(16):
                            W0 = b * S + sw * SWW
                            rpar = sw % 2

                            def f_sy(sy, W0=W0, rpar=rpar, cons=ld_cons,
                                     pd=(proj5_done_dg if b == 0 and sw == 0 else None),
                                     pq=(proj5_done_dq if b == 0 and sw == 0 else None)):
                                kb.wait(sy, pd)
                                kb.wait(sy, pq)
                                kb.wait(sy, cons)
                                kb.dma(sy, "dq", out=RATE5[:, rpar * SWW:(rpar + 1) * SWW],
                                       in_=d_rate5[:, W0:W0 + SWW])
                                kb.dma(sy, "dq", out=RSW5[:, :], in_=d_r5[:, W0:W0 + SWW])
                                kb.dma(sy, "dq", out=KSW5[:, :], in_=d_k5[:, W0:W0 + SWW])
                                kb.dma(sy, "dq", out=GSW5[:, :], in_=d_g5[:, W0:W0 + SWW])
                                for w in range(4):
                                    kb.dma(sy, "dq", out=VTM5[:, w * 128:(w + 1) * 128],
                                           in_=d_v5[:, W0 + w * 128:W0 + (w + 1) * 128],
                                           transpose=True)
                            kb.sy(f_sy)
                            tok_ld = kb.tok("dq")

                            def f_ve(v, rpar=rpar, tok=tok_ld, fc=fac_cons,
                                     tc=(tok_cst5 if b == 0 and sw == 0 else None)):
                                kb.wait(v, tok)
                                kb.wait(v, fc)
                                kb.wait(v, tc)
                                rt = RATE5[:, rpar * SWW:(rpar + 1) * SWW]
                                v.tensor_copy(REV5[:, :], rt[:, ::-1])
                                v.drain()
                                for g in range(5):
                                    v.tensor_tensor_scan(
                                        FS[:, g * SWW:(g + 1) * SWW],
                                        SEG5[:, g * SWW:(g + 1) * SWW], rt,
                                        0.0, ALU.mult, ALU.add)
                                ti = None
                                for g in range(5):
                                    v.tensor_tensor_scan(
                                        RSC[:, :], SEG5[:, g * SWW:(g + 1) * SWW],
                                        REV5[:, :], 0.0, ALU.mult, ALU.add)
                                    v.drain()
                                    ti = v.tensor_copy(RSF[:, g * SWW:(g + 1) * SWW],
                                                       RSC[:, ::-1])
                                    v.drain()
                                kb.tag(ti, "ev")
                            kb.ve(f_ve)
                            tok_scan = kb.tok("ev")

                            def f_sc(sc, tok=tok_scan):
                                kb.wait(sc, tok)
                                kb.tag(sc.activation(KD8X[:, :], FS[:, 0:SWW], AF.Exp), "ac")
                            kb.sc(f_sc)
                            tok_kd8x = kb.tok("ac")

                            def f_ve(v, rpar=rpar, tok=tok_kd8x):
                                kb.wait(v, tok)
                                rt = RATE5[:, rpar * SWW:(rpar + 1) * SWW]
                                v.tensor_mul(KP8[:, :], KSW5[:, :], KD8X[:, :])
                                ti = None
                                for g in range(5):
                                    v.tensor_sub(FS[:, g * SWW:(g + 1) * SWW],
                                                 FS[:, g * SWW:(g + 1) * SWW], rt)
                                    ti = v.tensor_sub(RSF[:, g * SWW:(g + 1) * SWW],
                                                      RSF[:, g * SWW:(g + 1) * SWW], rt)
                                kb.tag(ti, "ev")
                            kb.ve(f_ve)
                            tok_subs = kb.tok("ev")

                            def f_sc(sc, tok=tok_subs):
                                kb.wait(sc, tok)
                                for g in range(4):
                                    sc.activation(FS[:, g * SWW:(g + 1) * SWW],
                                                  FS[:, g * SWW:(g + 1) * SWW],
                                                  AF.Exp, scale=-1.0)
                                sc.activation(KD8X[:, :], FS[:, 4 * SWW:5 * SWW],
                                              AF.Exp, scale=-1.0)
                                ti = None
                                for g in range(5):
                                    ti = sc.activation(RSF[:, g * SWW:(g + 1) * SWW],
                                                       RSF[:, g * SWW:(g + 1) * SWW],
                                                       AF.Exp, scale=-1.0)
                                kb.tag(ti, "ac")
                            kb.sc(f_sc)
                            tok_exps = kb.tok("ac")

                            def f_ve(v, tok=tok_exps):
                                kb.wait(v, tok)
                                for g in range(4):
                                    v.tensor_mul(QP[:, g * SWW:(g + 1) * SWW], RSW5[:, :],
                                                 FS[:, g * SWW:(g + 1) * SWW])
                                v.tensor_mul(QP[:, 4 * SWW:5 * SWW], RSW5[:, :],
                                             KD8X[:, :])
                                for g in range(4):
                                    v.tensor_mul(KGP[:, g * SWW:(g + 1) * SWW], KSW5[:, :],
                                                 RSF[:, g * SWW:(g + 1) * SWW])
                                v.tensor_mul(KST[:, :], KSW5[:, :], RSF[:, 4 * SWW:5 * SWW])
                                v.drain()
                                ti = v.scalar_tensor_tensor(UKB[:, :], KSW5[:, :],
                                                            UC5[:, 0:1], Z512[:, :],
                                                            ALU.mult, ALU.add)
                                kb.tag(ti, "ev")
                            kb.ve(f_ve)
                            tok_fac = kb.tok("ev")

                            for w in range(4):
                                wpar = w % 2

                                def f_sy(sy, w=w, wpar=wpar, tok=tok_fac,
                                         cons=ktm_cons[wpar]):
                                    kb.wait(sy, tok)
                                    kb.wait(sy, cons)
                                    kb.dma(sy, "dq", out=KTM5[:, wpar * 128:(wpar + 1) * 128],
                                           in_=KST[:, w * 128:(w + 1) * 128], transpose=True)
                                kb.sy(f_sy)
                                tok_ktm = kb.tok("dq")

                                wsl = slice(w * 128, (w + 1) * 128)
                                opar = w % 2
                                at_toks = []
                                for h in range(2):
                                    hb = h * 64

                                    def f_te(te, h=h, hb=hb, wsl=wsl, tok=tok_fac,
                                             pf=psc_free, vt=(tok_ld if h == 0 else None)):
                                        kb.wait(te, tok)
                                        kb.wait(te, pf)
                                        kb.wait(te, vt)
                                        lhs = [(KP8, 0), (KGP, 0), (KGP, 1), (KGP, 2),
                                               (KGP, 3), (UKB, 0)]
                                        rhs = [0, 0, 1, 2, 3, None]
                                        mi = None
                                        for l in range(6):
                                            kt, ksl = lhs[l]
                                            ksl_ap = kt[hb:hb + 64,
                                                        ksl * SWW + w * 128: ksl * SWW + (w + 1) * 128]
                                            if rhs[l] is None:
                                                q_ap = RSW5[hb:hb + 64, wsl]
                                            else:
                                                q_ap = QP[hb:hb + 64,
                                                          rhs[l] * SWW + w * 128: rhs[l] * SWW + (w + 1) * 128]
                                            mi = te.matmul(PSC[:, l * 128:(l + 1) * 128],
                                                           ksl_ap, q_ap,
                                                           start=True, stop=True)
                                        kb.tag(mi, "mm")
                                    kb.te(f_te)
                                    tok_sc5 = kb.tok("mm")

                                    def f_ve(v, h=h, tok=tok_sc5,
                                             af=(at_toks[0] if h == 1 else at5_free[0]),
                                             avc=at5_free[1]):
                                        kb.wait(v, tok)
                                        kb.wait(v, af)
                                        kb.wait(v, avc)
                                        v.tensor_mul(AT5F[:, :], PSC[:, 0:128],
                                                     MSK5[:, 0:128])
                                        for l in range(1, 6):
                                            v.tensor_mul(GSCR[:, :],
                                                         PSC[:, l * 128:(l + 1) * 128],
                                                         MSK5[:, l * 128:(l + 1) * 128])
                                            v.drain()
                                            v.tensor_add(AT5F[:, :], AT5F[:, :],
                                                         GSCR[:, :])
                                        v.drain()
                                        kb.tag(v.tensor_copy(AT5B[:, h * 128:(h + 1) * 128],
                                                             AT5F[:, :]), "ev")
                                    kb.ve(f_ve)
                                    at_toks.append(kb.tok("ev"))
                                    at5_free[0] = at_toks[-1]
                                    psc_free = at_toks[-1]

                                def f_ve(v, st=st_tok):
                                    kb.wait(v, st)
                                    kb.tag(v.tensor_copy(S5B[:, :], S5[:, :]), "ev")
                                kb.ve(f_ve)
                                tok_s5b = kb.tok("ev")

                                def f_te(te, w=w, opar=opar, tok=tok_s5b, a0=at_toks[0],
                                         a1=at_toks[1], pf=pso_free[opar], tf=tok_fac):
                                    kb.wait(te, tok)
                                    kb.wait(te, a0)
                                    kb.wait(te, a1)
                                    kb.wait(te, pf)
                                    kb.wait(te, tf)
                                    osl = PSO5[:, opar * 128:(opar + 1) * 128]
                                    te.matmul(osl, S5B[:, :],
                                              QP[:, 4 * SWW + w * 128: 4 * SWW + (w + 1) * 128],
                                              start=True, stop=False)
                                    te.matmul(osl[0:64, :], VTM5[:, w * 128:w * 128 + 64],
                                              AT5B[:, 0:128], start=False, stop=True)
                                    kb.tag(te.matmul(osl[64:128, :],
                                                     VTM5[:, w * 128 + 64:(w + 1) * 128],
                                                     AT5B[:, 128:256], start=False,
                                                     stop=True), "mm")
                                kb.te(f_te)
                                tok_av = kb.tok("mm")
                                at5_free[1] = tok_av

                                def f_ve(v, w=w, opar=opar, tok=tok_av, oc=os_cons):
                                    kb.wait(v, tok)
                                    kb.wait(v, oc)
                                    kb.tag(v.tensor_copy(OS5[:, w * 128:(w + 1) * 128],
                                                         PSO5[:, opar * 128:(opar + 1) * 128]),
                                           "ev")
                                kb.ve(f_ve)
                                pso_free[opar] = kb.tok("ev")

                                def f_te(te, w=w, wpar=wpar, tok=tok_ktm, sf=sadd_free,
                                         vt=tok_ld):
                                    kb.wait(te, tok)
                                    kb.wait(te, sf)
                                    kb.wait(te, vt)
                                    kb.tag(te.matmul(PSADD[:, :],
                                                     KTM5[:, wpar * 128:(wpar + 1) * 128],
                                                     VTM5[:, w * 128:(w + 1) * 128],
                                                     start=True, stop=True), "mm")
                                kb.te(f_te)
                                tok_sadd = kb.tok("mm")

                                def f_sc(sc, w=w, rpar=rpar, tok=tok_scan):
                                    kb.wait(sc, tok)
                                    rt = RATE5[:, rpar * SWW:(rpar + 1) * SWW]
                                    sc.activation(DDsc[:, 1:2], rt[:, w * 128 + 127:w * 128 + 128],
                                                  AF.Copy, scale=-1.0)
                                    kb.tag(sc.activation(
                                        DDsc[:, 0:1],
                                        FS[:, 4 * SWW + w * 128 + 127: 4 * SWW + w * 128 + 128],
                                        AF.Exp, scale=-1.0, bias=DDsc[:, 1:2]), "ac")
                                kb.sc(f_sc)
                                tok_dd = kb.tok("ac")

                                def f_ve(v, tok=tok_sadd, td=tok_dd, ts=tok_s5b,
                                         st=st_tok):
                                    kb.wait(v, tok)
                                    kb.wait(v, td)
                                    kb.wait(v, ts)
                                    kb.wait(v, st)
                                    v.tensor_mul(SCR5[:, :], PSADD[:, :],
                                                 MSK5[:, 6 * 128:7 * 128])
                                    v.drain()
                                    kb.tag(v.scalar_tensor_tensor(
                                        S5[:, :], S5[:, :], DDsc[:, 0:1], SCR5[:, :],
                                        ALU.mult, ALU.add), "ev")
                                kb.ve(f_ve)
                                st_tok = kb.tok("ev")
                                sadd_free = st_tok
                                ktm_cons[wpar] = tok_sadd

                            fac_cons = kb.tok("mm")

                            def f_ve(v, pf=pso_free):
                                kb.wait(v, pf[0])
                                kb.wait(v, pf[1])
                                kb.tag(v.tensor_mul(SQ5[:, :], OS5[:, :], OS5[:, :]), "ev")
                            kb.ve(f_ve)
                            tok_sq5 = kb.tok("ev")

                            gn_rows = []
                            for qi in range(4):
                                srcb = (OS5, OS5, SQ5, SQ5)[qi]
                                hb = (0, 64, 0, 64)[qi]

                                def f_te(te, qi=qi, srcb=srcb, hb=hb, tok=tok_sq5,
                                         gp_=gn_done):
                                    kb.wait(te, tok)
                                    kb.wait(te, gp_)
                                    kb.tag(te.matmul(PGNS[0:1, :], ONESB[hb:hb + 64, 0:1],
                                                     srcb[hb:hb + 64, :],
                                                     start=True, stop=True), "mm")
                                kb.te(f_te)
                                tok_gmm = kb.tok("mm")

                                def f_sc(sc, qi=qi, tok=tok_gmm):
                                    kb.wait(sc, tok)
                                    kb.tag(sc.activation(GR[0:1, qi * SWW:(qi + 1) * SWW],
                                                         PGNS[0:1, :], AF.Copy,
                                                         scale=1.0 / 64.0), "ac")
                                kb.sc(f_sc)
                                gn_rows.append(kb.tok("ac"))
                                gn_done = gn_rows[-1]

                            def f_ve(v, tok=gn_rows[-1]):
                                kb.wait(v, tok)
                                for h in range(2):
                                    mrow = GR[0:1, h * SWW:(h + 1) * SWW]
                                    erow = GR[0:1, (2 + h) * SWW:(3 + h) * SWW]
                                    v.tensor_mul(TGN[0:1, :], mrow, mrow)
                                    v.drain()
                                    v.tensor_sub(erow, erow, TGN[0:1, :])
                                    v.drain()
                                    ti = v.tensor_add(erow, erow, EPSR[0:1, :])
                                    v.drain()
                                kb.tag(ti, "ev")
                            kb.ve(f_ve)
                            tok_var5 = kb.tok("ev")

                            def f_sc(sc, tok=tok_var5):
                                kb.wait(sc, tok)
                                sc.activation(GR[0:1, 2 * SWW:3 * SWW], GR[0:1, 2 * SWW:3 * SWW],
                                              AF.Sqrt)
                                kb.tag(sc.activation(GR[0:1, 3 * SWW:4 * SWW],
                                                     GR[0:1, 3 * SWW:4 * SWW], AF.Sqrt), "ac")
                            kb.sc(f_sc)
                            tok_sqr5 = kb.tok("ac")

                            def f_ve(v, tok=tok_sqr5):
                                kb.wait(v, tok)
                                v.reciprocal(GR[0:1, 2 * SWW:3 * SWW], GR[0:1, 2 * SWW:3 * SWW])
                                ti = v.reciprocal(GR[0:1, 3 * SWW:4 * SWW],
                                                  GR[0:1, 3 * SWW:4 * SWW])
                                v.drain()
                                kb.tag(ti, "ev")
                            kb.ve(f_ve)
                            tok_rst5 = kb.tok("ev")

                            def f_te(te, tok=tok_rst5):
                                kb.wait(te, tok)
                                te.matmul(PGNR[0:64, :], ONESROWF[0:1, 0:64],
                                          GR[0:1, 0:SWW], start=True, stop=True)
                                kb.tag(te.matmul(PGNR[64:128, :], ONESROWF[0:1, 0:64],
                                                 GR[0:1, SWW:2 * SWW], start=True,
                                                 stop=True), "mm")
                            kb.te(f_te)
                            tok_mrep = kb.tok("mm")

                            def f_ve(v, tok=tok_mrep):
                                kb.wait(v, tok)
                                kb.tag(v.tensor_copy(MREPS[:, :], PGNR[:, :]), "ev")
                            kb.ve(f_ve)
                            tok_mcp = kb.tok("ev")

                            def f_te(te, tok=tok_mcp):
                                kb.wait(te, tok)
                                te.matmul(PGNR[0:64, :], ONESROWF[0:1, 0:64],
                                          GR[0:1, 2 * SWW:3 * SWW], start=True, stop=True)
                                kb.tag(te.matmul(PGNR[64:128, :], ONESROWF[0:1, 0:64],
                                                 GR[0:1, 3 * SWW:4 * SWW], start=True,
                                                 stop=True), "mm")
                            kb.te(f_te)
                            tok_rrep = kb.tok("mm")

                            def f_ve(v, W0=W0, tok=tok_rrep):
                                kb.wait(v, tok)
                                v.tensor_copy(RREPS[:, :], PGNR[:, :])
                                v.drain()
                                v.tensor_sub(TGN[:, :], OS5[:, :], MREPS[:, :])
                                v.drain()
                                v.tensor_mul(TGN[:, :], TGN[:, :], RREPS[:, :])
                                v.drain()
                                ti = v.scalar_tensor_tensor(TGN[:, :], TGN[:, :],
                                                            UC5[:, 1:2], Z512[:, :],
                                                            ALU.mult, ALU.add)
                                kb.tag(ti, "ev")
                            kb.ve(f_ve)
                            tok_gn1 = kb.tok("ev")

                            def f_sc(sc, tok=tok_gn1):
                                kb.wait(sc, tok)
                                kb.tag(sc.activation(TGN[:, :], TGN[:, :], AF.Identity,
                                                     scale=1.0, bias=UC5[:, 2:3]), "ac")
                            kb.sc(f_sc)
                            tok_gn2 = kb.tok("ac")

                            def f_ve(v, tok=tok_gn2, rc=ret_toks):
                                kb.wait(v, tok)
                                kb.wait(v, rc)
                                kb.tag(v.tensor_mul(OT5[:, :], TGN[:, :], GSW5[:, :]), "ev")
                            kb.ve(f_ve)
                            tok_ot5 = kb.tok("ev")
                            os_cons = tok_ot5
                            gn_done = tok_ot5

                            def f_sy(sy, W0=W0, tok=tok_ot5, tsq=tok_sq5):
                                kb.wait(sy, tok)
                                kb.dma(sy, "dq", out=ret_in[:, W0:W0 + SWW], in_=OT5[:, :])
                                if DBG:
                                    kb.wait(sy, tsq)
                                    kb.dma(sy, "dq", out=d_dbg_os5[:, W0:W0 + SWW],
                                           in_=OS5[:, :])
                                    kb.dma(sy, "dq", out=d_dbg_ot5[:, W0:W0 + SWW],
                                           in_=OT5[:, :])
                            kb.sy(f_sy)
                            ret_toks = kb.tok("dq")
                            ld_cons = ret_toks

                    def f_gp(g, tok=ret_toks):
                        kb.wait(g, tok)
                        kb.tag(g.collective_compute(
                            "AllGather", ALU.bypass, ins=[ret_in[:, :]], outs=[ret_out[:, :]],
                            replica_groups=[list(range(NC))]), "cc")
                    kb.gp(f_gp)
                    tok_ret5 = kb.tok("cc")

            barrier()

            with contextlib.ExitStack() as w5x:
                XI = w5x.enter_context(nc.sbuf_tensor("XI5", [128, NCH * TL], BF16))
                w5y = contextlib.ExitStack()
                WO5 = w5y.enter_context(nc.sbuf_tensor("WO5", [128, 8 * 1024], BF16))
                ATT5 = w5y.enter_context(nc.sbuf_tensor("ATT5", [128, 2 * 8 * 512], BF16))
                TZ5 = w5y.enter_context(nc.sbuf_tensor("TZ5", [128, 2 * 512], F32))
                PWO5 = w5y.enter_context(nc.psum_tensor("pwo5", [128, 1024], F32))

                def f_gp(g):
                    kb.dma(g, "dg", out=WO5[:, :].rearrange("p (jc o) -> p jc o", o=1024),
                           in_=d_wo5[:, :].rearrange("(jc p) o -> p jc o", p=128))
                kb.gp(f_gp)
                tok_wo5 = kb.tok("dg")

                att_toks5 = [None, None]
                wpar5 = 0
                wo_ev5 = [None, None]
                for tb in range(4):
                    lpar = tb % 2

                    def f_sy(sy, tb=tb, lpar=lpar, tok=(tok_ret5 if tb == 0 else None),
                             cons=att_toks5[lpar]):
                        kb.wait(sy, tok)
                        kb.wait(sy, cons)
                        pid = kb.pid(sy)
                        src_ = ret_out[:, :].rearrange("(jc p) t -> p jc t", p=128)
                        kb.dma(sy, "dq",
                               out=ATT5[:, lpar * 4096:(lpar + 1) * 4096].rearrange(
                                   "p (jc t) -> p jc t", t=512),
                               in_=src_[:, :, bass.ts(pid, TL)][:, :, tb * 512:tb * 512 + 512])
                    kb.sy(f_sy)
                    tok_att5 = kb.tok("dq")

                    for oc in range(NCH):
                        def f_te(te, tb=tb, oc=oc, lpar=lpar, wpar=wpar5, tok=tok_att5,
                                 w8=(tok_wo5 if tb == 0 and oc == 0 else None),
                                 evp=wo_ev5[wpar5]):
                            kb.wait(te, w8)
                            kb.wait(te, tok)
                            kb.wait(te, evp)
                            for jc in range(8):
                                mm_ = te.matmul(
                                    PWO5[:, wpar * 512:wpar * 512 + 512],
                                    WO5[:, jc * 1024 + oc * 128: jc * 1024 + oc * 128 + 128],
                                    ATT5[:, lpar * 4096 + jc * 512: lpar * 4096 + jc * 512 + 512],
                                    start=(jc == 0), stop=(jc == 7))
                            kb.tag(mm_, "mm")
                        kb.te(f_te)
                        tok_wmm5 = kb.tok("mm")

                        def f_ve(v, tb=tb, oc=oc, wpar=wpar5, tok=tok_wmm5):
                            kb.wait(v, tok)
                            t0 = tb * 512
                            xs = X[:, oc * TL + t0: oc * TL + t0 + 512]
                            xisl = XI[:, oc * TL + t0: oc * TL + t0 + 512]
                            ps = PWO5[:, wpar * 512:wpar * 512 + 512]
                            tz = TZ5[:, wpar * 512:(wpar + 1) * 512]
                            v.tensor_mul(tz, xs, RSTDR[:, t0:t0 + 512])
                            v.drain()
                            v.tensor_sub(tz, tz, MRSR[:, t0:t0 + 512])
                            v.drain()
                            v.tensor_add(xisl, tz, ps)
                            v.tensor_add(xs, xs, tz)
                            v.drain()
                            kb.tag(v.tensor_add(xs, xs, ps), "ev")
                        kb.ve(f_ve)
                        wo_ev5[wpar5] = kb.tok("ev")
                        wpar5 ^= 1
                    att_toks5[lpar] = wo_ev5[(wpar5 + 1) % 2]
                w5y.close()
                if DBG:
                    def f_sy(sy, t0=wo_ev5[0], t1=wo_ev5[1]):
                        kb.wait(sy, t0)
                        kb.wait(sy, t1)
                        kb.dma(sy, "dq", out=d_dbg_xi5[:, :], in_=XI[:, :])
                    kb.sy(f_sy)

                with contextlib.ExitStack() as f5x:
                    R2R = f5x.enter_context(nc.sbuf_tensor("R2R", [128, TL], F32))
                    W2LC = f5x.enter_context(nc.sbuf_tensor("W2LC", [128, 8], F32))
                    Z5E = f5x.enter_context(nc.sbuf_tensor("Z5E", [128, 512], F32))
                    st5x = contextlib.ExitStack()
                    E6R = st5x.enter_context(nc.sbuf_tensor("E6R", [1, 512], F32))
                    SQ2 = st5x.enter_context(nc.sbuf_tensor("SQ2", [128, 2 * 512], BF16))
                    SROW2 = st5x.enter_context(nc.sbuf_tensor("SROW2", [1, 512], F32))
                    PST2 = st5x.enter_context(nc.psum_tensor("pst2", [1, 512], F32))
                    PRS2 = st5x.enter_context(nc.psum_tensor("prs2", [128, 512], F32))

                    def f_sy(sy, w0=wo_ev5[0], w1=wo_ev5[1]):
                        kb.wait(sy, w0)
                        kb.wait(sy, w1)
                        kb.dma(sy, "dq", out=W2LC[:, :], in_=d_w2lc[:, :])
                    kb.sy(f_sy)
                    tok_w2l = kb.tok("dq")

                    def f_ve(v, w0=wo_ev5[0], w1=wo_ev5[1]):
                        kb.wait(v, w0)
                        kb.wait(v, w1)
                        v.memset(E6R[:, :], 1e-6)
                        kb.tag(v.memset(Z5E[:, :], 0.0), "ev")
                    kb.ve(f_ve)
                    tok_e6 = kb.tok("ev")

                    sq_free = [None, None]
                    prs_free = None
                    for tb in range(4):
                        t0 = tb * 512
                        for c in range(NCH):
                            cpar = c % 2

                            def f_ve(v, c=c, tb=tb, cpar=cpar, sf=sq_free[cpar],
                                     xw=(wo_ev5[0] if c == 0 and tb == 0 else None),
                                     xw2=(wo_ev5[1] if c == 0 and tb == 0 else None)):
                                kb.wait(v, sf)
                                kb.wait(v, xw)
                                kb.wait(v, xw2)
                                xisl = XI[:, c * TL + tb * 512: c * TL + tb * 512 + 512]
                                kb.tag(v.tensor_mul(SQ2[:, cpar * 512:(cpar + 1) * 512],
                                                    xisl, xisl), "ev")
                            kb.ve(f_ve)
                            tok_sq2 = kb.tok("ev")

                            def f_te(te, c=c, cpar=cpar, tok=tok_sq2,
                                     pf=(prs_free if c == 0 else None)):
                                kb.wait(te, tok)
                                kb.wait(te, pf)
                                kb.tag(te.matmul(PST2[0:1, :], ONESB[:, :],
                                                 SQ2[:, cpar * 512:(cpar + 1) * 512],
                                                 start=(c == 0), stop=(c == NCH - 1)), "mm")
                            kb.te(f_te)
                            sq_free[cpar] = kb.tok("mm")
                        tok_st2 = kb.tok("mm")

                        def f_sc(sc, tok=tok_st2):
                            kb.wait(sc, tok)
                            kb.tag(sc.activation(SROW2[0:1, :], PST2[0:1, :], AF.Copy,
                                                 scale=1.0 / D), "ac")
                        kb.sc(f_sc)
                        tok_sr2 = kb.tok("ac")

                        def f_ve(v, tok=tok_sr2, te6=(tok_e6 if tb == 0 else None)):
                            kb.wait(v, tok)
                            kb.wait(v, te6)
                            kb.tag(v.tensor_add(SROW2[0:1, :], SROW2[0:1, :], E6R[0:1, :]),
                                   "ev")
                        kb.ve(f_ve)
                        tok_v2 = kb.tok("ev")

                        def f_sc(sc, tok=tok_v2):
                            kb.wait(sc, tok)
                            kb.tag(sc.activation(SROW2[0:1, :], SROW2[0:1, :], AF.Sqrt), "ac")
                        kb.sc(f_sc)
                        tok_s2 = kb.tok("ac")

                        def f_ve(v, tok=tok_s2):
                            kb.wait(v, tok)
                            ti = v.reciprocal(SROW2[0:1, :], SROW2[0:1, :])
                            v.drain()
                            kb.tag(ti, "ev")
                        kb.ve(f_ve)
                        tok_r2 = kb.tok("ev")

                        def f_te(te, tok=tok_r2, pf=prs_free):
                            kb.wait(te, tok)
                            kb.wait(te, pf)
                            kb.tag(te.matmul(PRS2[:, :], ONESROWF[:, :], SROW2[0:1, :],
                                             start=True, stop=True), "mm")
                        kb.te(f_te)
                        tok_rp2 = kb.tok("mm")

                        def f_ve(v, t0=t0, tok=tok_rp2):
                            kb.wait(v, tok)
                            kb.tag(v.tensor_copy(R2R[:, t0:t0 + 512], PRS2[:, :]), "ev")
                        kb.ve(f_ve)
                        prs_free = kb.tok("ev")
                    tok_r2r = prs_free
                    st5x.close()

                    with contextlib.ExitStack() as g5x:
                        H2H = g5x.enter_context(nc.sbuf_tensor("H2H", [128, 8 * 512], BF16))
                        TF2 = g5x.enter_context(nc.sbuf_tensor("TF2", [128, 2 * 512], F32))
                        GFB = g5x.enter_context(nc.sbuf_tensor("GFB", [128, 32 * 512], BF16))
                        W1FT = g5x.enter_context(nc.sbuf_tensor("W1FT", [128, 2 * 8 * 256], BF16))
                        W2FT = g5x.enter_context(nc.sbuf_tensor("W2FT", [128, 32 * 128], BF16))
                        PFH = g5x.enter_context(nc.psum_tensor("pfh5", [128, 1024], F32))
                        PFY = g5x.enter_context(nc.psum_tensor("pfy5", [128, 1024], F32))

                        tf_free = [None, None]
                        h2_done = None
                        w1_free = [None, None]
                        w2_free = [None, None]
                        pfh_free = [None, None]
                        pfy_free = [None, None]
                        gf_cons = None
                        xadd_last = None
                        for half in range(4):
                            t0 = half * 512
                            for c in range(NCH):
                                spar = c % 2

                                def f_ve(v, c=c, t0=t0, spar=spar,
                                         tr=(tok_r2r if half == 0 and c == 0 else None),
                                         tw=(tok_w2l if half == 0 and c == 0 else None),
                                         tz=(tok_e6 if half == 0 and c == 0 else None),
                                         tf=tf_free[spar], h2c=h2_done):
                                    kb.wait(v, tr)
                                    kb.wait(v, tw)
                                    kb.wait(v, tz)
                                    kb.wait(v, tf)
                                    if c < 2:
                                        kb.wait(v, h2c)
                                    xisl = XI[:, c * TL + t0: c * TL + t0 + 512]
                                    v.tensor_mul(TF2[:, spar * 512:(spar + 1) * 512],
                                                 xisl, R2R[:, t0:t0 + 512])
                                    v.drain()
                                    ti = v.scalar_tensor_tensor(
                                        H2H[:, c * 512:(c + 1) * 512],
                                        TF2[:, spar * 512:(spar + 1) * 512],
                                        W2LC[:, c:c + 1], Z5E[:, :],
                                        ALU.mult, ALU.add)
                                    kb.tag(ti, "ev")
                                kb.ve(f_ve)
                                tf_free[spar] = kb.tok("ev")
                            h2_done = tf_free[(NCH - 1) % 2]

                            for op2 in range(16):
                                wpar = op2 % 2

                                def f_gp(g, op2=op2, wpar=wpar, cons=w1_free[wpar],
                                         tr=(tok_r2r if half == 0 and op2 == 0 else None)):
                                    kb.wait(g, cons)
                                    kb.wait(g, tr)
                                    kb.dma(g, "dg",
                                           out=W1FT[:, wpar * 2048:(wpar + 1) * 2048].rearrange(
                                               "p (dc o) -> p dc o", o=256),
                                           in_=d_fw1[:, op2 * 256:(op2 + 1) * 256].rearrange(
                                               "(dc p) o -> p dc o", p=128))
                                kb.gp(f_gp)
                                tok_w1t = kb.tok("dg")

                                for sub in range(2):
                                    oc2 = op2 * 2 + sub

                                    def f_te(te, oc2=oc2, sub=sub, wpar=wpar, tok=tok_w1t,
                                             hd=h2_done, pf=pfh_free[oc2 % 2]):
                                        kb.wait(te, tok)
                                        kb.wait(te, hd)
                                        kb.wait(te, pf)
                                        for dc in range(NCH):
                                            mm_ = te.matmul(
                                                PFH[:, (oc2 % 2) * 512:(oc2 % 2) * 512 + 512],
                                                W1FT[:, wpar * 2048 + dc * 256 + sub * 128:
                                                     wpar * 2048 + dc * 256 + sub * 128 + 128],
                                                H2H[:, dc * 512:(dc + 1) * 512],
                                                start=(dc == 0), stop=(dc == NCH - 1))
                                        kb.tag(mm_, "mm")
                                    kb.te(f_te)
                                    tok_fh = kb.tok("mm")
                                    if sub == 1:
                                        w1_free[wpar] = tok_fh

                                    def f_sc(sc, oc2=oc2, tok=tok_fh, gc=gf_cons):
                                        kb.wait(sc, tok)
                                        if oc2 < 2:
                                            kb.wait(sc, gc)
                                        kb.tag(sc.activation(
                                            GFB[:, oc2 * 512:(oc2 + 1) * 512],
                                            PFH[:, (oc2 % 2) * 512:(oc2 % 2) * 512 + 512],
                                            AF.Gelu), "ac")
                                    kb.sc(f_sc)
                                    pfh_free[oc2 % 2] = kb.tok("ac")
                            gf_last = pfh_free[31 % 2]

                            for oc in range(8):
                                ypar = oc % 2

                                def f_gp(g, oc=oc, cons=w2_free[0]):
                                    kb.wait(g, cons)
                                    kb.dma(g, "dg",
                                           out=W2FT[:, :].rearrange(
                                               "p (hc o) -> p hc o", o=128),
                                           in_=d_fw2[:, oc * 128:(oc + 1) * 128].rearrange(
                                               "(hc p) o -> p hc o", p=128))
                                kb.gp(f_gp)
                                tok_w2t = kb.tok("dg")

                                def f_te(te, oc=oc, ypar=ypar, tok=tok_w2t,
                                         gl=(gf_last if oc == 0 else None),
                                         pf=pfy_free[ypar]):
                                    kb.wait(te, tok)
                                    kb.wait(te, gl)
                                    kb.wait(te, pf)
                                    for hc in range(32):
                                        mm_ = te.matmul(
                                            PFY[:, ypar * 512:(ypar + 1) * 512],
                                            W2FT[:, hc * 128:(hc + 1) * 128],
                                            GFB[:, hc * 512:(hc + 1) * 512],
                                            start=(hc == 0), stop=(hc == 31))
                                    kb.tag(mm_, "mm")
                                kb.te(f_te)
                                tok_fy = kb.tok("mm")
                                w2_free[0] = tok_fy

                                def f_ve(v, oc=oc, t0=t0, ypar=ypar, tok=tok_fy):
                                    kb.wait(v, tok)
                                    xs = X[:, oc * TL + t0: oc * TL + t0 + 512]
                                    kb.tag(v.tensor_add(
                                        xs, xs, PFY[:, ypar * 512:(ypar + 1) * 512]), "ev")
                                kb.ve(f_ve)
                                pfy_free[ypar] = kb.tok("ev")
                                xadd_last = pfy_free[ypar]
                            gf_cons = xadd_last

            barrier()

        if NST >= 1:
            emit_stage_l0ffn(1, 0)
        if NST >= 2:
            emit_stage_l0attn(1)
        if NST >= 3:
            emit_stage_l0ffn(3, 2)
        if NST >= 4:
            emit_stage_l1ffn(4)
        if NST >= 5:
            emit_stage_l1attn(3)
        if NST >= 6:
            emit_stage_l1ffn(6)

        def f_out(sy):
            sy.wait_ge(sems["ev"], C["ev"])
            sy.wait_ge(sems["mm"], C["mm"])
            sy.wait_ge(sems["ac"], C["ac"])
            sy.wait_ge(sems["gv"], C["gv"])
            sy.wait_ge(sems["cc"], C["cc"])
            kb.dma(sy, "dq", out=d_xout[:, :].rearrange("(c p) t -> p c t", p=128),
                   in_=X[:, :].rearrange("p (c t) -> p c t", t=TL))
            sy.wait_ge(sems["dq"], C["dq"])
        kb.sy(f_out)

    return nc



def kernel(**inputs):
    gi = {k: np.asarray(v) for k, v in inputs.items()}
    x = gi["L0_attn_x"].astype(np.float32)
    xT = np.ascontiguousarray(x.reshape(NT, D).T)

    w_inT = np.ascontiguousarray(gi["L0_ffn_w_in"].astype(np.float32).T)
    w_outT = np.ascontiguousarray(gi["L0_ffn_w_out"].astype(np.float32).T)
    ktf = np.ascontiguousarray(gi["L0_ffn_k_ffwd"].astype(np.float32).transpose(0, 1, 3, 2))
    vf = np.ascontiguousarray(gi["L0_ffn_v_ffwd"].astype(np.float32))

    half = 64
    inv_freq = 1.0 / 10000.0 ** (np.arange(half, dtype=np.float64) / half)
    ang = np.arange(S, dtype=np.float64)[None, :] * inv_freq[:, None]
    cosr = np.cos(ang).astype(np.float32).astype(ml_dtypes.bfloat16)
    sinr = np.sin(ang).astype(np.float32).astype(ml_dtypes.bfloat16)

    cmask = np.zeros((4, 128, 512), np.float32)
    kk = np.arange(128)[:, None]
    qq = np.arange(512)[None, :]
    for m in range(4):
        cmask[m] = (qq >= kk + 128 * m).astype(np.float32)
    cmask = cmask.astype(ml_dtypes.bfloat16)

    w1T = np.ascontiguousarray(gi["L1_ffn_w1"].astype(np.float32).T)
    w2T = np.ascontiguousarray(gi["L1_ffn_w2"].astype(np.float32).T)
    b1c = np.ascontiguousarray(gi["L1_ffn_b1"].astype(np.float32).reshape(32, 128).T)
    b2c = np.ascontiguousarray(gi["L1_ffn_b2"].astype(np.float32).reshape(8, 128).T)

    import jax
    with jax.default_device(jax.devices("cpu")[0]):
        keep = np.asarray(jax.random.bernoulli(jax.random.key(42), 0.8, (B, S, D)))
    mskT_all = np.ascontiguousarray(
        np.where(keep, np.float32(0.625), np.float32(0.0)).reshape(NT, D).T
    ).astype(ml_dtypes.bfloat16)

    Wr5 = gi["L1_attn_Wr"].astype(np.float32)
    Wk5 = gi["L1_attn_Wk"].astype(np.float32)
    Wv5 = gi["L1_attn_Wv"].astype(np.float32)
    Wg5 = gi["L1_attn_Wg"].astype(np.float32)
    wA5 = gi["L1_attn_wA"].astype(np.float32)
    mu5c = np.empty((128, 40), np.float32)
    for mi, nm in enumerate(["mu_r", "mu_k", "mu_v", "mu_g", "mu_w"]):
        mu5c[:, mi::5] = gi["L1_attn_" + nm].astype(np.float32).reshape(8, 128).T
    u_flat = gi["L1_attn_u"].astype(np.float32).reshape(1024)
    gnw_flat = gi["L1_attn_gn_w"].astype(np.float32).reshape(1024)
    gnb_flat = gi["L1_attn_gn_b"].astype(np.float32).reshape(1024)
    w0_5 = gi["L1_attn_w0"].astype(np.float32)
    seg5 = np.ones((128, 5 * 1024), np.float32)
    for gix, g in enumerate((8, 16, 32, 64, 128)):
        t = np.arange(1024)
        seg5[:, gix * 1024:(gix + 1) * 1024] = (t % g != 0).astype(np.float32)[None, :]
    s_i = np.arange(128)[:, None]
    t_i = np.arange(128)[None, :]
    msk5 = np.zeros((128, 7 * 128), np.float32)
    msk5[:, 0 * 128:1 * 128] = ((s_i // 8 == t_i // 8) & (s_i < t_i))
    msk5[:, 1 * 128:2 * 128] = ((s_i // 16 == t_i // 16) & (s_i // 8 == t_i // 8 - 1))
    msk5[:, 2 * 128:3 * 128] = ((s_i // 32 == t_i // 32) & (s_i // 16 == t_i // 16 - 1))
    msk5[:, 3 * 128:4 * 128] = ((s_i // 64 == t_i // 64) & (s_i // 32 == t_i // 32 - 1))
    msk5[:, 4 * 128:5 * 128] = (s_i // 64 == t_i // 64 - 1)
    msk5[:, 5 * 128:6 * 128] = np.eye(128)
    msk5[:, 6 * 128:7 * 128] = ((s_i // 64) == (t_i // 64))

    nc = build_program()

    in_maps = []
    for c in range(NC):
        in_maps.append({
            "xin": np.ascontiguousarray(xT[:, c * TL:(c + 1) * TL]),
            "wqfT": np.ascontiguousarray(w_inT[:, 0:1024]),
            "wrtT": np.ascontiguousarray(w_inT[:, 4096:4112]),
            "wqkvT": np.ascontiguousarray(np.concatenate(
                [w_inT[:, 1024 + 128 * c:1024 + 128 * (c + 1)],
                 w_inT[:, 2048 + 128 * c:2048 + 128 * (c + 1)],
                 w_inT[:, 3072 + 128 * c:3072 + 128 * (c + 1)]], axis=1)),
            "woutT": w_outT,
            "wqkv2": np.ascontiguousarray(np.concatenate(
                [gi["L0_attn_w_q"].astype(np.float32)[:, 128 * c:128 * (c + 1)],
                 gi["L0_attn_w_k"].astype(np.float32)[:, 128 * c:128 * (c + 1)],
                 gi["L0_attn_w_v"].astype(np.float32)[:, 128 * c:128 * (c + 1)]], axis=1)),
            "wo0": np.ascontiguousarray(gi["L0_attn_W_o"].astype(np.float32)),
            "w1T": w1T, "w2T": w2T, "b1c": b1c, "b2c": b2c,
            "mskT": np.ascontiguousarray(mskT_all[:, c * TL:(c + 1) * TL]),
            "w5T": np.ascontiguousarray(np.concatenate(
                [Wr5[:, 128 * c:128 * (c + 1)], Wk5[:, 128 * c:128 * (c + 1)],
                 Wv5[:, 128 * c:128 * (c + 1)], Wg5[:, 128 * c:128 * (c + 1)],
                 wA5], axis=1)),
            "mu5c": mu5c,
            "w0c": np.ascontiguousarray(w0_5[128 * c:128 * (c + 1)].reshape(128, 1)),
            "wb5": np.ascontiguousarray(
                gi["L1_attn_wB"].astype(np.float32)[:, 128 * c:128 * (c + 1)]),
            "uc": np.ascontiguousarray(u_flat[128 * c:128 * (c + 1)].reshape(128, 1)),
            "gnwc": np.ascontiguousarray(gnw_flat[128 * c:128 * (c + 1)].reshape(128, 1)),
            "gnbc": np.ascontiguousarray(gnb_flat[128 * c:128 * (c + 1)].reshape(128, 1)),
            "w1lc": np.ascontiguousarray(
                gi["L1_attn_ln1_w"].astype(np.float32).reshape(8, 128).T),
            "w2lc": np.ascontiguousarray(
                gi["L1_attn_ln2_w"].astype(np.float32).reshape(8, 128).T),
            "wo5": np.ascontiguousarray(gi["L1_attn_Wo"].astype(np.float32)),
            "fw1": np.ascontiguousarray(gi["L1_attn_ffn_w1"].astype(np.float32)),
            "fw2": np.ascontiguousarray(gi["L1_attn_ffn_w2"].astype(np.float32)),
            "seg5": seg5,
            "msk5": msk5,
            "ktf": ktf,
            "vf": vf,
            "cosr": cosr,
            "sinr": sinr,
            "cmask": cmask,
            "eye": np.eye(128, dtype=np.float32).astype(ml_dtypes.bfloat16),
        })

    res = run_bass_kernel_spmd(nc, in_maps, core_ids=list(range(NC)))
    outT = np.concatenate([res.results[c]["xout"] for c in range(NC)], axis=1)
    return np.ascontiguousarray(outT.T).reshape(B, S, D)
